# version 04
# baseline (speedup 1.0000x reference)
import math
import numpy as np
import ml_dtypes
from contextlib import ExitStack

import concourse.bass as bass
import concourse.mybir as mybir
from concourse.bass_utils import run_bass_kernel_spmd

F32 = mybir.dt.float32
BF16 = mybir.dt.bfloat16
I32 = mybir.dt.int32
AX = mybir.AxisListType
ALU = mybir.AluOpType
ACTF = mybir.ActivationFunctionType
BFNP = ml_dtypes.bfloat16

D = 1024
NCH = 8
EPS = 1e-5
ENGN = ("tensor", "vector", "scalar", "sync", "gpsimd")
NEG = -1e9


class Cfg:
    def __init__(self, S=4096, n_cores=8, parts=("l0a", "l0f", "l1a", "l1f")):
        self.S = S
        self.T = S // 2
        self.n_cores = n_cores
        self.parts = parts
        self.scan_iters = 12
        self.groups = [[2 * i, 2 * i + 1] for i in range(n_cores // 2)]


def cdiv(a, b):
    return (a + b - 1) // b


def row2(ap):
    return ap if len(ap.shape) == 2 else ap.rearrange("(o f) -> o f", o=1)


class Arena:
    def __init__(self, prog, name, dtype, cols):
        self.prog = prog
        self.dtype = dtype
        self.cols = cols
        self.t = prog.st.enter_context(prog.nc.sbuf_tensor(name, [128, cols], dtype))
        self.off = 0
        self.align = 16 if dtype == BF16 else 8
        self.hi = 0

    def alloc(self, cols, parts=128):
        cols = int(cols)
        self.off = (self.off + self.align - 1) // self.align * self.align
        assert self.off + cols <= self.cols, (
            f"arena overflow: {self.off}+{cols} > {self.cols}")
        ap = self.t[0:parts, self.off : self.off + cols]
        self.off += cols
        self.hi = max(self.hi, self.off)
        return ap

    def mark(self):
        return self.off

    def release(self, m):
        self.off = m


class Prog:
    def __init__(self, nc, cfg):
        self.nc = nc
        self.cfg = cfg
        self.st = ExitStack()
        self.streams = {e: [] for e in ENGN}
        self.sems = {}
        self.consts = {}
        self.percore = {}
        self.dram = {}
        names = ["s_mm", "s_dve", "s_act", "s_gp", "s_bar", "s_cc"]
        self.dlanes = {}
        for q in ("sync", "gpsimd", "scalar"):
            self.dlanes[q] = [f"d_{q}{i}" for i in range(8)]
            names += self.dlanes[q]
        for s in names:
            h = self.st.enter_context(nc.semaphore(s))
            self.sems[s] = [h, 0]
        self.dl_i = {q: 0 for q in ("sync", "gpsimd", "scalar")}
        self.af = Arena(self, "af", F32, 27000)
        self.ab = Arena(self, "ab", BF16, 52400)
        self.psum = self.st.enter_context(nc.psum_tensor("psumt", [128, 4096], F32))
        self.pfree = [("s_mm", 0)] * 8
        self._uid = 0
        self.tail = {}

    def uid(self):
        self._uid += 1
        return self._uid

    def on(self, eng, fn, is_op=False):
        if is_op:
            self.tail[eng] = self.tail.get(eng, 0) + 1
        self.streams[eng].append(fn)

    def wait(self, eng, tickets):
        co = {}
        for t in tickets:
            if t is None:
                continue
            s, v = t
            if v > 0 and co.get(s, -1) < v:
                co[s] = v
        for s, v in co.items():
            h = self.sems[s][0]
            self.on(eng, lambda e, h=h, v=v: e.wait_ge(h, v))

    def inc(self, eng, fn, sem, n=1):
        self.tail[eng] = 0
        h = self.sems[sem][0]
        self.sems[sem][1] += n
        val = self.sems[sem][1]
        self.on(eng, lambda e, h=h, n=n: fn(e).then_inc(h, n))
        return (sem, val)

    def tick(self, sem):
        return (sem, self.sems[sem][1])

    def barrier(self):
        for eng in ENGN:
            assert self.tail.get(eng, 0) == 0, f"uncovered tail ops on {eng}"
        targets = [(nm, sv[0], sv[1]) for nm, sv in self.sems.items() if sv[1] > 0]
        for eng in ENGN:
            for nm, h, v in targets:
                self.on(eng, lambda e, h=h, v=v: e.wait_ge(h, v))
        self.pfree = [("s_mm", self.sems["s_mm"][1])] * 8
        return ("s_mm", self.sems["s_mm"][1])

    def dram_t(self, name, shape, dtype, kind=None):
        t = (self.nc.dram_tensor(name, shape, dtype, kind=kind) if kind
             else self.nc.dram_tensor(name, shape, dtype))
        self.dram[name] = t
        return t

    def const(self, name, arr, dtype=F32):
        if name in self.dram:
            return self.dram[name]
        arr = np.asarray(arr)
        arr = arr.astype(BFNP) if dtype == BF16 else arr.astype(np.float32)
        self.consts[name] = arr
        return self.dram_t(name, list(arr.shape), dtype, kind="ExternalInput")

    def pc_input(self, name, arrs, dtype=F32):
        if name in self.dram:
            return self.dram[name]
        out = []
        for a in arrs:
            a = np.asarray(a)
            out.append(a.astype(BFNP) if dtype == BF16 else a.astype(np.float32))
        self.percore[name] = out
        return self.dram_t(name, list(out[0].shape), dtype, kind="ExternalInput")

    def dma(self, dst, src, queue="sync", wait=(), slow=False):
        eng = queue
        lane = self.dlanes[queue][self.dl_i[queue] % 8]
        self.dl_i[queue] += 1
        self.wait(eng, list(wait) + [self.tick(lane)])
        return self.inc(eng, lambda e: e.dma_start(
            out=dst, in_=src, allow_slow_non_contiguous=slow), lane, 16)

    def pbank(self, b, nb=1):
        return self.psum[:, b * 512 : (b + nb) * 512]

    def setup_shared(self):
        c = {}
        self.const("c_ones128", np.ones((128, 128)), BF16)
        self.const("c_ident", np.eye(128), BF16)
        self.const("c_onesrow", np.ones((1, 512)), BF16)
        ones = self.ab.alloc(128)
        ident = self.ab.alloc(128)
        identf = self.af.alloc(128)
        onesrow = self.ab.alloc(512, parts=1)
        onescol = self.ab.alloc(16)
        self.const("c_onescol16", np.ones((128, 16)), BF16)
        self.const("c_identf", np.eye(128), F32)
        self.const("c_epscol", np.full((128, 1), EPS), F32)
        epscol = self.af.alloc(1)
        t0 = self.dma(ones, self.dram["c_ones128"].ap())
        t1 = self.dma(ident, self.dram["c_ident"].ap())
        t4 = self.dma(identf, self.dram["c_identf"].ap())
        t2 = self.dma(onesrow, self.dram["c_onesrow"].ap())
        t3 = self.dma(onescol, self.dram["c_onescol16"].ap())
        t5 = self.dma(epscol, self.dram["c_epscol"].ap())
        c["ones"] = ones
        c["identf"] = identf
        c["ident"] = ident
        c["onesrow"] = onesrow
        c["onescol"] = onescol
        c["eps"] = epscol
        c["tick"] = [t0, t1, t2, t3, t4, t5]
        self.shared = c

    def play(self):
        with self.nc.Block() as block:
            for eng, reg in (("tensor", block.tensor), ("vector", block.vector),
                             ("scalar", block.scalar), ("sync", block.sync),
                             ("gpsimd", block.gpsimd)):
                def runner(e, eng=eng):
                    for fn in self.streams[eng]:
                        fn(e)
                reg(runner)



def load_bias_cols(P, wap, F, name=None):
    ncol = cdiv(F, 128)
    tile = P.af.alloc(ncol)
    pad = ncol * 128 - F
    src = wap.rearrange("(c p) -> p c", p=128) if pad == 0 else None
    if src is not None:
        t = P.dma(tile, src, slow=True)
        return tile, [t]
    full = (F // 128) * 128
    ts = []
    if full:
        ts.append(P.dma(tile[:, 0 : F // 128],
                        wap[0:full].rearrange("(c p) -> p c", p=128), slow=True))
    rem = F - full
    ts.append(P.dma(tile[0:rem, F // 128 : F // 128 + 1],
                    wap[full:F].rearrange("(c p) -> p c", p=rem), slow=True))
    return tile, ts


def load_bias_row(P, wap, F):
    tile = P.ab.alloc(F, parts=1)
    t = P.dma(tile, wap.rearrange("(o f) -> o f", o=1), queue="gpsimd")
    return tile, [t]


def tok_chunks(T, n=512):
    return [(t0, min(n, T - t0)) for t0 in range(0, T, n)]


def linear_fm(P, *, W, rows, Fout, rhs, tokens, out, bias=None, act=None,
              scale=1.0, evac="scalar", fblock=1024, banks=(0, 1, 2, 3, 4, 5),
              post_oc=None, wq="gpsimd", stt_resid=None, f_dst0=0,
              evac_wait=None, bias_aug=None, nparity=1, on_ev=None,
              on_slice=None):
    nin = cdiv(rows, 128)
    bi = 0
    last = None
    prev_mm = {}
    slab_sets = {}
    for fbi, fb0 in enumerate(range(0, Fout, fblock)):
        fb = min(fblock, Fout - fb0)
        par = fbi % nparity
        if par not in slab_sets:
            slab_sets[par] = [P.ab.alloc(fblock, parts=min(128, rows - c * 128))
                              for c in range(nin)]
        slabs = slab_sets[par]
        sticks = []
        for c in range(nin):
            p = min(128, rows - c * 128)
            t = P.dma(slabs[c][:, 0:fb], W[c * 128 : c * 128 + p, fb0 : fb0 + fb],
                      queue=wq, wait=[prev_mm.get(par)])
            sticks.append(t)
        for (t0, nt) in tokens:
            sl = [rhs(c, t0, nt) for c in range(nin)]
            aps = [s[0] for s in sl]
            rticks = [t for s in sl for t in s[1]]
            for oc in range(cdiv(fb, 128)):
                fh = min(128, fb - oc * 128)
                gf = fb0 + oc * 128
                b = banks[bi % len(banks)]
                bi += 1
                ps = P.pbank(b)
                P.wait("tensor", [P.pfree[b]] + sticks + rticks)
                if bias_aug is not None:
                    P.wait("tensor", list(bias_aug[1]) + P.shared["tick"])
                def mms(e, aps=aps, slabs=slabs, oc=oc, fh=fh, nt=nt, ps=ps,
                        nin=nin, rows=rows, gf=gf):
                    last = None
                    for c in range(nin):
                        p = min(128, rows - c * 128)
                        last = e.matmul(ps[0:fh, 0:nt],
                                        slabs[c][0:p, oc * 128 : oc * 128 + fh],
                                        aps[c][0:p, 0:nt],
                                        start=(c == 0), stop=(bias_aug is None
                                                              and c == nin - 1))
                    if bias_aug is not None:
                        last = e.matmul(ps[0:fh, 0:nt],
                                        row2(bias_aug[0])[0:1, gf : gf + fh],
                                        row2(P.shared["onesrow"])[0:1, 0:nt],
                                        start=False, stop=True)
                    return last
                mm_t = P.inc("tensor", mms, "s_mm")
                prev_mm[par] = mm_t
                dst = out(f_dst0 + gf, t0, nt)
                if evac_wait is not None:
                    P.wait("vector" if (stt_resid is not None or evac == "vector")
                           else "scalar", evac_wait(f_dst0 + gf, t0, nt))
                if stt_resid is not None:
                    P.wait("vector", [mm_t])
                    ev_t = P.inc("vector", lambda e, dst=dst, ps=ps, fh=fh, nt=nt,
                                 xap=stt_resid(f_dst0 + gf, t0, nt):
                                 e.scalar_tensor_tensor(dst, ps[0:fh, 0:nt], 1.0,
                                                        xap, ALU.mult, ALU.add),
                                 "s_dve")
                elif evac == "scalar":
                    bap = 0.0
                    if bias is not None:
                        bt, b0 = bias
                        bap = bt[0:fh, b0 + gf // 128 : b0 + gf // 128 + 1]
                    fn = act(gf) if callable(act) else (
                        act if act is not None else ACTF.Identity)
                    P.wait("scalar", [mm_t])
                    ev_t = P.inc("scalar", lambda e, dst=dst, ps=ps, fh=fh, nt=nt,
                                 fn=fn, bap=bap, scale=scale:
                                 e.activation(dst, ps[0:fh, 0:nt], fn, bias=bap,
                                              scale=scale), "s_act")
                else:
                    assert act is None and bias is None
                    P.wait("vector", [mm_t])
                    ev_t = P.inc("vector", lambda e, dst=dst, ps=ps, fh=fh, nt=nt:
                                 e.tensor_copy(dst, ps[0:fh, 0:nt]), "s_dve")
                P.pfree[b] = ev_t
                last = ev_t
                if on_ev is not None:
                    on_ev(f_dst0 + gf, ev_t)
                if on_slice is not None:
                    on_slice(f_dst0 + gf, fh, t0, nt, ev_t)
                if post_oc is not None and (t0, nt) == tokens[-1]:
                    post_oc(f_dst0 + gf, fh, ev_t)
    return last


def linear_tm(P, *, W, rows, Fout, rhs, tslices, out, bias_row=None, act=None,
              scale=1.0, banks=(0, 1, 2, 3, 4, 5), fcb=512, on_out=None,
              evac_wait=None):
    nin = cdiv(rows, 128)
    mark = P.ab.mark()
    slabs = []
    sticks = []
    for c in range(nin):
        p = min(128, rows - c * 128)
        s = P.ab.alloc(Fout, parts=p)
        slabs.append(s)
        sticks.append(P.dma(s, W[c * 128 : c * 128 + p, :], queue="gpsimd"))
    onesrow = P.shared["onesrow"]
    bi = 0
    last = None
    for ti, (t0, nt) in enumerate(tslices):
        sl = [rhs(c, t0, nt) for c in range(nin)]
        aps = [s[0] for s in sl]
        rticks = [t for s in sl for t in s[1]]
        for f0 in range(0, Fout, fcb):
            nf = min(fcb, Fout - f0)
            b = banks[bi % len(banks)]
            bi += 1
            ps = P.pbank(b)
            P.wait("tensor", [P.pfree[b]] + sticks + rticks
                   + (bias_row[1] if bias_row else []))
            def mms(e, aps=aps, nt=nt, f0=f0, nf=nf, ps=ps, nin=nin,
                    rows=rows, slabs=slabs, bias_row=bias_row, onesrow=onesrow):
                last = None
                for c in range(nin):
                    p = min(128, rows - c * 128)
                    last = e.matmul(ps[0:nt, 0:nf], aps[c][0:p, 0:nt],
                                    slabs[c][0:p, f0 : f0 + nf],
                                    start=(c == 0),
                                    stop=(bias_row is None and c == nin - 1))
                if bias_row is not None:
                    last = e.matmul(ps[0:nt, 0:nf], row2(onesrow)[0:1, 0:nt],
                                    row2(bias_row[0])[0:1, f0 : f0 + nf],
                                    start=False, stop=True)
                return last
            mm_t = P.inc("tensor", mms, "s_mm")
            dst = out(ti, f0, nf)
            P.wait("scalar", [mm_t] + (evac_wait(ti) if evac_wait else []))
            fn = act if act is not None else ACTF.Identity
            ev_t = P.inc("scalar", lambda e, dst=dst, ps=ps, nt=nt, nf=nf, fn=fn,
                         scale=scale: e.activation(dst, ps[0:nt, 0:nf], fn,
                                                   scale=scale), "s_act")
            P.pfree[b] = ev_t
            last = ev_t
            if on_out is not None:
                on_out(ti, f0, nf, ev_t)
    return last, mark


def bcast_rows(P, src_row_ap, nq_list, dst, *, row_ticks, banks=(6, 7),
               lhs_ones=None, nrows=128, evac="scalar", dtype_note=None):
    ones = lhs_ones if lhs_ones is not None else P.shared["ones"]
    bi = 0
    out_t = []
    for (t0, nt) in nq_list:
        b = banks[bi % len(banks)]
        bi += 1
        ps = P.pbank(b)
        P.wait("tensor", [P.pfree[b]] + row_ticks)
        mm_t = P.inc("tensor", lambda e, t0=t0, nt=nt, ps=ps, ones=ones, nrows=nrows,
                     src=src_row_ap: e.matmul(ps[0:nrows, 0:nt], ones[0:1, 0:nrows],
                                              src[0:1, t0 : t0 + nt],
                                              start=True, stop=True), "s_mm")
        eng = evac
        sem = {"scalar": "s_act", "vector": "s_dve"}[eng]
        P.wait(eng, [mm_t])
        if eng == "scalar":
            t = P.inc(eng, lambda e, ps=ps, t0=t0, nt=nt, nrows=nrows, dst=dst:
                      e.activation(dst[0:nrows, t0 : t0 + nt], ps[0:nrows, 0:nt],
                                   ACTF.Copy), "s_act")
        else:
            t = P.inc(eng, lambda e, ps=ps, t0=t0, nt=nt, nrows=nrows, dst=dst:
                      e.tensor_copy(dst[0:nrows, t0 : t0 + nt], ps[0:nrows, 0:nt]),
                      "s_dve")
        P.pfree[b] = t
        out_t.append(t)
    return out_t



def ln_stats(P, *, xhat, Ttot, sq_scratch, want_f32=False, banks=(6, 7),
             arena=None):
    ones = P.shared["ones"]
    ar = arena if arena is not None else P.ab
    mbc = ar.alloc(Ttot)
    rbc = ar.alloc(Ttot)
    mf = P.af.alloc(Ttot) if want_f32 else None
    rf = P.af.alloc(Ttot) if want_f32 else None
    mkst = P.af.mark()
    var = P.af.alloc(Ttot)
    ticks = []
    c1 = 1.0 / D
    sq_mm = [None]
    for (t0, nt) in tok_chunks(Ttot):
        bs, bq = banks
        ps_s, ps_q = P.pbank(bs), P.pbank(bq)
        P.wait("tensor", [P.pfree[bs], P.pfree[bq]])
        sqt = []
        aps = []
        for c in range(NCH):
            ap, tk = xhat(c, t0, nt)
            aps.append(ap)
            sq = sq_scratch[c]
            P.wait("vector", tk + ([sq_mm[0]] if sq_mm[0] else []))
            t_sq = P.inc("vector", lambda e, sq=sq, ap=ap, nt=nt:
                         e.tensor_mul(sq[:, 0:nt], ap[:, 0:nt], ap[:, 0:nt]),
                         "s_dve")
            sqt.append(t_sq)
        P.wait("tensor", sqt)
        def mmall(e, aps=aps, nt=nt, ps_s=ps_s, ps_q=ps_q, ones=ones,
                  sq_scratch=sq_scratch):
            for c in range(NCH):
                e.matmul(ps_s[0:128, 0:nt], ones[:, 0:128], aps[c][:, 0:nt],
                         start=(c == 0), stop=(c == NCH - 1))
            last = None
            for c in range(NCH):
                last = e.matmul(ps_q[0:128, 0:nt], ones[:, 0:128],
                                sq_scratch[c][:, 0:nt],
                                start=(c == 0), stop=(c == NCH - 1))
            return last
        mm_t = P.inc("tensor", mmall, "s_mm")
        sq_mm[0] = mm_t
        P.wait("scalar", [mm_t])
        t_mb = P.inc("scalar", lambda e, t0=t0, nt=nt, ps_s=ps_s, mbc=mbc, c1=c1:
                     e.activation(mbc[:, t0 : t0 + nt], ps_s[0:128, 0:nt],
                                  ACTF.Copy, scale=c1), "s_act")
        t_mf = None
        if want_f32:
            t_mf = P.inc("scalar", lambda e, t0=t0, nt=nt, ps_s=ps_s, mf=mf, c1=c1:
                         e.activation(mf[:, t0 : t0 + nt], ps_s[0:128, 0:nt],
                                      ACTF.Copy, scale=c1), "s_act")
        P.wait("vector", [mm_t, t_mb])
        t_v1 = P.inc("vector", lambda e, t0=t0, nt=nt, mbc=mbc, var=var:
                     e.scalar_tensor_tensor(var[:, t0 : t0 + nt],
                                            mbc[:, t0 : t0 + nt], -1.0,
                                            mbc[:, t0 : t0 + nt],
                                            ALU.mult, ALU.mult), "s_dve")
        P.wait("vector", [t_v1])
        t_v = P.inc("vector", lambda e, t0=t0, nt=nt, ps_q=ps_q, var=var, c1=c1:
                    e.scalar_tensor_tensor(var[:, t0 : t0 + nt],
                                           ps_q[0:128, 0:nt], c1,
                                           var[:, t0 : t0 + nt],
                                           ALU.mult, ALU.add), "s_dve")
        P.pfree[bs] = t_mf if want_f32 else t_mb
        P.pfree[bq] = t_v
        P.wait("scalar", [t_v] + P.shared["tick"])
        t_ln = P.inc("scalar", lambda e, t0=t0, nt=nt, var=var:
                     e.activation(var[:, t0 : t0 + nt], var[:, t0 : t0 + nt],
                                  ACTF.Ln,
                                  bias=P.shared["eps"][0:128, 0:1]), "s_act")
        P.wait("scalar", [t_ln])
        if want_f32:
            t_r = P.inc("scalar", lambda e, t0=t0, nt=nt, var=var, rf=rf:
                        e.activation(rf[:, t0 : t0 + nt], var[:, t0 : t0 + nt],
                                     ACTF.Exp, scale=-0.5), "s_act")
            t_rb = P.inc("scalar", lambda e, t0=t0, nt=nt, var=var, rbc=rbc:
                         e.activation(rbc[:, t0 : t0 + nt], var[:, t0 : t0 + nt],
                                      ACTF.Exp, scale=-0.5), "s_act")
            ticks += [t_mb, t_mf, t_r, t_rb]
        else:
            t_rb = P.inc("scalar", lambda e, t0=t0, nt=nt, var=var, rbc=rbc:
                         e.activation(rbc[:, t0 : t0 + nt], var[:, t0 : t0 + nt],
                                      ACTF.Exp, scale=-0.5), "s_act")
            ticks += [t_mb, t_rb]
    P.af.release(mkst)
    return dict(mbc=mbc, rbc=rbc, mf=mf, rf=rf, ticks=ticks)


def ln_apply_bf(P, *, xhat, st, Ttot, out, g_col=None, b_col=None, gb_ticks=()):
    ticks = []
    for (t0, nt) in tok_chunks(Ttot):
        for c in range(NCH):
            ap, tk = xhat(c, t0, nt)
            dst = out(c, t0, nt)
            P.wait("vector", tk + st["ticks"] + list(gb_ticks))
            ta_ = P.inc("vector", lambda e, ap=ap, dst=dst, st=st, t0=t0, nt=nt:
                        e.tensor_sub(dst, ap[:, 0:nt], st["mbc"][:, t0 : t0 + nt]),
                        "s_dve")
            P.wait("vector", [ta_])
            if g_col is not None:
                tb_ = P.inc("vector", lambda e, dst=dst, st=st, g=g_col, c=c, t0=t0,
                            nt=nt: e.scalar_tensor_tensor(dst, dst,
                                                          g[0:128, c : c + 1],
                                                          st["rbc"][:, t0 : t0 + nt],
                                                          ALU.mult, ALU.mult),
                            "s_dve")
                P.wait("vector", [tb_])
                t3 = P.inc("vector", lambda e, dst=dst, b=b_col, c=c:
                           e.tensor_scalar_add(dst, dst, b[0:128, c : c + 1]),
                           "s_dve")
                ticks.append(t3)
            else:
                t2 = P.inc("vector", lambda e, dst=dst, st=st, t0=t0, nt=nt:
                           e.tensor_mul(dst, dst, st["rbc"][:, t0 : t0 + nt]),
                           "s_dve")
                ticks.append(t2)
    return ticks


def topln(P, *, xadd_f32, xhat_out, st, T):
    ticks = []
    for c in range(NCH):
        ap, tk = xadd_f32(c)
        bf = xhat_out(c)
        P.wait("vector", tk + st["ticks"])
        t1_ = P.inc("vector", lambda e, ap=ap, st=st, T=T:
                    e.tensor_sub(ap[:, 0:T], ap[:, 0:T], st["mf"][:, 0:T]),
                    "s_dve")
        P.wait("vector", [t1_])
        t2 = P.inc("vector", lambda e, ap=ap, st=st, T=T:
                   e.tensor_mul(ap[:, 0:T], ap[:, 0:T], st["rf"][:, 0:T]), "s_dve")
        P.wait("vector", [t2])
        t3 = P.inc("vector", lambda e, ap=ap, bf=bf, T=T:
                   e.tensor_scalar_mul(bf[:, 0:T], ap[:, 0:T], 1.0), "s_dve")
        ticks += [t2, t3]
    return ticks



def attn_qtile(P, *, Kt, Kticks, Qt, Qticks, Vt, Vticks, dh, daug, q0, nq,
               chunks, scale, octx, vsum, onescol_b, sums_dst, O_dst, ptile):
    nch = len(chunks)
    GA, GB = (0, 1, 2), (3, 4, 5)
    ngroups = cdiv(nch, 3)
    ctx_started = [False]
    ctx_t = None
    for gi in range(ngroups):
        grp = chunks[gi * 3 : gi * 3 + 3]
        banks = GA if gi % 2 == 0 else GB
        waits = [P.pfree[b] for b in banks[: len(grp)]] + Kticks + Qticks
        P.wait("tensor", waits)
        def smms(e, grp=grp, banks=banks, Kt=Kt, Qt=Qt, daug=daug, q0=q0):
            last = None
            for j, ch in enumerate(grp):
                ps = P.pbank(banks[j])
                last = e.matmul(ps[0 : ch["nk"], ch["qc0"] : ch["qc0"] + ch["nqw"]],
                                Kt[0:daug, ch["kc"] : ch["kc"] + ch["nk"]],
                                Qt[0:daug, q0 + ch["qc0"] : q0 + ch["qc0"] + ch["nqw"]],
                                start=True, stop=True)
            return last
        mm_t = P.inc("tensor", smms, "s_mm")
        mask_t = []
        for j, ch in enumerate(grp):
            if ch.get("maskpre") is not None:
                b = banks[j]
                P.wait("vector", [mm_t])
                mt = P.inc("vector", lambda e, b=b, ch=ch:
                           e.tensor_add(P.pbank(b)[0 : ch["nk"],
                                                   ch["mq0"] : ch["mq0"] + ch["mqw"]],
                                        P.pbank(b)[0 : ch["nk"],
                                                   ch["mq0"] : ch["mq0"] + ch["mqw"]],
                                        ch["maskpre"]), "s_dve")
                mask_t.append(mt)
        g_exp = []
        for j, ch in enumerate(grp):
            b = banks[j]
            slot = (gi % 2) * 3 + j
            pt = ptile[:, slot * 512 : slot * 512 + 512]
            if ch.get("pzero"):
                z0, zw = ch["pzero"]
                P.wait("vector", [mm_t])
                zt = P.inc("vector", lambda e, pt=pt, z0=z0, zw=zw, ch=ch:
                           e.memset(pt[0 : ch["nk"], z0 : z0 + zw], 0.0), "s_dve")
                g_exp.append(zt)
            bias = ch.get("biascol")
            P.wait("scalar", [mm_t] + mask_t)
            et = P.inc("scalar", lambda e, pt=pt, b=b, ch=ch, bias=bias, scale=scale:
                       e.activation(pt[0 : ch["nk"], ch["qc0"] : ch["qc0"] + ch["nqw"]],
                                    P.pbank(b)[0 : ch["nk"],
                                               ch["qc0"] : ch["qc0"] + ch["nqw"]],
                                    ACTF.Exp,
                                    bias=(bias if bias is not None else 0.0),
                                    scale=scale), "s_act")
            g_exp.append(et)
            P.pfree[b] = et
        if gi == 0:
            P.wait("tensor", [P.pfree[octx]] + ([P.pfree[7]] if vsum == "mm"
                                                else []))
        P.wait("tensor", g_exp + Vticks)
        fin = gi == ngroups - 1
        def cmms(e, grp=grp, gi=gi, Vt=Vt, dh=dh, nq=nq, octx=octx, vsum=vsum,
                 ptile=ptile, ctx_started=ctx_started, onescol_b=onescol_b,
                 fin=fin):
            last = None
            vc = dh + (1 if vsum == "col" else 0)
            for j, ch in enumerate(grp):
                lastch = fin and j == len(grp) - 1
                slot = (gi % 2) * 3 + j
                pt = ptile[:, slot * 512 : slot * 512 + 512]
                st = not ctx_started[0]
                last = e.matmul(P.pbank(octx)[0:vc, 0:nq],
                                Vt[0 : ch["nk"], ch["vi"] * vc : ch["vi"] * vc + vc],
                                pt[0 : ch["nk"], 0:nq], start=st, stop=lastch)
                if vsum == "mm":
                    last = e.matmul(P.pbank(7)[0:1, 0:nq],
                                    onescol_b[0 : ch["nk"], 0:1],
                                    pt[0 : ch["nk"], 0:nq],
                                    start=st, stop=lastch)
                ctx_started[0] = True
            return last
        if fin:
            ctx_t = P.inc("tensor", cmms, "s_mm")
        else:
            P.on("tensor", lambda e, f=cmms: f(e), is_op=True)
    P.wait("vector", [ctx_t])
    t_o = P.inc("vector", lambda e: e.tensor_copy(O_dst, P.pbank(octx)[0:dh, 0:nq]),
                "s_dve")
    if vsum == "col":
        assert dh == 64
        t_s = P.inc("vector", lambda e:
                    e.tensor_copy(sums_dst, P.pbank(octx)[64:65, 0:nq]), "s_dve")
    else:
        t_s = P.inc("vector", lambda e:
                    e.tensor_copy(sums_dst, P.pbank(7)[0:1, 0:nq]), "s_dve")
        P.pfree[7] = t_s
    P.pfree[octx] = t_s
    return [t_o, t_s]



class Ring:

    def __init__(self, P, nslots, cols, parts=128, arena=None):
        self.P = P
        ar = arena if arena is not None else P.ab
        self.slots = [ar.alloc(cols, parts=parts) for _ in range(nslots)]
        self.n = nslots
        self.i = 0
        self.cache = {}

    def get(self, key, fill):
        if key in self.cache:
            return self.cache[key]
        P = self.P
        slot = self.slots[self.i % self.n]
        self.i += 1
        self.cache = {k: v for k, v in self.cache.items() if v[0] is not slot}
        war = [P.tick("s_mm"), P.tick("s_dve")]
        ticks = fill(slot, war)
        self.cache[key] = (slot, ticks)
        return self.cache[key]


_RING_UID = [0]


def _ring_uid():
    _RING_UID[0] += 1
    return _RING_UID[0]


def xg_ring_loader(P, ring, src_dram, Wcols, dtype_cast=True):
    uid = _ring_uid()
    def xg(c, t0, nt):
        def fill(slot, war):
            t = P.dma(slot[:, 0:nt], src_dram.ap()[c * 128 : c * 128 + 128,
                                                   t0 : t0 + nt],
                      queue="gpsimd", wait=war)
            return [t]
        ap, ticks = ring.get(("xg", uid, c, t0, nt), fill)
        return ap[:, 0:nt], ticks
    return xg


def norm_ring_loader(P, ring, base, st, g_col=None, b_col=None, gticks=()):
    uid = _ring_uid()
    def xn(c, t0, nt):
        def fill(slot, war):
            ap, tk = base(c, t0, nt)
            P.wait("vector", war + tk + st["ticks"] + list(gticks))
            tx_ = P.inc("vector", lambda e, slot=slot, ap=ap, t0=t0, nt=nt:
                        e.tensor_sub(slot[:, 0:nt], ap[:, 0:nt],
                                     st["mbc"][:, t0 : t0 + nt]), "s_dve")
            P.wait("vector", [tx_])
            if g_col is not None:
                ty_ = P.inc("vector", lambda e, slot=slot, t0=t0, nt=nt, c=c:
                            e.scalar_tensor_tensor(slot[:, 0:nt], slot[:, 0:nt],
                                                   g_col[0:128, c : c + 1],
                                                   st["rbc"][:, t0 : t0 + nt],
                                                   ALU.mult, ALU.mult), "s_dve")
                P.wait("vector", [ty_])
                t = P.inc("vector", lambda e, slot=slot, nt=nt, c=c:
                          e.tensor_scalar_add(slot[:, 0:nt], slot[:, 0:nt],
                                              b_col[0:128, c : c + 1]), "s_dve")
            else:
                t = P.inc("vector", lambda e, slot=slot, t0=t0, nt=nt:
                          e.tensor_mul(slot[:, 0:nt], slot[:, 0:nt],
                                       st["rbc"][:, t0 : t0 + nt]), "s_dve")
            return [t]
        ap, ticks = ring.get(("xn", uid, c, t0, nt), fill)
        return ap[:, 0:nt], ticks
    return xn


def sbuf_loader(P, tiles, ticks_map=None):
    def f(c, t0, nt):
        return tiles[c][:, t0 : t0 + nt], (ticks_map or {}).get(c, [])
    return f



def v_transpose(P, *, vT, vticks, Sk, dh, vtok, vsum, banks=(0, 1, 2, 3, 4, 5),
                ident_off=0):
    ident = P.shared["ident"]
    nkc = cdiv(Sk, 128)
    vc = dh + (1 if vsum == "col" else 0)
    per = 512 // dh
    bi = 0
    ticks = []
    for g0 in range(0, nkc, per):
        gn = min(per, nkc - g0)
        b = banks[bi % len(banks)]
        bi += 1
        ps = P.pbank(b)
        P.wait("tensor", [P.pfree[b]] + vticks + P.shared["tick"])
        def tmm(e, g0=g0, gn=gn, ps=ps, vT=vT, dh=dh, Sk=Sk, ident=ident,
                io=ident_off):
            last = None
            for j in range(gn):
                k0 = (g0 + j) * 128
                nk = min(128, Sk - k0)
                last = e.matmul(ps[0:nk, j * dh : j * dh + dh],
                                vT[0:dh, k0 : k0 + nk],
                                ident[io : io + dh, io : io + dh],
                                start=True, stop=True)
            return last
        mm_t = P.inc("tensor", tmm, "s_mm")
        P.wait("vector", [mm_t])
        def ev(e, g0=g0, gn=gn, ps=ps, vtok=vtok, dh=dh, vc=vc, Sk=Sk):
            last = None
            for j in range(gn):
                k0 = (g0 + j) * 128
                nk = min(128, Sk - k0)
                last = e.tensor_copy(
                    vtok[0:nk, (g0 + j) * vc : (g0 + j) * vc + dh],
                    ps[0:nk, j * dh : j * dh + dh])
            return last
        ev_t = P.inc("vector", ev, "s_dve")
        P.pfree[b] = ev_t
        ticks.append(ev_t)
    if vsum == "col":
        t = P.inc("vector", lambda e, vtok=vtok, vc=vc, nkc=nkc:
                  e.memset(vtok[:, 0 : nkc * vc].rearrange(
                      "p (k v) -> p k v", v=vc)[:, :, dh : dh + 1], 1.0), "s_dve")
        ticks.append(t)
    return ticks


class StageOC:

    def __init__(self, P, Ttot, n=4, queue="sync", parts=128, ncols=512):
        self.P = P
        self.tiles = [P.ab.alloc(ncols, parts=parts) for _ in range(n)]
        self.n = n
        self.i = 0
        self.cur = {}
        self.dmatick = [None] * n
        self.lastev = [None] * n
        self.queue = queue
        self.dram_fn = None

    def out(self, gf, t0, nt):
        i = self.i % self.n
        self.i += 1
        self.cur[(gf, t0)] = i
        return self.tiles[i][:, 0:nt]

    def evac_wait(self, gf, t0, nt):
        i = self.cur[(gf, t0)]
        return [self.dmatick[i], self.lastev[i]]

    def note_ev(self, gf, tick):
        pass

    def on_slice(self, gf, fh, t0, nt, tick):
        i = self.cur[(gf, t0)]
        self.lastev[i] = tick
        self.dmatick[i] = self.P.dma(self.dram_fn(gf, fh, t0, nt),
                                     self.tiles[i][0:fh, 0:nt],
                                     queue=self.queue, wait=[tick])



def build_l0a(P, C):
    cfg = P.cfg
    S, T = cfg.S, cfg.T
    W = C["w"]
    x = C["x"]
    xb = C["xb"]
    sq = C["sq"]
    own = sbuf_loader(P, xb)
    m_blk = P.ab.mark()
    mf_blk = P.af.mark()
    sel = C["sel"]

    cols = {}
    colticks = []
    for nm, F in [("L0_attn_ln_g", D), ("L0_attn_ln_b", D), ("L0_attn_ln1_g", D),
                  ("L0_attn_ln1_b", D), ("L0_attn_ln2_g", D), ("L0_attn_ln2_b", D),
                  ("L0_attn_lnf_g", D), ("L0_attn_lnf_b", D),
                  ("L0_attn_bgate", D), ("L0_attn_bq", D), ("L0_attn_b1", 4096),
                  ("L0_attn_bf", D)]:
        cols[nm], tk = load_bias_cols(P, W[nm].ap(), F)
        colticks += tk
    cols["l0_bdi"], tk = load_bias_cols(P, W["l0_bdi"].ap(), 64)
    colticks += tk
    cols["l0_bkv"], tk = load_bias_cols(P, W["l0_bkv"].ap(), 2 * D)
    colticks += tk
    rows = {}
    for nm in ("L0_attn_bo", "L0_attn_b2"):
        rows[nm], tk = load_bias_row(P, W[nm].ap(), D)
        colticks += tk
    P.barrier()
    m_work = P.ab.mark()
    mf_work = P.af.mark()
    C["scanf"] = P.af.alloc(S)
    C["scanb"] = P.ab.alloc(S + 8)

    xg = xg_ring_loader(P, C["ring"], C["x_bT"], S)
    mst = P.ab.mark()
    st_g_ln = ln_stats(P, xhat=xg, Ttot=S, sq_scratch=sq)
    P.barrier()

    scanf = C["scanf"]
    scanb = C["scanb"]
    xn_g = norm_ring_loader(P, C["ring"], xg, st_g_ln, cols["L0_attn_ln_g"],
                            cols["L0_attn_ln_b"], colticks)
    mrk = P.ab.mark()
    slabs_di = P.ab.alloc(8 * 64)
    tdis = [P.dma(slabs_di[:, c * 64 : c * 64 + 64],
                  W["l0_wdi"].ap()[c * 128 : c * 128 + 128, :], queue="gpsimd")
            for c in range(NCH)]
    tdi = tdis[-1]
    for ti, (t0, nt) in enumerate(tok_chunks(S)):
        b = (0, 1)[ti % 2]
        ps = P.pbank(b)
        for c in range(NCH):
            xg(c, t0, nt)
        sl = [xn_g(c, t0, nt) for c in range(NCH)]
        P.wait("tensor", [P.pfree[b]] + tdis + [t for s in sl for t in s[1]])
        def mms(e, sl=sl, ps=ps, nt=nt, slabs_di=slabs_di):
            last = None
            for c in range(NCH):
                last = e.matmul(ps[0:64, 0:nt],
                                slabs_di[:, c * 64 : c * 64 + 64],
                                sl[c][0][:, 0:nt], start=(c == 0),
                                stop=(c == NCH - 1))
            return last
        mm_t = P.inc("tensor", mms, "s_mm")
        P.wait("scalar", [mm_t])
        t1 = P.inc("scalar", lambda e, ps=ps, t0=t0, nt=nt:
                   e.activation(scanf[0:16, t0 : t0 + nt], ps[0:16, 0:nt],
                                ACTF.Sigmoid,
                                bias=cols["l0_bdi"][0:16, 0:1]), "s_act")
        t2 = P.inc("scalar", lambda e, ps=ps, t0=t0, nt=nt:
                   e.activation(scanf[32:48, t0 : t0 + nt], ps[32:48, 0:nt],
                                ACTF.Identity,
                                bias=cols["l0_bdi"][32:48, 0:1]), "s_act")
        P.pfree[b] = t2
    P.barrier()
    P.ab.release(mst)

    mst = P.ab.mark()
    st_o_ln = ln_stats(P, xhat=own, Ttot=T, sq_scratch=sq)
    xn_o = norm_ring_loader(P, C["ring"], own, st_o_ln, cols["L0_attn_ln_g"],
                            cols["L0_attn_ln_b"], colticks)
    stg = StageOC(P, T)
    stg.dram_fn = lambda gf, fh, t0, nt: C["scr_gate"].ap()[gf : gf + fh,
                                                            t0 : t0 + nt]
    linear_fm(P, W=W["L0_attn_Wgate"].ap(), rows=D, Fout=D, rhs=xn_o,
              tokens=tok_chunks(T), out=stg.out, evac_wait=stg.evac_wait,
              on_slice=stg.on_slice,
              bias=(cols["L0_attn_bgate"], 0), act=ACTF.Sigmoid, fblock=512,
              nparity=2)
    P.barrier()
    P.ab.release(mst)

    abuf = P.ab.alloc(16)[32:48, 0:16]
    ta = P.dma(abuf, W["L0_attn_A"].ap(), queue="gpsimd")
    ti = P.inc("vector", lambda e: e.tensor_scalar_mul(scanf[64:80, 0:S],
                                                       scanf[32:48, 0:S], 1.0),
               "s_dve")
    P.inc("vector", lambda e: e.memset(scanb[32:48, 0:1], 0.0), "s_dve")
    P.wait("vector", [ta, ti])
    for it in range(cfg.scan_iters):
        lastit = it == cfg.scan_iters - 1
        P.wait("scalar", [P.tick("s_dve"), P.tick("s_mm")])
        th = P.inc("scalar", lambda e: e.activation(scanb[0:16, 0:S],
                                                    scanf[64:80, 0:S],
                                                    ACTF.Tanh), "s_act")
        if lastit:
            break
        P.wait("vector", [th])
        tt = P.inc("vector", lambda e: e.tensor_mul(scanb[32:48, 1 : S + 1],
                                                    scanb[0:16, 0:S],
                                                    scanf[0:16, 0:S]), "s_dve")
        for ci, (t0, nt) in enumerate(tok_chunks(S)):
            b = 6 + (ci % 2)
            ps = P.pbank(b)
            P.wait("tensor", [P.pfree[b], tt, ta])
            mm_t = P.inc("tensor", lambda e, ps=ps, t0=t0, nt=nt:
                         e.matmul(ps[32:48, 0:nt], abuf,
                                  scanb[32:48, t0 : t0 + nt],
                                  start=True, stop=True), "s_mm")
            P.wait("vector", [mm_t])
            tz = P.inc("vector", lambda e, ps=ps, t0=t0, nt=nt:
                       e.scalar_tensor_tensor(scanf[64:80, t0 : t0 + nt],
                                              ps[32:48, 0:nt], 1.0,
                                              scanf[32:48, t0 : t0 + nt],
                                              ALU.mult, ALU.add), "s_dve")
            P.pfree[b] = tz
    P.barrier()

    mrk = P.ab.mark()
    hcst = [P.ab.alloc(S), P.ab.alloc(S)]
    gslot = [P.ab.alloc(T), P.ab.alloc(T)]
    hc_done = {}
    def hc_out(gf, t0, nt):
        return hcst[(gf // 128) % 2][:, t0 : t0 + nt]
    def hc_wait(gf, t0, nt):
        return [hc_done.get((gf // 128) % 2)]
    def hc_post(gf, fh, tick):
        oc = gf // 128
        i = oc % 2
        hl = hcst[i]
        gs = gslot[i]
        tg = P.dma(gs, C["scr_gate"].ap()[gf : gf + fh, :], queue="sync",
                   wait=[hc_done.get(i)])
        P.wait("vector", [tick, tg])
        h1 = P.inc("vector", lambda e, hl=hl, sel=sel:
                   e.tensor_scalar_mul(hl[:, 0:T], hl[:, 0:T], sel[0:128, 0:1]),
                   "s_dve")
        P.wait("vector", [h1])
        h2 = P.inc("vector", lambda e, hl=hl, sel=sel:
                   e.scalar_tensor_tensor(hl[:, 0:T], hl[:, T : 2 * T],
                                          sel[0:128, 1:2], hl[:, 0:T],
                                          ALU.mult, ALU.add), "s_dve")
        P.wait("vector", [h2])
        h3 = P.inc("vector", lambda e, hl=hl, gs=gs:
                   e.tensor_mul(hl[:, 0:T], hl[:, 0:T], gs[:, 0:T]), "s_dve")
        P.wait("vector", [h3])
        tm = P.inc("vector", lambda e, hl=hl, oc=oc:
                   e.tensor_add(hl[:, 0:T], hl[:, 0:T], xb[oc][:, 0:T]),
                   "s_dve")
        hc_done[i] = P.dma(C["scr_mamba"].ap()[gf : gf + fh, :],
                           hl[0:fh, 0:T], queue="sync", wait=[tm])
    linear_fm(P, W=W["L0_attn_C"].ap(), rows=16, Fout=D,
              rhs=lambda c, t0, nt: (C["scanb"][0:16, t0 : t0 + nt], []),
              tokens=tok_chunks(S), out=hc_out, evac_wait=hc_wait,
              post_oc=hc_post, fblock=256)
    P.barrier()
    P.ab.release(m_work)
    P.af.release(mf_work)

    mst = P.ab.mark()
    st_o_ln1 = ln_stats(P, xhat=own, Ttot=T, sq_scratch=sq)
    xn1_o = norm_ring_loader(P, C["ring"], own, st_o_ln1,
                             cols["L0_attn_ln1_g"], cols["L0_attn_ln1_b"],
                             colticks)
    stq = StageOC(P, T)
    stq.dram_fn = lambda gf, fh, t0, nt: C["scr_q"].ap()[gf // 128, 0:fh,
                                                         t0 : t0 + nt]
    linear_fm(P, W=W["L0_attn_Wq"].ap(), rows=D, Fout=D,
              rhs=xn1_o, tokens=tok_chunks(T),
              out=stq.out, evac_wait=stq.evac_wait, on_slice=stq.on_slice,
              bias=(cols["L0_attn_bq"], 0), fblock=512, nparity=2)
    P.barrier()
    P.ab.release(mst)

    mst = P.ab.mark()
    st_g_ln1 = ln_stats(P, xhat=xg, Ttot=S, sq_scratch=sq)
    xstage = [P.ab.alloc(512), P.ab.alloc(512)]
    xdma = [None, None]
    si = 0
    for c in range(NCH):
        for (t0, nt) in tok_chunks(S):
            raw, tk = xg(c, t0, nt)
            slot = xstage[si % 2]
            P.wait("vector", tk + st_g_ln1["ticks"] + [xdma[si % 2]])
            w1_ = P.inc("vector", lambda e, slot=slot, raw=raw, t0=t0, nt=nt:
                        e.tensor_sub(slot[:, 0:nt], raw[:, 0:nt],
                                     st_g_ln1["mbc"][:, t0 : t0 + nt]), "s_dve")
            P.wait("vector", [w1_])
            w2_ = P.inc("vector", lambda e, slot=slot, t0=t0, nt=nt, c=c:
                        e.scalar_tensor_tensor(
                            slot[:, 0:nt], slot[:, 0:nt],
                            cols["L0_attn_ln1_g"][0:128, c : c + 1],
                            st_g_ln1["rbc"][:, t0 : t0 + nt],
                            ALU.mult, ALU.mult), "s_dve")
            P.wait("vector", [w2_])
            w3_ = P.inc("vector", lambda e, slot=slot, nt=nt, c=c:
                        e.tensor_scalar_add(slot[:, 0:nt], slot[:, 0:nt],
                                            cols["L0_attn_ln1_b"][0:128, c : c + 1]),
                        "s_dve")
            xdma[si % 2] = P.dma(
                C["scr_xn1"].ap()[c * 128 : c * 128 + 128, t0 : t0 + nt],
                slot[:, 0:nt], queue="sync", wait=[w3_])
            si += 1
    P.barrier()
    P.ab.release(mst)

    mst = P.ab.mark()
    def kvrhs(c, t0, nt):
        def fill(slot, war):
            t = P.dma(slot[:, 0:nt],
                      C["scr_xn1"].ap()[c * 128 : c * 128 + 128, t0 : t0 + nt],
                      queue="gpsimd", wait=war)
            return [t]
        ap, ticks = C["ring"].get(("kv", c, t0, nt), fill)
        return ap[:, 0:nt], ticks
    stkv = StageOC(P, S)
    def kv_dst(gf, fh, t0, nt):
        h = gf // 128
        if h < 8:
            return C["scr_k"].ap()[h, 0:fh, t0 : t0 + nt]
        return C["scr_vT"].ap()[h - 8, 0:fh, t0 : t0 + nt]
    stkv.dram_fn = kv_dst
    linear_fm(P, W=W["l0_wkv"].ap(), rows=D, Fout=2 * D, rhs=kvrhs,
              tokens=tok_chunks(S), out=stkv.out, evac_wait=stkv.evac_wait,
              on_slice=stkv.on_slice,
              bias=(cols["l0_bkv"], 0), fblock=256, nparity=2)
    P.barrier()
    P.ab.release(mst)

    nqt = T // 512
    nkc = S // 128
    m_attn = P.ab.mark()
    mf_attn = P.af.mark()
    sums = P.af.alloc(2 * 512, parts=97)
    P.barrier()
    OtT = P.ab.alloc(T)
    recb = P.ab.alloc(2 * 512, parts=97)
    Kt = P.ab.alloc(S)
    Vtok = P.ab.alloc(nkc * 128)
    vst = P.ab.alloc(4 * 128)
    Qt = P.ab.alloc(T)
    ptile = P.ab.alloc(6 * 512)
    o_dma = None
    prev_bc = []
    vst_war = None
    tr = None
    hq_mm = None
    for h in range(NCH):
        prev = [hq_mm] if hq_mm else [P.tick("s_mm"), P.tick("s_dve"),
                                      P.tick("s_act")]
        tk_k = P.dma(Kt, C["scr_k"].ap()[h], queue="sync", wait=prev)
        tk_q = P.dma(Qt, C["scr_q"].ap()[h], queue="sync", wait=prev)
        vticks = []
        for g0 in range(0, nkc, 4):
            dts = [P.dma(vst[:, j * 128 : j * 128 + 128],
                         C["scr_vT"].ap()[h, :, (g0 + j) * 128 :
                                          (g0 + j) * 128 + 128],
                         queue="gpsimd", wait=[vst_war] + prev)
                   for j in range(4)]
            b = (g0 // 4) % 6
            ps = P.pbank(b)
            P.wait("tensor", [P.pfree[b]] + dts + P.shared["tick"])
            def tmm(e, ps=ps):
                last = None
                for j in range(4):
                    last = e.matmul(ps[0:128, j * 128 : j * 128 + 128],
                                    vst[0:128, j * 128 : j * 128 + 128],
                                    P.shared["ident"][0:128, 0:128],
                                    start=True, stop=True)
                return last
            mm_t = P.inc("tensor", tmm, "s_mm")
            vst_war = mm_t
            P.wait("vector", [mm_t])
            ev_t = P.inc("vector", lambda e, ps=ps, g0=g0:
                         e.tensor_copy(Vtok[:, g0 * 128 : g0 * 128 + 512],
                                       ps[0:128, 0:512]), "s_dve")
            P.pfree[b] = ev_t
            vticks.append(ev_t)
        P.wait("vector", [tr])
        ts_ms = P.inc("vector", lambda e: e.memset(sums[0:97, :], 1.0),
                      "s_dve")
        qticks = [ts_ms]
        for qt in range(nqt):
            sr, sc = 32 * (qt % 2), 512 * (qt // 2)
            chunks = [dict(kc=j * 128, nk=128, vi=j, qc0=0, nqw=512)
                      for j in range(nkc)]
            if o_dma is not None:
                P.wait("vector", [o_dma])
            ts = attn_qtile(P, Kt=Kt, Kticks=[tk_k], Qt=Qt, Qticks=[tk_q],
                            Vt=Vtok, Vticks=vticks + [ts_ms], dh=128, daug=128,
                            q0=qt * 512, nq=512, chunks=chunks,
                            scale=1.0 / math.sqrt(128.0), octx=6, vsum="mm",
                            onescol_b=P.shared["onescol"],
                            sums_dst=sums[sr : sr + 1, sc : sc + 512],
                            O_dst=OtT[:, qt * 512 : qt * 512 + 512],
                            ptile=ptile)
            qticks += ts
        hq_mm = P.tick("s_mm")
        P.wait("scalar", qticks + prev_bc)
        tl = P.inc("scalar", lambda e:
                   e.activation(sums[0:97, :], sums[0:97, :],
                                ACTF.Ln), "s_act")
        P.wait("scalar", [tl])
        tr = P.inc("scalar", lambda e:
                   e.activation(recb[0:97, :], sums[0:97, :],
                                ACTF.Exp, scale=-1.0), "s_act")
        prev_bc = []
        stt_ts = []
        for qt in range(nqt):
            sr, sc = 32 * (qt % 2), 512 * (qt // 2)
            b = 6 + qt % 2
            ps = P.pbank(b)
            P.wait("tensor", [P.pfree[b], tr] + P.shared["tick"])
            mm_t = P.inc("tensor", lambda e, ps=ps, sr=sr, sc=sc:
                         e.matmul(ps[0:128, 0:512],
                                  P.shared["ones"][sr : sr + 1, 0:128],
                                  recb[sr : sr + 1, sc : sc + 512],
                                  start=True, stop=True), "s_mm")
            prev_bc.append(mm_t)
            P.wait("vector", [mm_t])
            tn = P.inc("vector", lambda e, ps=ps, qt=qt:
                       e.scalar_tensor_tensor(
                           OtT[:, qt * 512 : qt * 512 + 512],
                           OtT[:, qt * 512 : qt * 512 + 512], 1.0,
                           ps[0:128, 0:512], ALU.mult, ALU.mult), "s_dve")
            P.pfree[b] = tn
            stt_ts.append(tn)
        o_dma = P.dma(C["scr_xn1"].ap()[h * 128 : h * 128 + 128, 0:T],
                      OtT[:, 0:T], queue="sync", wait=stt_ts)
    P.wait("sync", [o_dma])
    P.barrier()
    P.ab.release(m_attn)
    P.af.release(mf_attn)

    m_x2h = P.ab.mark()
    x2h = [P.ab.alloc(T) for _ in range(NCH)]
    mrk = P.ab.mark()
    def worhs(c, t0, nt):
        def fill(slot, war, c=c, t0=t0, nt=nt):
            t = P.dma(slot[:, 0:nt],
                      C["scr_xn1"].ap()[c * 128 : c * 128 + 128, t0 : t0 + nt],
                      queue="sync", wait=war)
            return [t]
        ap, tk = C["ring"].get(("ctx", c, t0, nt), fill)
        return ap[:, 0:nt], tk
    linear_fm(P, W=W["L0_attn_Wo"].ap(), rows=D, Fout=D,
              rhs=worhs, tokens=tok_chunks(T),
              out=lambda gf, t0, nt: x2h[gf // 128][:, t0 : t0 + nt],
              stt_resid=lambda gf, t0, nt: xb[gf // 128][:, t0 : t0 + nt],
              bias_aug=(rows["L0_attn_bo"], []), fblock=256, nparity=2)
    P.barrier()
    P.ab.release(mrk)

    mf_ffn = P.af.mark()
    mrk = P.ab.mark()
    st2 = ln_stats(P, xhat=sbuf_loader(P, x2h), Ttot=T, sq_scratch=sq,
                   arena=P.af)
    xn2 = norm_ring_loader(P, C["ring"], sbuf_loader(P, x2h), st2,
                           cols["L0_attn_ln2_g"], cols["L0_attn_ln2_b"],
                           colticks)
    P.barrier()
    stmid = StageOC(P, T, n=2)
    stmid.dram_fn = lambda gf, fh, t0, nt: C["scr_mid"].ap()[gf : gf + fh,
                                                             t0 : t0 + nt]
    linear_fm(P, W=W["L0_attn_W1"].ap(), rows=D, Fout=4096,
              rhs=xn2, tokens=tok_chunks(T),
              out=stmid.out, evac_wait=stmid.evac_wait, on_slice=stmid.on_slice,
              bias=(cols["L0_attn_b1"], 0), act=ACTF.Relu, fblock=512)
    P.barrier()
    P.ab.release(mrk)
    P.af.release(mf_ffn)

    mrk = P.ab.mark()
    mfk = P.af.mark()
    w2slab = [[P.ab.alloc(256) for _ in range(NCH)] for _ in range(2)]
    halfac = [P.af.alloc(512) for _ in range(NCH)]
    prev_w2 = {}
    ev_q = [None] * NCH
    for ti2, (t0, nt) in enumerate(tok_chunks(T)):
        for g in range(4):
            rsl = []
            for c in range(NCH):
                gc = g * 8 + c
                def fill(slot, war, gc=gc, t0=t0, nt=nt):
                    t = P.dma(slot[:, 0:nt],
                              C["scr_mid"].ap()[gc * 128 : gc * 128 + 128,
                                                t0 : t0 + nt],
                              queue="sync", wait=war)
                    return [t]
                rsl.append(C["ring"].get(("mid", ti2, gc), fill))
            for q4 in range(4):
                par = (g * 4 + q4) % 2
                slabs = w2slab[par]
                sticks = []
                for c in range(NCH):
                    t = P.dma(slabs[c],
                              W["L0_attn_W2"].ap()[g * 1024 + c * 128 :
                                                   g * 1024 + c * 128 + 128,
                                                   q4 * 256 : q4 * 256 + 256],
                              queue="gpsimd", wait=[prev_w2.get(par)])
                    sticks.append(t)
                bks = (0, 1) if q4 % 2 == 0 else (2, 3)
                P.wait("tensor", [P.pfree[bk] for bk in bks] + sticks
                       + [t for (_, tk) in rsl for t in tk])
                def mmg(e, rsl=rsl, g=g, nt=nt, q4=q4, slabs=slabs, bks=bks):
                    last = None
                    for ol in range(2):
                        for c in range(NCH):
                            last = e.matmul(
                                P.pbank(bks[ol])[0:128, 0:nt],
                                slabs[c][:, ol * 128 : ol * 128 + 128],
                                rsl[c][0][:, 0:nt],
                                start=(c == 0), stop=(g < 3 and c == NCH - 1))
                    if g == 3:
                        for ol in range(2):
                            oc = q4 * 2 + ol
                            last = e.matmul(
                                P.pbank(bks[ol])[0:128, 0:nt],
                                rows["L0_attn_b2"][0:1,
                                                   oc * 128 : oc * 128 + 128],
                                P.shared["onesrow"][0:1, 0:nt],
                                start=False, stop=True)
                    return last
                mm_t = P.inc("tensor", mmg, "s_mm")
                prev_w2[par] = mm_t
                P.wait("vector", [mm_t] + [ev_q[q4 * 2 + ol]
                                           for ol in range(2)])
                if g == 0:
                    def ev0(e, nt=nt, q4=q4, bks=bks):
                        last = None
                        for ol in range(2):
                            last = e.tensor_copy(halfac[q4 * 2 + ol][:, 0:nt],
                                                 P.pbank(bks[ol])[0:128, 0:nt])
                        return last
                    ev_t = P.inc("vector", ev0, "s_dve")
                else:
                    def ev1(e, nt=nt, q4=q4, bks=bks):
                        last = None
                        for ol in range(2):
                            oc = q4 * 2 + ol
                            last = e.scalar_tensor_tensor(
                                halfac[oc][:, 0:nt],
                                P.pbank(bks[ol])[0:128, 0:nt],
                                1.0, halfac[oc][:, 0:nt], ALU.mult, ALU.add)
                        return last
                    ev_t = P.inc("vector", ev1, "s_dve")
                for ol in range(2):
                    ev_q[q4 * 2 + ol] = ev_t
                for bk in bks:
                    P.pfree[bk] = ev_t
        P.wait("vector", [t for t in ev_q])
        def ev2(e, t0=t0, nt=nt):
            last = None
            for oc in range(NCH):
                last = e.tensor_add(x2h[oc][:, t0 : t0 + nt],
                                    x2h[oc][:, t0 : t0 + nt],
                                    halfac[oc][:, 0:nt])
            return last
        ev_t = P.inc("vector", ev2, "s_dve")
        for oc in range(NCH):
            ev_q[oc] = ev_t
        P.wait("vector", [ev_t])
    P.barrier()
    P.ab.release(mrk)
    P.af.release(mfk)

    wf = xb
    mrk = P.ab.mark()
    def wfrhs(c, t0, nt):
        if c >= 8:
            return x2h[c - 8][:, t0 : t0 + nt], []
        def fill(slot, war, c=c, t0=t0, nt=nt):
            t = P.dma(slot[:, 0:nt],
                      C["scr_mamba"].ap()[c * 128 : c * 128 + 128, t0 : t0 + nt],
                      queue="sync", wait=war)
            return [t]
        ap, tk = C["ring"].get(("mam", c, t0, nt), fill)
        return ap[:, 0:nt], tk
    linear_fm(P, W=W["L0_attn_Wf"].ap(), rows=2 * D, Fout=D, rhs=wfrhs,
              tokens=tok_chunks(T),
              out=lambda gf, t0, nt: wf[gf // 128][:, t0 : t0 + nt],
              bias=(cols["L0_attn_bf"], 0), fblock=256)
    P.barrier()
    P.ab.release(mrk)

    m_lnf = P.ab.mark()
    stf = ln_stats(P, xhat=sbuf_loader(P, wf), Ttot=T, sq_scratch=sq)
    ln_apply_bf(P, xhat=sbuf_loader(P, wf), st=stf, Ttot=T,
                out=lambda c, t0, nt: wf[c][:, t0 : t0 + nt],
                g_col=cols["L0_attn_lnf_g"], b_col=cols["L0_attn_lnf_b"],
                gb_ticks=colticks)
    P.barrier()
    for c in range(NCH):
        P.inc("vector", lambda e, c=c: e.tensor_add(x[c][:, 0:T], x[c][:, 0:T],
                                                    wf[c][:, 0:T]), "s_dve")
    P.barrier()
    P.ab.release(m_lnf)

    for c in range(NCH):
        P.inc("vector", lambda e, c=c: e.tensor_scalar_mul(x2h[c][:, 0:T],
                                                           x[c][:, 0:T], 1.0),
              "s_dve")
    P.barrier()
    stt = ln_stats(P, xhat=sbuf_loader(P, x2h), Ttot=T, sq_scratch=sq,
                   want_f32=True)
    topln(P, xadd_f32=lambda c: (x[c], []), xhat_out=lambda c: xb[c],
          st=stt, T=T)
    P.barrier()
    P.ab.release(m_blk)
    P.af.release(mf_blk)


def allgather(P, *, tiles, ticks, ccin, ccout):
    dts = []
    for c in range(NCH):
        dts.append(P.dma(ccin.ap()[c * 128 : c * 128 + 128, :], tiles[c][:, :],
                         queue="gpsimd", wait=ticks))
    P.wait("gpsimd", dts)
    t = P.inc("gpsimd", lambda e: e.collective_compute(
        "AllGather", ALU.bypass, replica_groups=P.cfg.groups,
        ins=[ccin.ap()], outs=[ccout.ap()]), "s_cc")
    return t



def build_program(cfg, arrs, percore):
    nc = bass.Bass("TRN2", target_bir_lowering=False, debug=False)
    P = Prog(nc, cfg)
    S, T = cfg.S, cfg.T
    dt_of = {}
    for name, a in arrs.items():
        dt = BF16 if a.dtype == BFNP else F32
        P.dram_t(name, list(a.shape), dt, kind="ExternalInput")
    for name, lst in percore.items():
        a = lst[0]
        dt = BF16 if a.dtype == BFNP else F32
        P.dram_t(name, list(a.shape), dt, kind="ExternalInput")
    out_t = P.dram_t("out", [D, T], F32, kind="ExternalOutput")
    P.setup_shared()

    C = {"w": P.dram}
    C["x_bT"] = P.dram["x_bT"]
    C["sel"] = None
    C["x"] = [P.af.alloc(T) for _ in range(NCH)]
    C["xb"] = [P.ab.alloc(T) for _ in range(NCH)]
    C["sq"] = [P.ab.alloc(512) for _ in range(NCH)]
    C["ring"] = Ring(P, 12, 512)
    sel = P.af.alloc(2)
    P.dma(sel, P.dram["sel01"].ap())
    C["sel"] = sel
    if "l0a" in cfg.parts:
        for nm, shp, dt in [("scr_gate", [D, T], BF16), ("scr_q", [8, 128, T], BF16),
                            ("scr_k", [8, 128, S], BF16),
                            ("scr_vT", [8, 128, S], BF16),
                            ("scr_xn1", [D, S], BF16), ("scr_mid", [4096, T], BF16),
                            ("scr_mamba", [D, T], BF16)]:
            C[nm] = P.dram_t(nm, shp, dt)
        C["cc1_in"] = P.dram_t("cc1_in", [D, T], BF16)
        C["cc1_out"] = P.dram_t("cc1_out", [2, D, T], BF16)
    if "l1a" in cfg.parts:
        C["l1a_q"] = P.dram_t("l1a_q", [8, 128, T], BF16)
        C["l1a_k"] = P.dram_t("l1a_k", [8, 128, S + 2], BF16)
        C["l1a_vT"] = P.dram_t("l1a_vT", [8, 128, S + 2], BF16)
        C["cc3_in"] = P.dram_t("cc3_in", [D, T], BF16)
        C["cc3_out"] = P.dram_t("cc3_out", [2, D, T], BF16)
    if "l1f" in cfg.parts:
        KV = T + 512
        C["l1f_x3"] = P.dram_t("l1f_x3", [D, T], F32)
        C["l1f_mid"] = P.dram_t("l1f_mid", [2816, T], BF16)
        C["l1f_q"] = P.dram_t("l1f_q", [4, 128, T], BF16)
        C["l1f_k"] = P.dram_t("l1f_k", [4, 128, KV], BF16)
        C["l1f_vT"] = P.dram_t("l1f_vT", [4, 128, KV], BF16)
        C["cc4_in"] = P.dram_t("cc4_in", [D, T], BF16)
        C["cc4_out"] = P.dram_t("cc4_out", [2, D, T], BF16)
        C["ccc_in"] = P.dram_t("ccc_in", [D, 1], BF16)
        C["ccc_out"] = P.dram_t("ccc_out", [2, D, 1], BF16)
    if "l0f" in cfg.parts:
        C["l0f_q"] = P.dram_t("l0f_q", [4, 128, T], BF16)
        C["l0f_k"] = P.dram_t("l0f_k", [4, 128, S], BF16)
        C["l0f_vT"] = P.dram_t("l0f_vT", [4, 128, S], BF16)
        C["cc2_in"] = P.dram_t("cc2_in", [D, T], BF16)
        C["cc2_out"] = P.dram_t("cc2_out", [2, D, T], BF16)
    for c in range(NCH):
        P.dma(C["x"][c], P.dram["x_ownT"].ap()[c * 128 : c * 128 + 128, :])
        P.dma(C["xb"][c], P.dram["x_ownT"].ap()[c * 128 : c * 128 + 128, :],
              queue="gpsimd")
    P.barrier()

    if "l0a" in cfg.parts:
        build_l0a(P, C)
        if len(cfg.parts) > 1:
            t_cc = allgather(P, tiles=C["xb"], ticks=[P.tick("s_dve")],
                             ccin=C["cc1_in"], ccout=C["cc1_out"])
            C["cc_x1"] = (C["cc1_out"], t_cc)
    if "l0f" in cfg.parts:
        build_l0f(P, C)
        if "l1a" in cfg.parts or "l1f" in cfg.parts:
            t_cc = allgather(P, tiles=C["xb"], ticks=[P.tick("s_dve")],
                             ccin=C["cc2_in"], ccout=C["cc2_out"])
            C["cc_x2"] = (C["cc2_out"], t_cc)
    if "l1a" in cfg.parts:
        build_l1a(P, C)
        if "l1f" in cfg.parts:
            t_cc = allgather(P, tiles=C["xb"], ticks=[P.tick("s_dve")],
                             ccin=C["cc3_in"], ccout=C["cc3_out"])
            C["cc_x3"] = (C["cc3_out"], t_cc)
    if "l1f" in cfg.parts:
        build_l1f(P, C)

    ots = []
    for c in range(NCH):
        ots.append(P.dma(out_t.ap()[c * 128 : c * 128 + 128, :], C["x"][c],
                         queue="sync", wait=[P.tick("s_dve"), P.tick("s_act"),
                                             P.tick("s_mm")]))
    P.wait("sync", ots)
    P.play()
    nc.finalize()
    return nc, P


def host_prep(inputs, cfg):
    S, T, n = cfg.S, cfg.T, cfg.n_cores
    f32 = np.float32
    arrs = {k: np.ascontiguousarray(np.asarray(v, dtype=f32))
            for k, v in inputs.items()}
    wdi = np.zeros((D, 64), f32)
    wdi[:, 0:16] = arrs["L0_attn_Wdelta"]
    wdi[:, 32:48] = arrs["L0_attn_Win"]
    bdi = np.zeros((64,), f32)
    bdi[0:16] = arrs["L0_attn_bdelta"]
    bdi[32:48] = arrs["L0_attn_b_in"]
    arrs["l0_wdi"] = wdi
    arrs["l0_bdi"] = bdi
    arrs["l0_wkv"] = np.concatenate([arrs["L0_attn_Wk"], arrs["L0_attn_Wv"]], 1)
    arrs["l0_bkv"] = np.concatenate([arrs["L0_attn_bk"], arrs["L0_attn_bv"]])
    arrs["l0_wv2"] = np.ascontiguousarray(
        arrs["L0_ffn_Wv"].transpose(1, 0, 2).reshape(D, 320))
    arrs["l0_wo2"] = np.ascontiguousarray(arrs["L0_ffn_Wo"].reshape(320, D))
    SP = S + 128
    nkc = SP // 128
    H = 16
    arrs["l1a_qs"] = np.tile(arrs["L1_attn_q_scale"], 2)[:, None]
    arrs["l1a_ks"] = np.tile(arrs["L1_attn_k_scale"], 2)[:, None]
    nl = arrs["L1_attn_null_kv"].reshape(16, 2, 2, 64)
    arrs["l1a_nk"] = np.ascontiguousarray(
        nl[:, :, 0, :].transpose(0, 2, 1).reshape(D, 2)).astype(BFNP)
    arrs["l1a_nv"] = np.ascontiguousarray(
        nl[:, :, 1, :].transpose(0, 2, 1).reshape(D, 2)).astype(BFNP)
    slopes = np.array([(2.0 ** (-2.0 ** (-(math.log2(H) - 3)))) ** (i + 1)
                       for i in range(H)], f32)
    dg = np.zeros((4, 128, 512), f32)
    for j in range(4):
        pp, cc2 = np.meshgrid(np.arange(128), np.arange(512), indexing="ij")
        dg[j] = np.where(pp + 128 * j > cc2, NEG, 0.0)
    arrs["c_l1a_diag"] = dg
    ka = np.zeros((H, 2, SP), f32)
    for h in range(H):
        ka[h, 0, :] = slopes[h] * 64.0 / 8.0
        ka[h, 1, :] = slopes[h] / 8.0
    arrs["c_l1a_kaug"] = ka.astype(BFNP)
    KV = T + 512
    arrs["l1f_qs"] = np.tile(arrs["L1_ffn_q_scale"], (1, 2))[:, :, None]
    arrs["l1f_ks"] = np.tile(arrs["L1_ffn_k_scale"], (1, 2))[:, :, None]
    pp, cc2 = np.meshgrid(np.arange(128), np.arange(128), indexing="ij")
    arrs["c_l1f_diag"] = np.where(pp > cc2, NEG, 0.0).astype(f32)
    arrs["c_l1f_prev"] = np.where(cc2 > pp, NEG, 0.0).astype(f32)
    w1 = arrs["L1_ffn_w1"]
    b1 = arrs["L1_ffn_b1"]
    w1p = np.zeros((2, D, 5632), f32)
    b1p = np.zeros((2, 5632), f32)
    for j in range(22):
        a0, a1 = j * 128, min(2730, (j + 1) * 128)
        nn_ = a1 - a0
        w1p[:, :, j * 256 : j * 256 + nn_] = w1[:, :, a0:a1]
        w1p[:, :, j * 256 + 128 : j * 256 + 128 + nn_] = w1[:, :, 2730 + a0 :
                                                            2730 + a1]
        b1p[:, j * 256 : j * 256 + nn_] = b1[:, a0:a1]
        b1p[:, j * 256 + 128 : j * 256 + 128 + nn_] = b1[:, 2730 + a0 :
                                                         2730 + a1]
    arrs["l1f_w1p"] = w1p
    arrs["l1f_b1p"] = b1p
    x = arrs.pop("L0_attn_x")
    B = x.shape[0]
    percore = {"x_bT": [], "x_ownT": [], "sel01": [], "l1a_qaug": [],
               "l1a_row66": [], "l1a_bias": [], "l1f_prev0": [],
               "l1f_qcs": [], "l1f_qss": [], "l1f_kcs": [], "l1f_kss": []}
    for c in range(n):
        b, h = c // 2, c % 2
        xbT = np.ascontiguousarray(x[b].T)
        percore["x_bT"].append(xbT)
        percore["x_ownT"].append(np.ascontiguousarray(xbT[:, h * T:(h + 1) * T]))
        s01 = np.zeros((128, 2), f32)
        s01[:, 0] = 1.0 if h == 0 else 0.0
        s01[:, 1] = 0.0 if h == 0 else 1.0
        percore["sel01"].append(s01)
        own0 = h * T
        pair0 = (1 - h) * T
        qg = own0 + np.arange(T)
        qa = np.zeros((3, T), f32)
        qa[0] = -(qg // 64)
        qa[1] = -(qg % 64)
        qa[2] = 1.0
        percore["l1a_qaug"].append(qa.astype(BFNP))
        r66 = np.zeros((1, SP), f32)
        if h == 0:
            r66[0, T:S] = NEG
        r66[0, S + 2 :] = NEG
        percore["l1a_row66"].append(r66.astype(BFNP))
        bias = np.zeros((H, 128, nkc), f32)
        cloc = (np.arange(128)[:, None] + 128 * np.arange(nkc)[None, :])
        jg = np.zeros_like(cloc, dtype=np.float64)
        m_own = cloc < T
        m_pair = (cloc >= T) & (cloc < S)
        m_null = (cloc >= S) & (cloc < S + 2)
        jg[m_own] = own0 + cloc[m_own] + 2
        jg[m_pair] = pair0 + (cloc[m_pair] - T) + 2
        jg[m_null] = cloc[m_null] - S
        for hh in range(H):
            bias[hh] = slopes[hh] * (jg - 2.0)
            bias[hh][~(m_own | m_pair | m_null)] = 0.0
        percore["l1a_bias"].append(bias)
        pv0 = np.zeros((4, 128, 512), f32)
        for j in range(4):
            nqw = j * 128 + 128
            if h == 0:
                pv0[j, :, 0:nqw] = NEG
            else:
                ppj, ccj = np.meshgrid(np.arange(128), np.arange(128),
                                       indexing="ij")
                pv0[j, :, j * 128 : j * 128 + 128] = np.where(ccj > ppj, NEG,
                                                              0.0)
        percore["l1f_prev0"].append(pv0)
        DH2, WIN = 64, 512
        half_idx = np.arange(0, DH2, 2, dtype=np.float64)
        inv_freq = 1.0 / 10000.0 ** (half_idx / DH2)
        base = (half_idx + 0.4 * DH2) / (1.4 * DH2)
        def rot_tabs(tg):
            fr = tg[None, :] * inv_freq[:, None]
            fr = np.concatenate([fr, fr], 0)
            power = (tg - S // 2) / float(WIN // 2)
            sc = base[:, None] ** power[None, :]
            sc = np.concatenate([sc, sc], 0)
            return np.cos(fr), np.sin(fr), sc
        tq = (own0 + np.arange(T)).astype(np.float64)
        cosq, sinq, scq = rot_tabs(tq)
        tk = (own0 - 512 + np.arange(KV)).astype(np.float64)
        cosk, sink, sck = rot_tabs(tk)
        percore["l1f_qcs"].append(np.tile(cosq * scq, (2, 1)).astype(BFNP))
        percore["l1f_qss"].append(np.tile(sinq * scq, (2, 1)).astype(BFNP))
        percore["l1f_kcs"].append(np.tile(cosk / sck, (2, 1)).astype(BFNP))
        percore["l1f_kss"].append(np.tile(sink / sck, (2, 1)).astype(BFNP))
    return arrs, percore


_CACHE = {}


def run_cores(inputs, cfg):
    arrs, percore = host_prep(inputs, cfg)
    key = (cfg.S, cfg.n_cores, tuple(cfg.parts))
    if key not in _CACHE:
        nc, P = build_program(cfg, arrs, percore)
        _CACHE[key] = (nc, P)
    nc, P = _CACHE[key]
    in_maps = []
    for c in range(cfg.n_cores):
        m = {}
        for name, a in arrs.items():
            m[name] = a
        for name, a in P.consts.items():
            m[name] = a
        for name, lst in percore.items():
            m[name] = lst[c]
        in_maps.append(m)
    res = run_bass_kernel_spmd(nc, in_maps, core_ids=list(range(cfg.n_cores)))
    return res.results


def kernel(**inputs):
    cfg = Cfg(S=4096, n_cores=8)
    results = run_cores(inputs, cfg)
    B, S = 4, cfg.S
    out = np.zeros((B, S, D), np.float32)
    for c in range(cfg.n_cores):
        b, h = c // 2, c % 2
        out[b, h * cfg.T:(h + 1) * cfg.T, :] = results[c]["out"].T
    return out


def cc_ring_loader(P, ring, cc_out, cc_tick, S):
    T = S // 2
    def f(c, t0, nt):
        def fill(slot, war):
            sl_ = 0 if t0 < T else 1
            tl = t0 - sl_ * T
            t = P.dma(slot[:, 0:nt],
                      cc_out.ap()[sl_, c * 128 : c * 128 + 128, tl : tl + nt],
                      queue="sync", wait=war + [cc_tick])
            return [t]
        assert t0 // T == (t0 + nt - 1) // T
        ap, ticks = ring.get(("cc", id(cc_out), c, t0, nt), fill)
        return ap[:, 0:nt], ticks
    return f


def pe_transpose128(P, *, src_ap, src_ticks, dst_ap, nrows, ident_off=0,
                    bank=5, dst_war=(), fp32=False):
    ident = P.shared["identf"] if fp32 else P.shared["ident"]
    ps = P.pbank(bank)
    P.wait("tensor", [P.pfree[bank]] + src_ticks + P.shared["tick"])
    nc_ = src_ap.shape[1]
    mm_t = P.inc("tensor", lambda e: e.matmul(
        ps[0:nc_, 0:nrows], src_ap,
        ident[ident_off : ident_off + nrows, ident_off : ident_off + nrows],
        start=True, stop=True), "s_mm")
    P.wait("vector", [mm_t] + [t for t in dst_war if t])
    ev = P.inc("vector", lambda e: e.tensor_copy(dst_ap, ps[0:nc_, 0:nrows]),
               "s_dve")
    P.pfree[bank] = ev
    return ev



def build_l0f(P, C):
    cfg = P.cfg
    S, T = cfg.S, cfg.T
    W = C["w"]
    x = C["x"]
    xb = C["xb"]
    sq = C["sq"]
    cc_out, cc_tick = C["cc_x1"]
    nts_f = S // 128
    nts_o = T // 128
    nqt = T // 512
    nkc = S // 128
    H, DH, E = 8, 64, 5
    m_blk = P.ab.mark()
    mf_blk = P.af.mark()
    ccl = cc_ring_loader(P, C["ring"], cc_out, cc_tick, S)
    own = sbuf_loader(P, xb)

    glv = P.af.alloc(nts_f * 40)
    glo = P.af.alloc(nts_o * 40)
    wv = P.ab.alloc(nts_f * 40)
    mo = P.ab.alloc(nts_o * 40)
    sg_t = P.af.alloc(40)
    mx = P.af.alloc(16)
    def topk_ops(gt, dst, ts, do_sig, tick_in):
        g5 = gt.rearrange("p (h e) -> p h e", e=5)
        m1 = mx[:, 0:8]
        m2 = mx[:, 8:16]
        P.wait("vector", [tick_in, P.tick("s_act")])
        t1 = P.inc("vector", lambda e: e.reduce_max(m1, g5, axis=AX.X), "s_dve")
        P.wait("vector", [t1])
        t2 = P.inc("vector", lambda e: e.tensor_tensor(
            dst.rearrange("p (h e) -> p h e", e=5), g5,
            m1.rearrange("p (h o) -> p h o", o=1).broadcast_to((128, 8, 5)),
            ALU.is_ge), "s_dve")
        P.wait("vector", [t2])
        t3 = P.inc("vector", lambda e: e.scalar_tensor_tensor(
            sg_t[0:128, 0:40], dst, -1e30, gt, ALU.mult, ALU.add), "s_dve")
        P.wait("vector", [t3])
        t4 = P.inc("vector", lambda e: e.reduce_max(
            m2, sg_t[0:128, 0:40].rearrange("p (h e) -> p h e", e=5),
            axis=AX.X), "s_dve")
        P.wait("vector", [t4])
        t5 = P.inc("vector", lambda e: e.tensor_tensor(
            dst.rearrange("p (h e) -> p h e", e=5), g5,
            m2.rearrange("p (h o) -> p h o", o=1).broadcast_to((128, 8, 5)),
            ALU.is_ge), "s_dve")
        out_t = t5
        if do_sig:
            P.wait("scalar", [t5, tick_in])
            t6 = P.inc("scalar", lambda e: e.activation(
                sg_t[0:128, 0:40], gt, ACTF.Sigmoid), "s_act")
            P.wait("vector", [t6])
            out_t = P.inc("vector", lambda e: e.tensor_mul(
                dst, dst, sg_t[0:128, 0:40]), "s_dve")
        return out_t

    wv_ticks = {}
    def glv_out(ti, f0, nf):
        return glv[:, ti * 40 : ti * 40 + 40]
    lt, mk0 = linear_tm(P, W=W["L0_ffn_Ws"].ap(), rows=D, Fout=40, rhs=ccl,
                        tslices=[(i * 128, 128) for i in range(nts_f)],
                        out=glv_out, banks=(0, 1, 2, 3))
    for ti in range(nts_f):
        wv_ticks[ti] = topk_ops(glv[:, ti * 40 : ti * 40 + 40],
                                wv[:, ti * 40 : ti * 40 + 40], ti, True, lt)
    P.barrier()
    P.ab.release(mk0)
    lt, mk0 = linear_tm(P, W=W["L0_ffn_Wd"].ap(), rows=D, Fout=40, rhs=own,
                        tslices=[(i * 128, 128) for i in range(nts_o)],
                        out=lambda ti, f0, nf: glo[:, ti * 40 : ti * 40 + 40],
                        banks=(0, 1, 2, 3))
    mo_ticks = {}
    for ti in range(nts_o):
        mo_ticks[ti] = topk_ops(glo[:, ti * 40 : ti * 40 + 40],
                                mo[:, ti * 40 : ti * 40 + 40], ti, False, lt)
    P.barrier()
    P.ab.release(mk0)

    mkv = P.ab.mark()
    xvt = [P.ab.alloc(320), P.ab.alloc(320)]
    vtt = [P.ab.alloc(512), P.ab.alloc(512)]
    vstage = [P.ab.alloc(512) for _ in range(4)]
    vst_dma = [None] * 4
    asm_tick = {}

    def v_asm(ti, f0, nf, tick):
        xv = xvt[ti % 2]
        vt = vtt[ti % 2]
        wvt = wv[:, ti * 40 : ti * 40 + 40].rearrange("p (h e) -> p h e", e=5)
        P.wait("vector", [tick, wv_ticks[ti], asm_tick.get(ti % 2)])
        tacc = None
        for e_ in range(E):
            wcol = wvt[:, :, e_ : e_ + 1].broadcast_to((128, 8, 64)) \
                .rearrange("p h k -> p (h k)")
            xcol = xv[:, e_ * 64 : e_ * 64 + 64] \
                .rearrange("p (o k) -> p o k", o=1).broadcast_to((128, 8, 64)) \
                .rearrange("p h k -> p (h k)")
            if e_ == 0:
                tacc = P.inc("vector", lambda e, vt=vt, wcol=wcol, xcol=xcol:
                             e.tensor_tensor(vt[:, 0:512], wcol, xcol, ALU.mult),
                             "s_dve")
            else:
                tmp = sg_t[0:128, 0:40]
                pass
        return tacc

    vtmp = P.ab.alloc(512)

    def v_asm(ti, f0, nf, tick):
        xv = xvt[ti % 2]
        vt = vtt[ti % 2]
        wvt = wv[:, ti * 40 : ti * 40 + 40].rearrange("p (h e) -> p h e", e=5)
        P.wait("vector", [tick, wv_ticks[ti], asm_tick.get(ti % 2)])
        tl = None
        tl = None
        for e_ in range(E):
            wcol = wvt[:, :, e_ : e_ + 1].broadcast_to((128, 8, 64))
            xcol = xv[:, e_ * 64 : e_ * 64 + 64] \
                .rearrange("p (o k) -> p o k", o=1).broadcast_to((128, 8, 64))
            dst = (vt[:, 0:512] if e_ == 0 else
                   vtmp[:, 0:512]).rearrange("p (h k) -> p h k", k=64)
            tm = P.inc("vector", lambda e, dst=dst, wcol=wcol, xcol=xcol:
                       e.tensor_tensor(dst, wcol, xcol, ALU.mult), "s_dve")
            P.wait("vector", [tm])
            if e_ > 0:
                tl = P.inc("vector", lambda e, vt=vt:
                           e.tensor_add(vt[:, 0:512], vt[:, 0:512],
                                        vtmp[:, 0:512]), "s_dve")
                P.wait("vector", [tl])
        for pr in range(4):
            ev = pe_transpose128(P, src_ap=vt[:, pr * 128 : pr * 128 + 128],
                                 src_ticks=[tl], nrows=128,
                                 dst_ap=vstage[pr][:, (ti % 4) * 128 :
                                                   (ti % 4) * 128 + 128],
                                 bank=4 + (pr % 2), dst_war=[vst_dma[pr]])
            if ti % 4 == 3:
                vst_dma[pr] = P.dma(
                    C["l0f_vT"].ap()[pr, :, (ti - 3) * 128 : (ti + 1) * 128],
                    vstage[pr][:, 0:512], queue="sync", wait=[ev])
        asm_tick[ti % 2] = tl

    def xv_wait(ti):
        w = [asm_tick.get(ti % 2)]
        if ti % 4 < 4:
            w += [vst_dma[pr] for pr in range(4)]
        return [t for t in w if t]

    lt2, mk1 = linear_tm(P, W=W["l0_wv2"].ap(), rows=D, Fout=320, rhs=ccl,
                         tslices=[(i * 128, 128) for i in range(nts_f)],
                         out=lambda ti, f0, nf: xvt[ti % 2][:, f0 : f0 + nf],
                         banks=(0, 1, 2, 3), on_out=v_asm, evac_wait=xv_wait)
    P.barrier()
    P.ab.release(mkv)

    mkq = P.ab.mark()
    stq = StageOC(P, T)
    stq.dram_fn = lambda gf, fh, t0, nt: C["l0f_q"].ap()[gf // 128, 0:fh,
                                                         t0 : t0 + nt]
    linear_fm(P, W=W["L0_ffn_Wq"].ap(), rows=D, Fout=512,
              rhs=own, tokens=tok_chunks(T),
              out=stq.out, evac_wait=stq.evac_wait, on_slice=stq.on_slice,
              fblock=256, nparity=2)
    stk = StageOC(P, S)
    stk.dram_fn = lambda gf, fh, t0, nt: C["l0f_k"].ap()[gf // 128, 0:fh,
                                                         t0 : t0 + nt]
    linear_fm(P, W=W["L0_ffn_Wk"].ap(), rows=D, Fout=512,
              rhs=ccl, tokens=tok_chunks(S),
              out=stk.out, evac_wait=stk.evac_wait, on_slice=stk.on_slice,
              fblock=256, nparity=2)
    P.barrier()
    P.ab.release(mkq)

    m_attn = P.ab.mark()
    mf_attn = P.af.mark()
    Ot = [P.ab.alloc(T) for _ in range(4)]
    sums = P.af.alloc(2 * 512, parts=97)
    recb = P.ab.alloc(2 * 512, parts=97)
    m_heads = P.ab.mark()
    Kt = P.ab.alloc(S)
    vst = P.ab.alloc(8 * 128)
    Vtok = P.ab.alloc(nkc * 65)
    Qt = P.ab.alloc(T)
    ptile = P.ab.alloc(6 * 512)
    P.inc("vector", lambda e: e.memset(
        Vtok[:, 0 : nkc * 65].rearrange("p (k v) -> p k v", v=65)
        [:, :, 64:65], 1.0), "s_dve")
    P.barrier()
    vst_war = None
    prev_bc = []
    tr = None
    hq_mm = None
    for pr in range(4):
        prev = [hq_mm] if hq_mm else [P.tick("s_mm"), P.tick("s_dve"),
                                      P.tick("s_act")]
        tk_k = P.dma(Kt, C["l0f_k"].ap()[pr], queue="sync", wait=prev)
        tk_q = P.dma(Qt, C["l0f_q"].ap()[pr], queue="sync", wait=prev)
        for hh in range(2):
            off = hh * 64
            vprev = [hq_mm] if hq_mm else [P.tick("s_mm")]
            vticks = []
            for g0 in range(0, nkc, 8):
                dts = [P.dma(vst[:, j * 128 : j * 128 + 128],
                             C["l0f_vT"].ap()[pr, :, (g0 + j) * 128 :
                                              (g0 + j) * 128 + 128],
                             queue="gpsimd", wait=[vst_war] + vprev)
                       for j in range(8)]
                b = (g0 // 8) % 6
                ps = P.pbank(b)
                P.wait("tensor", [P.pfree[b]] + dts + P.shared["tick"])
                def tmm(e, ps=ps, off=off):
                    last = None
                    for j in range(8):
                        last = e.matmul(
                            ps[0:128, j * 64 : j * 64 + 64],
                            vst[off : off + 64, j * 128 : j * 128 + 128],
                            P.shared["ident"][off : off + 64, off : off + 64],
                            start=True, stop=True)
                    return last
                mm_t = P.inc("tensor", tmm, "s_mm")
                vst_war = mm_t
                P.wait("vector", [mm_t])
                def ev(e, ps=ps, g0=g0):
                    last = None
                    for j in range(8):
                        last = e.tensor_copy(
                            Vtok[:, (g0 + j) * 65 : (g0 + j) * 65 + 64],
                            ps[0:128, j * 64 : j * 64 + 64])
                    return last
                ev_t = P.inc("vector", ev, "s_dve")
                P.pfree[b] = ev_t
                vticks.append(ev_t)
            h = pr * 2 + hh
            P.wait("vector", [tr])
            ts_ms = P.inc("vector", lambda e: e.memset(sums[0:97, :], 1.0),
                          "s_dve")
            qticks = [ts_ms]
            for qt in range(nqt):
                sr, sc = 32 * (qt % 2), 512 * (qt // 2)
                chunks = [dict(kc=j * 128, nk=128, vi=j, qc0=0, nqw=512)
                          for j in range(nkc)]
                ts = attn_qtile(P, Kt=Kt[off : off + 64, :], Kticks=[tk_k],
                                Qt=Qt[off : off + 64, :], Qticks=[tk_q],
                                Vt=Vtok, Vticks=vticks + [ts_ms], dh=64,
                                daug=64,
                                q0=qt * 512, nq=512,
                                chunks=chunks, scale=1.0 / 8.0, octx=6,
                                vsum="col", onescol_b=P.shared["onescol"],
                                sums_dst=sums[sr : sr + 1, sc : sc + 512],
                                O_dst=Ot[pr][off : off + 64,
                                             qt * 512 : qt * 512 + 512],
                                ptile=ptile)
                qticks += ts
            hq_mm = P.tick("s_mm")
            P.wait("scalar", qticks + prev_bc)
            tl = P.inc("scalar", lambda e:
                       e.activation(sums[0:97, :], sums[0:97, :],
                                    ACTF.Ln), "s_act")
            P.wait("scalar", [tl])
            tr = P.inc("scalar", lambda e:
                       e.activation(recb[0:97, :], sums[0:97, :],
                                    ACTF.Exp, scale=-1.0), "s_act")
            prev_bc = []
            for qt in range(nqt):
                sr, sc = 32 * (qt % 2), 512 * (qt // 2)
                b = 6 + qt % 2
                ps = P.pbank(b)
                P.wait("tensor", [P.pfree[b], tr] + P.shared["tick"])
                mm_t = P.inc("tensor", lambda e, ps=ps, sr=sr, sc=sc, hh=hh:
                             e.matmul(ps[hh * 64 : hh * 64 + 64, 0:512],
                                      P.shared["ones"][sr : sr + 1,
                                                       hh * 64 : hh * 64 + 64],
                                      recb[sr : sr + 1, sc : sc + 512],
                                      start=True, stop=True), "s_mm")
                prev_bc.append(mm_t)
                P.wait("vector", [mm_t])
                tn = P.inc("vector", lambda e, ps=ps, pr=pr, hh=hh, qt=qt:
                           e.scalar_tensor_tensor(
                               Ot[pr][hh * 64 : hh * 64 + 64,
                                      qt * 512 : qt * 512 + 512],
                               Ot[pr][hh * 64 : hh * 64 + 64,
                                      qt * 512 : qt * 512 + 512], 1.0,
                               ps[hh * 64 : hh * 64 + 64, 0:512],
                               ALU.mult, ALU.mult), "s_dve")
                P.pfree[b] = tn
    P.barrier()
    P.ab.release(m_heads)

    atok = [P.ab.alloc(512), P.ab.alloc(512)]
    ztok = [P.af.alloc(320), P.af.alloc(320)]
    ztmp = P.ab.alloc(512)
    zT = [P.ab.alloc(T) for _ in range(3)]
    zstage = [P.ab.alloc(512) for _ in range(3)]
    zst_ev = {}
    at_tick = {}
    zt_tick = {}
    zdma = [None] * 3
    for ti in range(nts_o):
        ts0 = ti * 128
        at = atok[ti % 2]
        evs = []
        for pr in range(4):
            ev = pe_transpose128(P, src_ap=Ot[pr][:, ts0 : ts0 + 128],
                                 src_ticks=[], nrows=128,
                                 dst_ap=at[:, pr * 128 : pr * 128 + 128],
                                 bank=4 + (pr % 2),
                                 dst_war=[at_tick.get(ti % 2)])
            evs.append(ev)
        zt = ztok[ti % 2]
        moi = mo[:, ti * 40 : ti * 40 + 40].rearrange("p (h e) -> p h e", e=5)
        tz = None
        for e_ in range(E):
            mcol = moi[:, :, e_ : e_ + 1].broadcast_to((128, 8, 64))
            P.wait("vector", evs + [mo_ticks[ti], zt_tick.get(ti % 2)])
            tm = P.inc("vector", lambda e, at=at, mcol=mcol:
                       e.tensor_tensor(
                           ztmp[:, 0:512].rearrange("p (h k) -> p h k", k=64),
                           at.rearrange("p (h k) -> p h k", k=64), mcol,
                           ALU.mult), "s_dve")
            P.wait("vector", [tm])
            tz = P.inc("vector", lambda e, zt=zt, e_=e_:
                       e.reduce_sum(zt[:, e_ * 64 : e_ * 64 + 64],
                                    ztmp[:, 0:512]
                                    .rearrange("p (h k) -> p k h", h=8),
                                    axis=AX.X), "s_dve")
            P.wait("vector", [tz])
        zt_tick[ti % 2] = tz
        at_tick[ti % 2] = tz
        for zb in range(3):
            ncol = 128 if zb < 2 else 64
            ev = pe_transpose128(P, src_ap=zt[:, zb * 128 : zb * 128 + ncol],
                                 src_ticks=[tz], nrows=128,
                                 dst_ap=zT[zb][0:ncol, ts0 : ts0 + 128],
                                 bank=4 + (zb % 2), dst_war=[], fp32=True)
        zt_tick[ti % 2] = ev
    P.barrier()

    mko = P.ab.mark()
    def zrhs(c, t0, nt):
        return zT[c][0 : (128 if c < 2 else 64), t0 : t0 + nt], []
    linear_fm(P, W=W["l0_wo2"].ap(), rows=320, Fout=D, rhs=zrhs,
              tokens=tok_chunks(T),
              out=lambda gf, t0, nt: x[gf // 128][:, t0 : t0 + nt],
              stt_resid=lambda gf, t0, nt: x[gf // 128][:, t0 : t0 + nt],
              fblock=256, nparity=2)
    P.barrier()
    P.ab.release(m_attn)
    P.af.release(mf_attn)

    mkt = P.ab.mark()
    xcast = [P.ab.alloc(T) for _ in range(NCH)]
    for c in range(NCH):
        P.inc("vector", lambda e, c=c: e.tensor_scalar_mul(xcast[c][:, 0:T],
                                                           x[c][:, 0:T], 1.0),
              "s_dve")
    P.barrier()
    stt = ln_stats(P, xhat=sbuf_loader(P, xcast), Ttot=T, sq_scratch=sq,
                   want_f32=True)
    topln(P, xadd_f32=lambda c: (x[c], []), xhat_out=lambda c: xb[c],
          st=stt, T=T)
    P.barrier()
    P.ab.release(mkt)
    P.ab.release(m_blk)
    P.af.release(mf_blk)




def build_l1a(P, C):
    cfg = P.cfg
    S, T = cfg.S, cfg.T
    W = C["w"]
    x, xb, sq = C["x"], C["xb"], C["sq"]
    cc_out, cc_tick = C["cc_x2"]
    SP = S + 128
    nkc = SP // 128
    nqt = T // 512
    H = 16
    m_blk = P.ab.mark()
    mf_blk = P.af.mark()
    own = sbuf_loader(P, xb)

    cols = {}
    colticks = []
    for nm in ("L1_attn_norm_g", "L1_attn_norm_b"):
        cols[nm], tk = load_bias_cols(P, W[nm].ap(), D)
        colticks += tk
    qs_col = P.af.alloc(2)
    ks_col = P.af.alloc(2)
    colticks.append(P.dma(qs_col[:, 0:1], P.dram["l1a_qs"].ap()))
    colticks.append(P.dma(ks_col[:, 0:1], P.dram["l1a_ks"].ap()))
    blk2T = P.ab.alloc(16)
    blk2 = P.ab.alloc(128, parts=2)
    P.const("c_blk2T", np.concatenate(
        [np.concatenate([np.ones((64, 1)), np.zeros((64, 1))]),
         np.concatenate([np.zeros((64, 1)), np.ones((64, 1))])], 1), BF16)
    P.const("c_blk2", np.stack([np.concatenate([np.ones(64), np.zeros(64)]),
                                np.concatenate([np.zeros(64), np.ones(64)])]),
            BF16)
    colticks.append(P.dma(blk2T[:, 0:2], P.dram["c_blk2T"].ap()))
    colticks.append(P.dma(blk2, P.dram["c_blk2"].ap()))
    l2lanes = [dict(nsq=P.ab.alloc(512), nrm=P.af.alloc(512, parts=2),
                    bank=6, mm=None, t5=None),
               dict(nsq=P.ab.alloc(512), nrm=P.af.alloc(512, parts=2),
                    bank=7, mm=None, t5=None)]
    l2i = [0]
    P.barrier()

    mrk = P.ab.mark()
    st_o = ln_stats(P, xhat=own, Ttot=T, sq_scratch=sq)
    xn = norm_ring_loader(P, C["ring"], own, st_o,
                          cols["L1_attn_norm_g"], cols["L1_attn_norm_b"],
                          colticks)

    def l2n_slice(stage_ap, fh, t0, nt, tick, sc_col, bank=None):
        L = l2lanes[l2i[0] % 2]
        l2i[0] += 1
        nsq, nrm, bank = L["nsq"], L["nrm"], L["bank"]
        ps = P.pbank(bank)
        P.wait("vector", [tick, L["mm"]])
        t1 = P.inc("vector", lambda e: e.tensor_mul(nsq[:, 0:nt], stage_ap,
                                                    stage_ap), "s_dve")
        P.wait("tensor", [t1, P.pfree[bank]] + colticks)
        t2 = P.inc("tensor", lambda e: e.matmul(ps[0:2, 0:nt], blk2T[:, 0:2],
                                                nsq[:, 0:nt], start=True,
                                                stop=True), "s_mm")
        P.wait("scalar", [t2, L["t5"]])
        t3 = P.inc("scalar", lambda e: e.activation(nrm[0:2, 0:nt],
                                                    ps[0:2, 0:nt], ACTF.Ln),
                   "s_act")
        P.wait("scalar", [t3])
        t4 = P.inc("scalar", lambda e: e.activation(nrm[0:2, 0:nt],
                                                    nrm[0:2, 0:nt], ACTF.Exp,
                                                    scale=-0.5), "s_act")
        P.wait("vector", [t4])
        t5 = P.inc("vector", lambda e: e.tensor_scalar_mul(
            nsq[0:2, 0:nt], nrm[0:2, 0:nt], 1.0), "s_dve")
        P.wait("tensor", [t5, P.pfree[bank]])
        t6 = P.inc("tensor", lambda e: e.matmul(ps[0:128, 0:nt], blk2,
                                                nsq[0:2, 0:nt], start=True,
                                                stop=True), "s_mm")
        L["mm"] = t6
        L["t5"] = t5
        P.wait("vector", [t6])
        t7 = P.inc("vector", lambda e: e.scalar_tensor_tensor(
            stage_ap, stage_ap, sc_col[0:128, 0:1], ps[0:128, 0:nt],
            ALU.mult, ALU.mult), "s_dve")
        P.pfree[bank] = t7
        return t7

    stq = StageOC(P, T)
    stq.dram_fn = lambda gf, fh, t0, nt: C["l1a_q"].ap()[gf // 128, 0:fh,
                                                         t0 : t0 + nt]
    orig_slice = stq.on_slice
    def q_slice(gf, fh, t0, nt, tick):
        i = stq.cur[(gf, t0)]
        t7 = l2n_slice(stq.tiles[i][0:fh, 0:nt], fh, t0, nt, tick, qs_col)
        stq.lastev[i] = t7
        stq.dmatick[i] = P.dma(stq.dram_fn(gf, fh, t0, nt),
                               stq.tiles[i][0:fh, 0:nt], queue="sync",
                               wait=[t7])
    linear_fm(P, W=W["L1_attn_Wq"].ap(), rows=D, Fout=D,
              rhs=xn, tokens=tok_chunks(T),
              out=stq.out, evac_wait=stq.evac_wait, on_slice=q_slice,
              fblock=512, nparity=2)
    P.barrier()
    P.ab.release(mrk)

    mrk = P.ab.mark()
    exA = P.ab.alloc(512)
    exB = P.ab.alloc(512)
    ex_t = [None]
    sel = C["sel"]

    def kvrhs(c, t0, nt):
        if t0 < T:
            return xb[c][:, t0 : t0 + nt], []
        def fill(slot, war):
            tl = t0 - T
            tA = P.dma(exA[:, 0:nt], cc_out.ap()[0, c * 128 : c * 128 + 128,
                                                 tl : tl + nt],
                       queue="sync", wait=war + [cc_tick, ex_t[0]])
            tB = P.dma(exB[:, 0:nt], cc_out.ap()[1, c * 128 : c * 128 + 128,
                                                 tl : tl + nt],
                       queue="sync", wait=[cc_tick, ex_t[0]])
            P.wait("vector", [tA, tB])
            t1 = P.inc("vector", lambda e: e.tensor_scalar_mul(
                slot[:, 0:nt], exA[:, 0:nt], sel[0:128, 1:2]), "s_dve")
            P.wait("vector", [t1])
            t2 = P.inc("vector", lambda e: e.scalar_tensor_tensor(
                slot[:, 0:nt], exB[:, 0:nt], sel[0:128, 0:1], slot[:, 0:nt],
                ALU.mult, ALU.add), "s_dve")
            ex_t[0] = t2
            return [t2]
        ap, tk = C["ring"].get(("l1kv", c, t0, nt), fill)
        return ap[:, 0:nt], tk

    stk = StageOC(P, S)
    def kv_dst(gf, fh, t0, nt):
        h = gf // 128
        if h < 8:
            return C["l1a_k"].ap()[h, 0:fh, t0 : t0 + nt]
        return C["l1a_vT"].ap()[h - 8, 0:fh, t0 : t0 + nt]
    stk.dram_fn = kv_dst
    def kv_slice(gf, fh, t0, nt, tick):
        i = stk.cur[(gf, t0)]
        if gf < D:
            t7 = l2n_slice(stk.tiles[i][0:fh, 0:nt], fh, t0, nt, tick, ks_col)
            stk.lastev[i] = t7
            stk.dmatick[i] = P.dma(stk.dram_fn(gf, fh, t0, nt),
                                   stk.tiles[i][0:fh, 0:nt], queue="sync",
                                   wait=[t7])
        else:
            stk.on_slice(gf, fh, t0, nt, tick)
    linear_fm(P, W=W["L1_attn_Wkv"].ap(), rows=D, Fout=2 * D, rhs=kvrhs,
              tokens=tok_chunks(S), out=stk.out, evac_wait=stk.evac_wait,
              on_slice=kv_slice, fblock=256, nparity=2)
    P.barrier()

    nkt = P.ab.alloc(16)
    nvt = P.ab.alloc(16)
    od1 = od2 = None
    for pr in range(8):
        tk1 = P.dma(nkt[:, 0:2], P.dram["l1a_nk"].ap()
                    [pr * 128 : pr * 128 + 128, :], queue="gpsimd",
                    wait=[P.tick("s_dve"), P.tick("s_mm"), od1])
        tk2 = P.dma(nvt[:, 0:2], P.dram["l1a_nv"].ap()
                    [pr * 128 : pr * 128 + 128, :], queue="gpsimd", wait=[od2])
        t7 = l2n_slice(nkt[:, 0:2], 128, 0, 2, tk1, ks_col, bank=6)
        od1 = P.dma(C["l1a_k"].ap()[pr, :, S : S + 2], nkt[:, 0:2],
                    queue="sync", wait=[t7])
        od2 = P.dma(C["l1a_vT"].ap()[pr, :, S : S + 2], nvt[:, 0:2],
                    queue="sync", wait=[tk2])
    P.barrier()
    P.ab.release(mrk)

    m_attn = P.ab.mark()
    mf_attn = P.af.mark()
    sums = P.af.alloc(2 * 512, parts=97)
    recb = P.ab.alloc(2 * 512, parts=97)
    dmask = P.af.alloc(4 * 512)
    for j in range(4):
        dmt = P.dma(dmask[:, j * 512 : j * 512 + 512],
                    P.dram["c_l1a_diag"].ap()[j], queue="sync")
    bias_t = P.af.alloc(H * nkc)
    for h_ in range(H):
        bt_t = P.dma(bias_t[:, h_ * nkc : h_ * nkc + nkc],
                     P.dram["l1a_bias"].ap()[h_], queue="sync")
    P.barrier()
    OtT = P.ab.alloc(T)
    Kt = P.ab.alloc(SP)
    vst = P.ab.alloc(8 * 128)
    Vtok = P.ab.alloc(nkc * 65)
    Qt = P.ab.alloc(T)
    ptile = P.ab.alloc(6 * 512)
    P.inc("vector", lambda e: e.memset(
        Vtok[:, 0 : nkc * 65].rearrange("p (k v) -> p k v", v=65)
        [:, :, 64:65], 1.0), "s_dve")
    P.barrier()
    tq_aug = P.dma(Qt[64:67, :], P.dram["l1a_qaug"].ap(), queue="sync")
    tq_one = tq_aug
    tk_r66 = P.dma(Kt[66:67, :], P.dram["l1a_row66"].ap(), queue="sync")
    vst_war = None
    prev_bc = []
    tr = None
    o_dma = None
    hq_mm = None
    for h in range(H):
        pr, hh = h // 2, h % 2
        prev = [hq_mm] if hq_mm else [P.tick("s_mm"), P.tick("s_dve"),
                                      P.tick("s_act")]
        tk_k = P.dma(Kt[0:64, 0 : S + 2],
                     C["l1a_k"].ap()[pr, hh * 64 : hh * 64 + 64, :],
                     queue="sync", wait=prev)
        tk_ka = P.dma(Kt[64:66, :], P.dram["c_l1a_kaug"].ap()[h], queue="sync",
                      wait=prev)
        tk_q = P.dma(Qt[0:64, :], C["l1a_q"].ap()[pr, hh * 64 : hh * 64 + 64, :],
                     queue="sync", wait=prev)
        tpad = P.inc("vector", lambda e: e.memset(Kt[0:64, S + 2 : SP], 0.0),
                     "s_dve")
        vticks = []
        for g0 in range(0, nkc, 8):
            gn = min(8, nkc - g0)
            dts = []
            for j in range(gn):
                k0 = (g0 + j) * 128
                ncl = max(0, min(128, S + 2 - k0))
                if ncl > 0:
                    dts.append(P.dma(vst[:, j * 128 : j * 128 + ncl],
                                     C["l1a_vT"].ap()[pr, :, k0 : k0 + ncl],
                                     queue="gpsimd", wait=[vst_war] + prev))
                if ncl < 128:
                    P.wait("vector", [vst_war] + prev)
                    dts.append(P.inc("vector", lambda e, j=j, ncl=ncl:
                                     e.memset(vst[:, j * 128 + ncl :
                                                  j * 128 + 128], 0.0),
                                     "s_dve"))
            b = (g0 // 8) % 6
            ps = P.pbank(b)
            P.wait("tensor", [P.pfree[b]] + dts + P.shared["tick"])
            def tmm(e, ps=ps, off=hh * 64, gn=gn):
                last = None
                for j in range(gn):
                    last = e.matmul(
                        ps[0:128, j * 64 : j * 64 + 64],
                        vst[off : off + 64, j * 128 : j * 128 + 128],
                        P.shared["ident"][off : off + 64, off : off + 64],
                        start=True, stop=True)
                return last
            mm_t = P.inc("tensor", tmm, "s_mm")
            vst_war = mm_t
            P.wait("vector", [mm_t])
            def ev(e, ps=ps, g0=g0, gn=gn):
                last = None
                for j in range(gn):
                    last = e.tensor_copy(
                        Vtok[:, (g0 + j) * 65 : (g0 + j) * 65 + 64],
                        ps[0:128, j * 64 : j * 64 + 64])
                return last
            ev_t = P.inc("vector", ev, "s_dve")
            P.pfree[b] = ev_t
            vticks.append(ev_t)
        loadt = [tk_k, tk_ka, tk_q, tq_aug, tq_one, tk_r66, tpad, bt_t, dmt]
        P.wait("vector", [tr])
        ts_ms = P.inc("vector", lambda e: e.memset(sums[0:97, :], 1.0),
                      "s_dve")
        qticks = [ts_ms]
        for qt in range(nqt):
            q0 = qt * 512
            chunks = []
            nown = q0 // 128 + 4
            for j in range(nown):
                ch = dict(kc=j * 128, nk=128, vi=j, qc0=0, nqw=512,
                          biascol=bias_t[:, h * nkc + j : h * nkc + j + 1])
                jj = j - q0 // 128
                if jj >= 0:
                    ch["maskpre"] = dmask[:, jj * 512 : jj * 512 + 512]
                    ch["mq0"] = 0
                    ch["mqw"] = 512
                chunks.append(ch)
            for j in range(T // 128, nkc):
                chunks.append(dict(kc=j * 128, nk=128, vi=j, qc0=0, nqw=512,
                                   biascol=bias_t[:, h * nkc + j :
                                                  h * nkc + j + 1]))
            sr, sc = 32 * (qt % 2), 512 * (qt // 2)
            if o_dma is not None:
                P.wait("vector", [o_dma, ts_ms])
            ts = attn_qtile(P, Kt=Kt, Kticks=loadt, Qt=Qt, Qticks=loadt,
                            Vt=Vtok, Vticks=vticks + [ts_ms], dh=64, daug=67,
                            q0=q0, nq=512,
                            chunks=chunks, scale=8.0, octx=6, vsum="col",
                            onescol_b=P.shared["onescol"],
                            sums_dst=sums[sr : sr + 1, sc : sc + 512],
                            O_dst=OtT[hh * 64 : hh * 64 + 64,
                                      qt * 512 : qt * 512 + 512],
                            ptile=ptile)
            qticks += ts
        hq_mm = P.tick("s_mm")
        P.wait("scalar", qticks + prev_bc)
        tl = P.inc("scalar", lambda e:
                   e.activation(sums[0:97, :], sums[0:97, :], ACTF.Ln),
                   "s_act")
        P.wait("scalar", [tl])
        tr = P.inc("scalar", lambda e:
                   e.activation(recb[0:97, :], sums[0:97, :],
                                ACTF.Exp, scale=-1.0), "s_act")
        prev_bc = []
        stt_ts = []
        for qt in range(nqt):
            sr, sc = 32 * (qt % 2), 512 * (qt // 2)
            b = 6 + qt % 2
            ps = P.pbank(b)
            P.wait("tensor", [P.pfree[b], tr] + P.shared["tick"])
            mm_t = P.inc("tensor", lambda e, ps=ps, sr=sr, sc=sc, hh=hh:
                         e.matmul(ps[hh * 64 : hh * 64 + 64, 0:512],
                                  P.shared["ones"][sr : sr + 1,
                                                   hh * 64 : hh * 64 + 64],
                                  recb[sr : sr + 1, sc : sc + 512],
                                  start=True, stop=True), "s_mm")
            prev_bc.append(mm_t)
            P.wait("vector", [mm_t])
            tn = P.inc("vector", lambda e, ps=ps, hh=hh, qt=qt:
                       e.scalar_tensor_tensor(
                           OtT[hh * 64 : hh * 64 + 64,
                               qt * 512 : qt * 512 + 512],
                           OtT[hh * 64 : hh * 64 + 64,
                               qt * 512 : qt * 512 + 512], 1.0,
                           ps[hh * 64 : hh * 64 + 64, 0:512],
                           ALU.mult, ALU.mult), "s_dve")
            P.pfree[b] = tn
            stt_ts.append(tn)
        if hh == 1:
            o_dma = P.dma(C["l1a_q"].ap()[pr], OtT[:, 0:T], queue="sync",
                          wait=stt_ts + [P.tick("s_dve")])
    P.wait("sync", [o_dma])
    P.barrier()

    def ctxrhs(c, t0, nt):
        def fill(slot, war, c=c, t0=t0, nt=nt):
            t = P.dma(slot[:, 0:nt],
                      C["l1a_q"].ap()[c, :, t0 : t0 + nt],
                      queue="sync", wait=war)
            return [t]
        ap, tk = C["ring"].get(("l1ctx", c, t0, nt), fill)
        return ap[:, 0:nt], tk
    linear_fm(P, W=W["L1_attn_Wout"].ap(), rows=D, Fout=D,
              rhs=ctxrhs, tokens=tok_chunks(T),
              out=lambda gf, t0, nt: x[gf // 128][:, t0 : t0 + nt],
              stt_resid=lambda gf, t0, nt: x[gf // 128][:, t0 : t0 + nt],
              fblock=256, nparity=2)
    P.barrier()
    P.ab.release(m_attn)
    P.af.release(mf_attn)

    mkt = P.ab.mark()
    for c in range(NCH):
        P.inc("vector", lambda e, c=c: e.tensor_scalar_mul(xb[c][:, 0:T],
                                                           x[c][:, 0:T], 1.0),
              "s_dve")
    P.barrier()
    stt = ln_stats(P, xhat=sbuf_loader(P, xb), Ttot=T, sq_scratch=sq,
                   want_f32=True)
    topln(P, xadd_f32=lambda c: (x[c], []), xhat_out=lambda c: xb[c],
          st=stt, T=T)
    P.barrier()
    P.ab.release(mkt)
    P.ab.release(m_blk)
    P.af.release(mf_blk)



def build_l1f(P, C):
    cfg = P.cfg
    S, T = cfg.S, cfg.T
    W = C["w"]
    x, xb, sq = C["x"], C["xb"], C["sq"]
    sel = C["sel"]
    KV = T + 512
    nw = T // 512
    m_blk = P.ab.mark()
    mf_blk = P.af.mark()

    for c in range(NCH):
        P.dma(C["l1f_x3"].ap()[c * 128 : c * 128 + 128, :], x[c][:, 0:T],
              queue="sync", wait=[P.tick("s_dve"), P.tick("s_act")])
    P.barrier()

    qs_col = [P.af.alloc(2) for _ in range(2)]
    ks_col = [P.af.alloc(2) for _ in range(2)]
    colticks = []
    for l in range(2):
        colticks.append(P.dma(qs_col[l][:, 0:1], P.dram["l1f_qs"].ap()[l]))
        colticks.append(P.dma(ks_col[l][:, 0:1], P.dram["l1f_ks"].ap()[l]))
    mtri_d = P.af.alloc(128)
    mtri_p = P.af.alloc(128)
    colticks.append(P.dma(mtri_d, P.dram["c_l1f_diag"].ap()))
    colticks.append(P.dma(mtri_p, P.dram["c_l1f_prev"].ap()))
    mprev0 = P.af.alloc(4 * 512)
    for j in range(4):
        colticks.append(P.dma(mprev0[:, j * 512 : j * 512 + 512],
                              P.dram["l1f_prev0"].ap()[j], queue="sync"))
    def rtab(nm, t0, nt):
        def fill(slot, war):
            t = P.dma(slot[:, 0:nt], P.dram[nm].ap()[:, t0 : t0 + nt],
                      queue="sync", wait=war)
            return [t]
        ap, tk = C["ring"].get((nm, t0), fill)
        return ap[:, 0:nt], tk
    l2lanes = [dict(nsq=P.ab.alloc(512), nrm=P.af.alloc(512, parts=2),
                    rsc=P.ab.alloc(512), rs2=P.ab.alloc(512),
                    bank=6, mm=None, t5=None)
               for _ in range(2)]
    l2lanes[1]["bank"] = 7
    l2i = [0]
    blk2T = P.ab.alloc(16)
    blk2 = P.ab.alloc(128, parts=2)
    colticks.append(P.dma(blk2T[:, 0:2], P.dram["c_blk2T"].ap()))
    colticks.append(P.dma(blk2, P.dram["c_blk2"].ap()))
    lng = {}
    for l in range(2):
        lng[(l, "g")], tk = load_bias_cols(P, W["L1_ffn_ln_g"].ap()[l], D)
        colticks += tk
        lng[(l, "b")], tk = load_bias_cols(P, W["L1_ffn_ln_b"].ap()[l], D)
        colticks += tk
        lng[(l, "b1")], tk = load_bias_cols(P, W["l1f_b1p"].ap()[l], 5632)
        colticks += tk
    b2row = {}
    for l in range(2):
        b2row[l], tk = load_bias_row(P, W["L1_ffn_b2"].ap()[l], D)
        colticks += tk
    P.barrier()

    def l2n_rot(stage_ap, nt, tick, sc, cs_sl, ss_sl, tabticks, bank=None):
        L = l2lanes[l2i[0] % 2]
        l2i[0] += 1
        nsq, nrm, rsc, rs2, bank = (L["nsq"], L["nrm"], L["rsc"], L["rs2"],
                                    L["bank"])
        ps = P.pbank(bank)
        P.wait("vector", [tick, L["mm"]])
        t1 = P.inc("vector", lambda e: e.tensor_mul(nsq[:, 0:nt], stage_ap,
                                                    stage_ap), "s_dve")
        P.wait("tensor", [t1, P.pfree[bank]] + colticks)
        t2 = P.inc("tensor", lambda e: e.matmul(ps[0:2, 0:nt], blk2T[:, 0:2],
                                                nsq[:, 0:nt], start=True,
                                                stop=True), "s_mm")
        P.wait("scalar", [t2, L["t5"]])
        t3 = P.inc("scalar", lambda e: e.activation(nrm[0:2, 0:nt],
                                                    ps[0:2, 0:nt], ACTF.Ln),
                   "s_act")
        P.wait("scalar", [t3])
        t4 = P.inc("scalar", lambda e: e.activation(nrm[0:2, 0:nt],
                                                    nrm[0:2, 0:nt], ACTF.Exp,
                                                    scale=-0.5), "s_act")
        P.wait("vector", [t4])
        t5 = P.inc("vector", lambda e: e.tensor_scalar_mul(
            nsq[0:2, 0:nt], nrm[0:2, 0:nt], 1.0), "s_dve")
        P.wait("tensor", [t5, P.pfree[bank]])
        t6 = P.inc("tensor", lambda e: e.matmul(ps[0:128, 0:nt], blk2,
                                                nsq[0:2, 0:nt], start=True,
                                                stop=True), "s_mm")
        L["mm"] = t6
        L["t5"] = t5
        P.wait("vector", [t6])
        t7 = P.inc("vector", lambda e: e.scalar_tensor_tensor(
            stage_ap, stage_ap, sc[0:128, 0:1], ps[0:128, 0:nt],
            ALU.mult, ALU.mult), "s_dve")
        P.pfree[bank] = t7
        P.wait("vector", [t7])
        tprev = t7
        for bb in range(4):
            sgn = -1.0 if bb % 2 == 0 else 1.0
            srcb = (bb + 1) if bb % 2 == 0 else (bb - 1)
            tprev = P.inc("vector", lambda e, bb=bb, srcb=srcb, sgn=sgn:
                          e.tensor_scalar_mul(
                              rsc[bb * 32 : bb * 32 + 32, 0:nt],
                              stage_ap[srcb * 32 : srcb * 32 + 32, 0:nt], sgn),
                          "s_dve")
        P.wait("vector", [tprev] + tabticks)
        ta = P.inc("vector", lambda e: e.tensor_mul(rs2[:, 0:nt], stage_ap,
                                                    cs_sl), "s_dve")
        P.wait("vector", [ta])
        tb = P.inc("vector", lambda e: e.tensor_mul(rsc[:, 0:nt], rsc[:, 0:nt],
                                                    ss_sl), "s_dve")
        P.wait("vector", [tb])
        tc = P.inc("vector", lambda e: e.tensor_add(stage_ap, rs2[:, 0:nt],
                                                    rsc[:, 0:nt]), "s_dve")
        return tc

    for l in range(2):
        ccl_out, ccl_tick = (C["cc_x3"] if l == 0 else C["cc_x4"])
        mlay = P.ab.mark()
        mflay = P.af.mark()
        halo = [P.ab.alloc(514) for _ in range(NCH)]
        hticks = {}
        hl0 = max(0, T - 514)
        hlen = T - hl0
        for c in range(NCH):
            tks = [P.dma(halo[c][:, 514 - hlen : 514],
                         ccl_out.ap()[0, c * 128 : c * 128 + 128, hl0:T],
                         queue="sync", wait=[ccl_tick])]
            if hlen < 514:
                tks.append(P.inc("vector", lambda e, c=c: e.memset(
                    halo[c][:, 0 : 514 - hlen], 0.0), "s_dve"))
            hticks[c] = tks
        shf0 = [P.ab.alloc(512) for _ in range(NCH)]
        for c in range(NCH):
            P.wait("vector", hticks[c] + [P.tick("s_mm"), P.tick("s_act")])
            tb0 = P.inc("vector", lambda e, c=c: e.tensor_scalar_mul(
                shf0[c][:, 0:1], halo[c][:, 513:514], sel[0:128, 1:2]), "s_dve")
            tb1 = P.inc("vector", lambda e, c=c: e.tensor_scalar_mul(
                shf0[c][:, 1:512], xb[c][:, 0:511], 1.0), "s_dve")
            hticks[c] += [tb0, tb1]

        def qrhs(c, t0, nt):
            if c < 4:
                return xb[c][:, t0 : t0 + nt], []
            if t0 == 0:
                return shf0[c][:, 0:nt], hticks[c]
            return xb[c][:, t0 - 1 : t0 - 1 + nt], []

        def kvrhs(c, t0, nt):
            if t0 < 512:
                assert nt <= 512
                off = (2 + t0) if c < 4 else (1 + t0)
                return halo[c][:, off : off + nt], hticks[c]
            tl = t0 - 512
            if c < 4:
                return xb[c][:, tl : tl + nt], []
            if tl == 0:
                return shf0[c][:, 0:nt], hticks[c]
            return xb[c][:, tl - 1 : tl - 1 + nt], []

        wq_ap = W["L1_ffn_wqkv"].ap()[l]
        mkq_ = P.ab.mark()
        stq = StageOC(P, T)
        stq.dram_fn = lambda gf, fh, t0, nt: C["l1f_q"].ap()[gf // 128, 0:fh,
                                                             t0 : t0 + nt]
        def q_slice(gf, fh, t0, nt, tick):
            i = stq.cur[(gf, t0)]
            csl, tk1 = rtab("l1f_qcs", t0, nt)
            ssl, tk2 = rtab("l1f_qss", t0, nt)
            tc_ = l2n_rot(stq.tiles[i][0:fh, 0:nt], nt, tick, qs_col[l],
                          csl, ssl, tk1 + tk2)
            stq.lastev[i] = tc_
            stq.dmatick[i] = P.dma(stq.dram_fn(gf, fh, t0, nt),
                                   stq.tiles[i][0:fh, 0:nt], queue="sync",
                                   wait=[tc_])
        linear_fm(P, W=wq_ap[:, 0:512], rows=D, Fout=512, rhs=qrhs,
                  tokens=tok_chunks(T), out=stq.out, evac_wait=stq.evac_wait,
                  on_slice=q_slice, fblock=256, nparity=2)
        P.barrier()
        P.ab.release(mkq_)

        mkv_ = P.ab.mark()
        stk = StageOC(P, KV)
        def kv_dst(gf, fh, t0, nt):
            h = gf // 128
            if h < 4:
                return C["l1f_k"].ap()[h, 0:fh, t0 : t0 + nt]
            return C["l1f_vT"].ap()[h - 4, 0:fh, t0 : t0 + nt]
        stk.dram_fn = kv_dst
        def kv_slice(gf, fh, t0, nt, tick):
            i = stk.cur[(gf, t0)]
            if gf < 512:
                csl, tk1 = rtab("l1f_kcs", t0, nt)
                ssl, tk2 = rtab("l1f_kss", t0, nt)
                tc_ = l2n_rot(stk.tiles[i][0:fh, 0:nt], nt, tick, ks_col[l],
                              csl, ssl, tk1 + tk2)
                stk.lastev[i] = tc_
                stk.dmatick[i] = P.dma(stk.dram_fn(gf, fh, t0, nt),
                                       stk.tiles[i][0:fh, 0:nt], queue="sync",
                                       wait=[tc_])
            else:
                stk.on_slice(gf, fh, t0, nt, tick)
        linear_fm(P, W=wq_ap[:, 512:1536], rows=D, Fout=1024, rhs=kvrhs,
                  tokens=tok_chunks(KV), out=stk.out, evac_wait=stk.evac_wait,
                  on_slice=kv_slice, fblock=256, nparity=2)
        P.barrier()
        P.ab.release(mlay)

        m_attn = P.ab.mark()
        mf_attn = P.af.mark()
        sums = P.af.alloc(2 * 512, parts=97)
        recb = P.ab.alloc(2 * 512, parts=97)
        m_heads = P.ab.mark()
        nkv = KV // 128
        OtT = P.ab.alloc(T)
        Kt = P.ab.alloc(KV)
        vst = P.ab.alloc(8 * 128)
        Vtok = P.ab.alloc(nkv * 65)
        Qt = P.ab.alloc(T)
        ptile = P.ab.alloc(6 * 512)
        P.inc("vector", lambda e: e.memset(
            Vtok[:, 0 : nkv * 65].rearrange("p (k v) -> p k v", v=65)
            [:, :, 64:65], 1.0), "s_dve")
        P.barrier()
        vst_war = None
        prev_bc = []
        tr = None
        o_dma = None
        hq_mm = None
        for pr in range(4):
            prev = [hq_mm] if hq_mm else [P.tick("s_mm"), P.tick("s_dve"),
                                          P.tick("s_act")]
            tk_k = P.dma(Kt, C["l1f_k"].ap()[pr], queue="sync", wait=prev)
            tk_q = P.dma(Qt, C["l1f_q"].ap()[pr], queue="sync", wait=prev)
            stt_pair = []
            for hh in range(2):
                off = hh * 64
                h = pr * 2 + hh
                vprev = [hq_mm] if hq_mm else [P.tick("s_mm")]
                vticks = []
                for g0 in range(0, nkv, 8):
                    gn = min(8, nkv - g0)
                    dts = [P.dma(vst[:, j * 128 : j * 128 + 128],
                                 C["l1f_vT"].ap()[pr, :, (g0 + j) * 128 :
                                                  (g0 + j) * 128 + 128],
                                 queue="gpsimd", wait=[vst_war] + vprev)
                           for j in range(gn)]
                    b = (g0 // 8) % 6
                    ps = P.pbank(b)
                    P.wait("tensor", [P.pfree[b]] + dts + P.shared["tick"])
                    def tmm(e, ps=ps, off=off, gn=gn):
                        last = None
                        for j in range(gn):
                            last = e.matmul(
                                ps[0:128, j * 64 : j * 64 + 64],
                                vst[off : off + 64, j * 128 : j * 128 + 128],
                                P.shared["ident"][off : off + 64,
                                                  off : off + 64],
                                start=True, stop=True)
                        return last
                    mm_t = P.inc("tensor", tmm, "s_mm")
                    vst_war = mm_t
                    P.wait("vector", [mm_t])
                    def ev(e, ps=ps, g0=g0, gn=gn):
                        last = None
                        for j in range(gn):
                            last = e.tensor_copy(
                                Vtok[:, (g0 + j) * 65 : (g0 + j) * 65 + 64],
                                ps[0:128, j * 64 : j * 64 + 64])
                        return last
                    ev_t = P.inc("vector", ev, "s_dve")
                    P.pfree[b] = ev_t
                    vticks.append(ev_t)
                P.wait("vector", [tr])
                ts_ms = P.inc("vector", lambda e: e.memset(sums[0:97, :], 1.0),
                              "s_dve")
                qticks = [ts_ms]
                for w in range(nw):
                    q0 = w * 512
                    chunks = []
                    for j in range(8):
                        kc = q0 + j * 128
                        vi = kc // 128
                        if j < 4:
                            nqw = j * 128 + 128
                            ch = dict(kc=kc, nk=128, vi=vi, qc0=0, nqw=nqw,
                                      pzero=(nqw, 512 - nqw) if nqw < 512
                                      else None)
                            if w == 0:
                                ch["maskpre"] = mprev0[:, j * 512 :
                                                       j * 512 + nqw]
                                ch["mq0"] = 0
                                ch["mqw"] = nqw
                            else:
                                ch["maskpre"] = mtri_p[:, 0:128]
                                ch["mq0"] = j * 128
                                ch["mqw"] = 128
                        else:
                            qc0 = (j - 4) * 128
                            nqw = 512 - qc0
                            ch = dict(kc=kc, nk=128, vi=vi, qc0=qc0, nqw=nqw,
                                      pzero=(0, qc0) if qc0 > 0 else None,
                                      maskpre=mtri_d[:, 0:128], mq0=qc0,
                                      mqw=128)
                        chunks.append(ch)
                    sr, sc = 32 * (w % 2), 512 * (w // 2)
                    if o_dma is not None:
                        P.wait("vector", [o_dma, ts_ms])
                    ts = attn_qtile(P, Kt=Kt[off : off + 64, :],
                                    Kticks=[tk_k],
                                    Qt=Qt[off : off + 64, :], Qticks=[tk_q],
                                    Vt=Vtok, Vticks=vticks + [ts_ms],
                                    dh=64, daug=64,
                                    q0=q0, nq=512, chunks=chunks, scale=8.0,
                                    octx=6, vsum="col",
                                    onescol_b=P.shared["onescol"],
                                    sums_dst=sums[sr : sr + 1, sc : sc + 512],
                                    O_dst=OtT[off : off + 64, q0 : q0 + 512],
                                    ptile=ptile)
                    qticks += ts
                hq_mm = P.tick("s_mm")
                P.wait("scalar", qticks + prev_bc)
                tl = P.inc("scalar", lambda e:
                           e.activation(sums[0:97, :], sums[0:97, :],
                                        ACTF.Ln), "s_act")
                P.wait("scalar", [tl])
                tr = P.inc("scalar", lambda e:
                           e.activation(recb[0:97, :], sums[0:97, :],
                                        ACTF.Exp, scale=-1.0), "s_act")
                prev_bc = []
                for w in range(nw):
                    sr, sc = 32 * (w % 2), 512 * (w // 2)
                    b = 6 + w % 2
                    ps = P.pbank(b)
                    P.wait("tensor", [P.pfree[b], tr] + P.shared["tick"])
                    mm_t = P.inc("tensor", lambda e, ps=ps, sr=sr, sc=sc,
                                 hh=hh:
                                 e.matmul(ps[hh * 64 : hh * 64 + 64, 0:512],
                                          P.shared["ones"][sr : sr + 1,
                                                           hh * 64 :
                                                           hh * 64 + 64],
                                          recb[sr : sr + 1, sc : sc + 512],
                                          start=True, stop=True), "s_mm")
                    prev_bc.append(mm_t)
                    P.wait("vector", [mm_t])
                    tn = P.inc("vector", lambda e, ps=ps, hh=hh, w=w:
                               e.scalar_tensor_tensor(
                                   OtT[hh * 64 : hh * 64 + 64,
                                       w * 512 : w * 512 + 512],
                                   OtT[hh * 64 : hh * 64 + 64,
                                       w * 512 : w * 512 + 512], 1.0,
                                   ps[hh * 64 : hh * 64 + 64, 0:512],
                                   ALU.mult, ALU.mult), "s_dve")
                    P.pfree[b] = tn
                    stt_pair.append(tn)
            o_dma = P.dma(C["l1f_q"].ap()[pr], OtT[:, 0:T], queue="sync",
                          wait=stt_pair)
        P.wait("sync", [o_dma])
        P.barrier()
        P.ab.release(m_heads)

        mko = P.ab.mark()
        def ctxrhs(c, t0, nt, l=l):
            def fill(slot, war, c=c, t0=t0, nt=nt):
                t = P.dma(slot[:, 0:nt],
                          C["l1f_q"].ap()[c, :, t0 : t0 + nt],
                          queue="sync", wait=war)
                return [t]
            ap, tk = C["ring"].get(("l1fctx", l, c, t0, nt), fill)
            return ap[:, 0:nt], tk
        linear_fm(P, W=W["L1_ffn_wo"].ap()[l], rows=512, Fout=D,
                  rhs=ctxrhs, tokens=tok_chunks(T),
                  out=lambda gf, t0, nt: x[gf // 128][:, t0 : t0 + nt],
                  stt_resid=lambda gf, t0, nt: x[gf // 128][:, t0 : t0 + nt],
                  fblock=256, nparity=2)
        P.barrier()
        P.ab.release(m_attn)
        P.af.release(mf_attn)

        for c in range(NCH):
            P.inc("vector", lambda e, c=c: e.tensor_scalar_mul(
                xb[c][:, 0:T], x[c][:, 0:T], 1.0), "s_dve")
        P.barrier()
        tcc = allgather(P, tiles=[xb[c][:, T - 1 : T] for c in range(NCH)],
                        ticks=[P.tick("s_dve")], ccin=C["ccc_in"],
                        ccout=C["ccc_out"])
        bnd = P.ab.alloc(16)
        tbd = P.dma(bnd[:, 0:8], C["ccc_out"].ap()[0].rearrange(
            "(c p) o -> p (c o)", p=128), queue="sync", wait=[tcc], slow=True)
        P.wait("vector", [tbd])
        tbs = P.inc("vector", lambda e: e.tensor_scalar_mul(
            bnd[:, 0:8], bnd[:, 0:8], sel[0:128, 1:2]), "s_dve")

        def h2base(c, t0, nt):
            if c < 4:
                return xb[c][:, t0 : t0 + nt], []
            if t0 == 0:
                return ffn0[c][:, 0:nt], f0t[c]
            return xb[c][:, t0 - 1 : t0 - 1 + nt], []
        ffn0 = [P.ab.alloc(512) for _ in range(NCH)]
        f0t = {}
        for c in range(NCH):
            P.wait("vector", [tbs, P.tick("s_mm"), P.tick("s_act")])
            ta_ = P.inc("vector", lambda e, c=c: e.tensor_scalar_mul(
                ffn0[c][:, 0:1], bnd[:, c : c + 1], 1.0), "s_dve")
            tb_ = P.inc("vector", lambda e, c=c: e.tensor_scalar_mul(
                ffn0[c][:, 1:512], xb[c][:, 0:511], 1.0), "s_dve")
            f0t[c] = [ta_, tb_]
        st_f = ln_stats(P, xhat=h2base, Ttot=T, sq_scratch=sq)
        h2n = norm_ring_loader(P, C["ring"], h2base, st_f, lng[(l, "g")],
                               lng[(l, "b")], colticks)
        mkw = P.ab.mark()
        apair = [P.ab.alloc(512), P.ab.alloc(512)]
        at_t = {}
        stm_dma = {}
        def w1out(gf, t0, nt):
            blk = (gf // 128) % 2
            return apair[blk][:, 0:nt]
        def w1wait(gf, t0, nt):
            return [at_t.get((gf // 128) % 2), stm_dma.get(0)]
        def w1slice(gf, fh, t0, nt, tick):
            blk = (gf // 128) % 2
            at_t[blk] = tick
            if blk == 1:
                pairi = (gf // 256)
                P.wait("vector", [at_t.get(0), tick, stm_dma.get(0)])
                tm = P.inc("vector", lambda e, nt=nt:
                           e.tensor_mul(apair[1][:, 0:nt], apair[0][:, 0:nt],
                                        apair[1][:, 0:nt]), "s_dve")
                stm_dma[0] = P.dma(
                    C["l1f_mid"].ap()[pairi * 128 : pairi * 128 + 128,
                                      t0 : t0 + nt],
                    apair[1][:, 0:nt], queue="sync", wait=[tm])
        linear_fm(P, W=W["l1f_w1p"].ap()[l], rows=D, Fout=5632, rhs=h2n,
                  tokens=tok_chunks(T), out=w1out, evac_wait=w1wait,
                  on_slice=w1slice, bias=(lng[(l, "b1")], 0), fblock=512,
                  nparity=2,
                  act=lambda gf: ACTF.Gelu if (gf // 128) % 2 == 1
                  else ACTF.Identity)
        P.barrier()
        P.ab.release(mkw)

        mkw2 = P.ab.mark()
        w2slab = [[P.ab.alloc(512) for _ in range(8)] for _ in range(2)]
        halfac = [P.af.alloc(512) for _ in range(NCH)]
        groups = [list(range(0, 8)), list(range(8, 16)), list(range(16, 22))]
        prev_w2 = {}
        ev_h = [None, None]
        pidx = 0
        for ti2, (t0, nt) in enumerate(tok_chunks(T)):
            for gi, grp in enumerate(groups):
                rsl = []
                for sidx, mc in enumerate(grp):
                    p = min(128, 2730 - mc * 128)
                    def fill(slot, war, mc=mc, t0=t0, nt=nt, p=p):
                        t = P.dma(slot[0:p, 0:nt],
                                  C["l1f_mid"].ap()[mc * 128 : mc * 128 + p,
                                                    t0 : t0 + nt],
                                  queue="sync", wait=war)
                        return [t]
                    rsl.append((C["ring"].get(("l1fmid", l, ti2, mc), fill), p))
                for half in range(2):
                    par = pidx % 2
                    pidx += 1
                    slabs = w2slab[par]
                    bks = (0, 1, 2, 3) if half == 0 else (4, 5, 6, 7)
                    sticks = []
                    for sidx, mc in enumerate(grp):
                        p = min(128, 2730 - mc * 128)
                        t = P.dma(slabs[sidx][0:p, :],
                                  W["L1_ffn_w2"].ap()[l][mc * 128 :
                                                         mc * 128 + p,
                                                         half * 512 :
                                                         half * 512 + 512],
                                  queue="gpsimd", wait=[prev_w2.get(par)])
                        sticks.append(t)
                    P.wait("tensor", [P.pfree[bk] for bk in bks] + sticks
                           + [t for ((_, tk), _) in rsl for t in tk])
                    def mmg(e, rsl=rsl, gi=gi, nt=nt, grp=grp, slabs=slabs,
                            bks=bks, half=half):
                        last = None
                        for ol in range(4):
                            oc = half * 4 + ol
                            for sidx in range(len(grp)):
                                (ap, tk), p = rsl[sidx]
                                last = e.matmul(
                                    P.pbank(bks[ol])[0:128, 0:nt],
                                    slabs[sidx][0:p,
                                                ol * 128 : ol * 128 + 128],
                                    ap[0:p, 0:nt],
                                    start=(sidx == 0),
                                    stop=(gi < 2 and sidx == len(grp) - 1))
                        if gi == 2:
                            for ol in range(4):
                                oc = half * 4 + ol
                                last = e.matmul(
                                    P.pbank(bks[ol])[0:128, 0:nt],
                                    b2row[l][0:1, oc * 128 : oc * 128 + 128],
                                    P.shared["onesrow"][0:1, 0:nt],
                                    start=False, stop=True)
                        return last
                    mm_t = P.inc("tensor", mmg, "s_mm")
                    prev_w2[par] = mm_t
                    P.wait("vector", [mm_t, ev_h[half]])
                    if gi == 0:
                        def ev0(e, nt=nt, half=half, bks=bks):
                            last = None
                            for ol in range(4):
                                last = e.tensor_copy(
                                    halfac[half * 4 + ol][:, 0:nt],
                                    P.pbank(bks[ol])[0:128, 0:nt])
                            return last
                        ev_t = P.inc("vector", ev0, "s_dve")
                    else:
                        def ev1(e, nt=nt, half=half, bks=bks):
                            last = None
                            for ol in range(4):
                                oc = half * 4 + ol
                                last = e.scalar_tensor_tensor(
                                    halfac[oc][:, 0:nt],
                                    P.pbank(bks[ol])[0:128, 0:nt],
                                    1.0, halfac[oc][:, 0:nt],
                                    ALU.mult, ALU.add)
                            return last
                        ev_t = P.inc("vector", ev1, "s_dve")
                    ev_h[half] = ev_t
                    for bk in bks:
                        P.pfree[bk] = ev_t
            P.wait("vector", ev_h)
            def ev2(e, t0=t0, nt=nt):
                last = None
                for oc in range(NCH):
                    last = e.tensor_add(x[oc][:, t0 : t0 + nt],
                                        x[oc][:, t0 : t0 + nt],
                                        halfac[oc][:, 0:nt])
                return last
            ev_t = P.inc("vector", ev2, "s_dve")
            ev_h = [ev_t, ev_t]
            P.wait("vector", [ev_t])
        P.barrier()
        P.ab.release(mkw2)
        P.af.release(mflay)

        for c in range(NCH):
            P.inc("vector", lambda e, c=c: e.tensor_scalar_mul(
                xb[c][:, 0:T], x[c][:, 0:T], 1.0), "s_dve")
        P.barrier()
        P.ab.release(mlay)
        if l == 0:
            t_cc = allgather(P, tiles=xb, ticks=[P.tick("s_dve")],
                             ccin=C["cc4_in"], ccout=C["cc4_out"])
            C["cc_x4"] = (C["cc4_out"], t_cc)

    mkt = P.ab.mark()
    mft = P.af.mark()
    x3r = Ring(P, 4, 512, arena=P.af)
    for c in range(NCH):
        for (t0, nt) in tok_chunks(T):
            def fill(slot, war, c=c, t0=t0, nt=nt):
                t = P.dma(slot[0:128, 0:nt],
                          C["l1f_x3"].ap()[c * 128 : c * 128 + 128,
                                           t0 : t0 + nt],
                          queue="sync", wait=war)
                return [t]
            ap, tk = x3r.get(("x3", c, t0), fill)
            P.wait("vector", tk + [P.tick("s_act")])
            P.inc("vector", lambda e, c=c, t0=t0, nt=nt, ap=ap:
                  e.tensor_add(x[c][:, t0 : t0 + nt], x[c][:, t0 : t0 + nt],
                               ap[0:128, 0:nt]), "s_dve")
    P.barrier()
    P.af.release(mft)
    for c in range(NCH):
        P.inc("vector", lambda e, c=c: e.tensor_scalar_mul(xb[c][:, 0:T],
                                                           x[c][:, 0:T], 1.0),
              "s_dve")
    P.barrier()
    stt = ln_stats(P, xhat=sbuf_loader(P, xb), Ttot=T, sq_scratch=sq,
                   want_f32=True)
    topln(P, xadd_f32=lambda c: (x[c], []), xhat_out=lambda c: xb[c],
          st=stt, T=T)
    P.barrier()
    P.ab.release(mkt)
    P.ab.release(m_blk)
    P.af.release(mf_blk)


# version 05 (final)
# speedup vs baseline: 1.1800x; 1.1800x over previous
import math
import numpy as np
import ml_dtypes
from contextlib import ExitStack

import concourse.bass as bass
import concourse.mybir as mybir
from concourse.bass_utils import run_bass_kernel_spmd

F32 = mybir.dt.float32
BF16 = mybir.dt.bfloat16
I32 = mybir.dt.int32
AX = mybir.AxisListType
ALU = mybir.AluOpType
ACTF = mybir.ActivationFunctionType
BFNP = ml_dtypes.bfloat16

D = 1024
NCH = 8
EPS = 1e-5
ENGN = ("tensor", "vector", "scalar", "sync", "gpsimd")
NEG = -1e9


class Cfg:
    def __init__(self, S=4096, n_cores=8, parts=("l0a", "l0f", "l1a", "l1f")):
        self.S = S
        self.T = S // 2
        self.n_cores = n_cores
        self.parts = parts
        self.scan_iters = 12
        self.groups = [[2 * i, 2 * i + 1] for i in range(n_cores // 2)]


def cdiv(a, b):
    return (a + b - 1) // b


def row2(ap):
    return ap if len(ap.shape) == 2 else ap.rearrange("(o f) -> o f", o=1)


class Arena:
    def __init__(self, prog, name, dtype, cols):
        self.prog = prog
        self.dtype = dtype
        self.cols = cols
        self.t = prog.st.enter_context(prog.nc.sbuf_tensor(name, [128, cols], dtype))
        self.off = 0
        self.align = 16 if dtype == BF16 else 8
        self.hi = 0

    def alloc(self, cols, parts=128):
        cols = int(cols)
        self.off = (self.off + self.align - 1) // self.align * self.align
        assert self.off + cols <= self.cols, (
            f"arena overflow: {self.off}+{cols} > {self.cols}")
        ap = self.t[0:parts, self.off : self.off + cols]
        self.off += cols
        self.hi = max(self.hi, self.off)
        return ap

    def mark(self):
        return self.off

    def release(self, m):
        self.off = m


class Prog:
    def __init__(self, nc, cfg):
        self.nc = nc
        self.cfg = cfg
        self.st = ExitStack()
        self.streams = {e: [] for e in ENGN}
        self.sems = {}
        self.consts = {}
        self.percore = {}
        self.dram = {}
        names = ["s_mm", "s_dve", "s_act", "s_gp", "s_bar", "s_cc"]
        self.dlanes = {}
        for q in ("sync", "gpsimd", "scalar"):
            self.dlanes[q] = [f"d_{q}{i}" for i in range(8)]
            names += self.dlanes[q]
        for s in names:
            h = self.st.enter_context(nc.semaphore(s))
            self.sems[s] = [h, 0]
        self.dl_i = {q: 0 for q in ("sync", "gpsimd", "scalar")}
        self.af = Arena(self, "af", F32, 27000)
        self.ab = Arena(self, "ab", BF16, 52400)
        self.psum = self.st.enter_context(nc.psum_tensor("psumt", [128, 4096], F32))
        self.pfree = [("s_mm", 0)] * 8
        self._uid = 0
        self.tail = {}

    def uid(self):
        self._uid += 1
        return self._uid

    def on(self, eng, fn, is_op=False):
        if is_op:
            self.tail[eng] = self.tail.get(eng, 0) + 1
        self.streams[eng].append(fn)

    def wait(self, eng, tickets):
        co = {}
        for t in tickets:
            if t is None:
                continue
            s, v = t
            if v > 0 and co.get(s, -1) < v:
                co[s] = v
        for s, v in co.items():
            h = self.sems[s][0]
            self.on(eng, lambda e, h=h, v=v: e.wait_ge(h, v))

    def inc(self, eng, fn, sem, n=1):
        self.tail[eng] = 0
        h = self.sems[sem][0]
        self.sems[sem][1] += n
        val = self.sems[sem][1]
        self.on(eng, lambda e, h=h, n=n: fn(e).then_inc(h, n))
        return (sem, val)

    def tick(self, sem):
        return (sem, self.sems[sem][1])

    def barrier(self):
        for eng in ENGN:
            assert self.tail.get(eng, 0) == 0, f"uncovered tail ops on {eng}"
        targets = [(nm, sv[0], sv[1]) for nm, sv in self.sems.items() if sv[1] > 0]
        for eng in ENGN:
            for nm, h, v in targets:
                self.on(eng, lambda e, h=h, v=v: e.wait_ge(h, v))
        self.pfree = [("s_mm", self.sems["s_mm"][1])] * 8
        return ("s_mm", self.sems["s_mm"][1])

    def dram_t(self, name, shape, dtype, kind=None):
        t = (self.nc.dram_tensor(name, shape, dtype, kind=kind) if kind
             else self.nc.dram_tensor(name, shape, dtype))
        self.dram[name] = t
        return t

    def const(self, name, arr, dtype=F32):
        if name in self.dram:
            return self.dram[name]
        arr = np.asarray(arr)
        arr = arr.astype(BFNP) if dtype == BF16 else arr.astype(np.float32)
        self.consts[name] = arr
        return self.dram_t(name, list(arr.shape), dtype, kind="ExternalInput")

    def pc_input(self, name, arrs, dtype=F32):
        if name in self.dram:
            return self.dram[name]
        out = []
        for a in arrs:
            a = np.asarray(a)
            out.append(a.astype(BFNP) if dtype == BF16 else a.astype(np.float32))
        self.percore[name] = out
        return self.dram_t(name, list(out[0].shape), dtype, kind="ExternalInput")

    def dma(self, dst, src, queue="sync", wait=(), slow=False):
        eng = queue
        lane = self.dlanes[queue][self.dl_i[queue] % 8]
        self.dl_i[queue] += 1
        self.wait(eng, list(wait) + [self.tick(lane)])
        return self.inc(eng, lambda e: e.dma_start(
            out=dst, in_=src, allow_slow_non_contiguous=slow), lane, 16)

    def pbank(self, b, nb=1):
        return self.psum[:, b * 512 : (b + nb) * 512]

    def setup_shared(self):
        c = {}
        self.const("c_ones128", np.ones((128, 128)), BF16)
        self.const("c_ident", np.eye(128), BF16)
        self.const("c_onesrow", np.ones((1, 512)), BF16)
        ones = self.ab.alloc(128)
        ident = self.ab.alloc(128)
        identf = self.af.alloc(128)
        onesrow = self.ab.alloc(512, parts=1)
        onescol = self.ab.alloc(16)
        self.const("c_onescol16", np.ones((128, 16)), BF16)
        self.const("c_identf", np.eye(128), F32)
        self.const("c_epscol", np.full((128, 1), EPS), F32)
        epscol = self.af.alloc(1)
        t0 = self.dma(ones, self.dram["c_ones128"].ap())
        t1 = self.dma(ident, self.dram["c_ident"].ap())
        t4 = self.dma(identf, self.dram["c_identf"].ap())
        t2 = self.dma(onesrow, self.dram["c_onesrow"].ap())
        t3 = self.dma(onescol, self.dram["c_onescol16"].ap())
        t5 = self.dma(epscol, self.dram["c_epscol"].ap())
        c["ones"] = ones
        c["identf"] = identf
        c["ident"] = ident
        c["onesrow"] = onesrow
        c["onescol"] = onescol
        c["eps"] = epscol
        c["tick"] = [t0, t1, t2, t3, t4, t5]
        self.shared = c

    def play(self):
        with self.nc.Block() as block:
            for eng, reg in (("tensor", block.tensor), ("vector", block.vector),
                             ("scalar", block.scalar), ("sync", block.sync),
                             ("gpsimd", block.gpsimd)):
                def runner(e, eng=eng):
                    for fn in self.streams[eng]:
                        fn(e)
                reg(runner)



def load_bias_cols(P, wap, F, name=None):
    ncol = cdiv(F, 128)
    tile = P.af.alloc(ncol)
    pad = ncol * 128 - F
    src = wap.rearrange("(c p) -> p c", p=128) if pad == 0 else None
    if src is not None:
        t = P.dma(tile, src, slow=True)
        return tile, [t]
    full = (F // 128) * 128
    ts = []
    if full:
        ts.append(P.dma(tile[:, 0 : F // 128],
                        wap[0:full].rearrange("(c p) -> p c", p=128), slow=True))
    rem = F - full
    ts.append(P.dma(tile[0:rem, F // 128 : F // 128 + 1],
                    wap[full:F].rearrange("(c p) -> p c", p=rem), slow=True))
    return tile, ts


def load_bias_row(P, wap, F):
    tile = P.ab.alloc(F, parts=1)
    t = P.dma(tile, wap.rearrange("(o f) -> o f", o=1), queue="gpsimd")
    return tile, [t]


def tok_chunks(T, n=512):
    return [(t0, min(n, T - t0)) for t0 in range(0, T, n)]


def linear_fm(P, *, W, rows, Fout, rhs, tokens, out, bias=None, act=None,
              scale=1.0, evac="scalar", fblock=1024, banks=(0, 1, 2, 3, 4, 5),
              post_oc=None, wq="gpsimd", stt_resid=None, f_dst0=0,
              evac_wait=None, bias_aug=None, nparity=1, on_ev=None,
              on_slice=None, prefetch=True):
    nin = cdiv(rows, 128)
    bi = 0
    last = None
    prev_mm = {}
    slab_sets = {}
    for fbi, fb0 in enumerate(range(0, Fout, fblock)):
        fb = min(fblock, Fout - fb0)
        par = fbi % nparity
        if par not in slab_sets:
            slab_sets[par] = [P.ab.alloc(fblock, parts=min(128, rows - c * 128))
                              for c in range(nin)]
        slabs = slab_sets[par]
        sticks = []
        for c in range(nin):
            p = min(128, rows - c * 128)
            t = P.dma(slabs[c][:, 0:fb], W[c * 128 : c * 128 + p, fb0 : fb0 + fb],
                      queue=wq, wait=[prev_mm.get(par)])
            sticks.append(t)
        for ti_, (t0, nt) in enumerate(tokens):
            sl = [rhs(c, t0, nt) for c in range(nin)]
            aps = [s[0] for s in sl]
            rticks = [t for s in sl for t in s[1]]
            for oc in range(cdiv(fb, 128)):
                fh = min(128, fb - oc * 128)
                gf = fb0 + oc * 128
                b = banks[bi % len(banks)]
                bi += 1
                ps = P.pbank(b)
                P.wait("tensor", [P.pfree[b]] + sticks + rticks)
                if bias_aug is not None:
                    P.wait("tensor", list(bias_aug[1]) + P.shared["tick"])
                def mms(e, aps=aps, slabs=slabs, oc=oc, fh=fh, nt=nt, ps=ps,
                        nin=nin, rows=rows, gf=gf):
                    last = None
                    for c in range(nin):
                        p = min(128, rows - c * 128)
                        last = e.matmul(ps[0:fh, 0:nt],
                                        slabs[c][0:p, oc * 128 : oc * 128 + fh],
                                        aps[c][0:p, 0:nt],
                                        start=(c == 0), stop=(bias_aug is None
                                                              and c == nin - 1))
                    if bias_aug is not None:
                        last = e.matmul(ps[0:fh, 0:nt],
                                        row2(bias_aug[0])[0:1, gf : gf + fh],
                                        row2(P.shared["onesrow"])[0:1, 0:nt],
                                        start=False, stop=True)
                    return last
                mm_t = P.inc("tensor", mms, "s_mm")
                prev_mm[par] = mm_t
                dst = out(f_dst0 + gf, t0, nt)
                if evac_wait is not None:
                    P.wait("vector" if (stt_resid is not None or evac == "vector")
                           else "scalar", evac_wait(f_dst0 + gf, t0, nt))
                if stt_resid is not None:
                    P.wait("vector", [mm_t])
                    ev_t = P.inc("vector", lambda e, dst=dst, ps=ps, fh=fh, nt=nt,
                                 xap=stt_resid(f_dst0 + gf, t0, nt):
                                 e.scalar_tensor_tensor(dst, ps[0:fh, 0:nt], 1.0,
                                                        xap, ALU.mult, ALU.add),
                                 "s_dve")
                elif evac == "scalar":
                    bap = 0.0
                    if bias is not None:
                        bt, b0 = bias
                        bap = bt[0:fh, b0 + gf // 128 : b0 + gf // 128 + 1]
                    fn = act(gf) if callable(act) else (
                        act if act is not None else ACTF.Identity)
                    P.wait("scalar", [mm_t])
                    ev_t = P.inc("scalar", lambda e, dst=dst, ps=ps, fh=fh, nt=nt,
                                 fn=fn, bap=bap, scale=scale:
                                 e.activation(dst, ps[0:fh, 0:nt], fn, bias=bap,
                                              scale=scale), "s_act")
                else:
                    assert act is None and bias is None
                    P.wait("vector", [mm_t])
                    ev_t = P.inc("vector", lambda e, dst=dst, ps=ps, fh=fh, nt=nt:
                                 e.tensor_copy(dst, ps[0:fh, 0:nt]), "s_dve")
                P.pfree[b] = ev_t
                last = ev_t
                if on_ev is not None:
                    on_ev(f_dst0 + gf, ev_t)
                if on_slice is not None:
                    on_slice(f_dst0 + gf, fh, t0, nt, ev_t)
                if post_oc is not None and (t0, nt) == tokens[-1]:
                    post_oc(f_dst0 + gf, fh, ev_t)
            if prefetch:
                if ti_ + 1 < len(tokens):
                    pt0, pnt = tokens[ti_ + 1]
                    for c in range(nin):
                        rhs(c, pt0, pnt)
                elif fb0 + fblock < Fout:
                    pt0, pnt = tokens[0]
                    for c in range(nin):
                        rhs(c, pt0, pnt)
    return last


def linear_tm(P, *, W, rows, Fout, rhs, tslices, out, bias_row=None, act=None,
              scale=1.0, banks=(0, 1, 2, 3, 4, 5), fcb=512, on_out=None,
              evac_wait=None):
    nin = cdiv(rows, 128)
    mark = P.ab.mark()
    slabs = []
    sticks = []
    for c in range(nin):
        p = min(128, rows - c * 128)
        s = P.ab.alloc(Fout, parts=p)
        slabs.append(s)
        sticks.append(P.dma(s, W[c * 128 : c * 128 + p, :], queue="gpsimd"))
    onesrow = P.shared["onesrow"]
    bi = 0
    last = None
    for ti, (t0, nt) in enumerate(tslices):
        sl = [rhs(c, t0, nt) for c in range(nin)]
        aps = [s[0] for s in sl]
        rticks = [t for s in sl for t in s[1]]
        for f0 in range(0, Fout, fcb):
            nf = min(fcb, Fout - f0)
            b = banks[bi % len(banks)]
            bi += 1
            ps = P.pbank(b)
            P.wait("tensor", [P.pfree[b]] + sticks + rticks
                   + (bias_row[1] if bias_row else []))
            def mms(e, aps=aps, nt=nt, f0=f0, nf=nf, ps=ps, nin=nin,
                    rows=rows, slabs=slabs, bias_row=bias_row, onesrow=onesrow):
                last = None
                for c in range(nin):
                    p = min(128, rows - c * 128)
                    last = e.matmul(ps[0:nt, 0:nf], aps[c][0:p, 0:nt],
                                    slabs[c][0:p, f0 : f0 + nf],
                                    start=(c == 0),
                                    stop=(bias_row is None and c == nin - 1))
                if bias_row is not None:
                    last = e.matmul(ps[0:nt, 0:nf], row2(onesrow)[0:1, 0:nt],
                                    row2(bias_row[0])[0:1, f0 : f0 + nf],
                                    start=False, stop=True)
                return last
            mm_t = P.inc("tensor", mms, "s_mm")
            dst = out(ti, f0, nf)
            P.wait("scalar", [mm_t] + (evac_wait(ti) if evac_wait else []))
            fn = act if act is not None else ACTF.Identity
            ev_t = P.inc("scalar", lambda e, dst=dst, ps=ps, nt=nt, nf=nf, fn=fn,
                         scale=scale: e.activation(dst, ps[0:nt, 0:nf], fn,
                                                   scale=scale), "s_act")
            P.pfree[b] = ev_t
            last = ev_t
            if on_out is not None:
                on_out(ti, f0, nf, ev_t)
    return last, mark


def bcast_rows(P, src_row_ap, nq_list, dst, *, row_ticks, banks=(6, 7),
               lhs_ones=None, nrows=128, evac="scalar", dtype_note=None):
    ones = lhs_ones if lhs_ones is not None else P.shared["ones"]
    bi = 0
    out_t = []
    for (t0, nt) in nq_list:
        b = banks[bi % len(banks)]
        bi += 1
        ps = P.pbank(b)
        P.wait("tensor", [P.pfree[b]] + row_ticks)
        mm_t = P.inc("tensor", lambda e, t0=t0, nt=nt, ps=ps, ones=ones, nrows=nrows,
                     src=src_row_ap: e.matmul(ps[0:nrows, 0:nt], ones[0:1, 0:nrows],
                                              src[0:1, t0 : t0 + nt],
                                              start=True, stop=True), "s_mm")
        eng = evac
        sem = {"scalar": "s_act", "vector": "s_dve"}[eng]
        P.wait(eng, [mm_t])
        if eng == "scalar":
            t = P.inc(eng, lambda e, ps=ps, t0=t0, nt=nt, nrows=nrows, dst=dst:
                      e.activation(dst[0:nrows, t0 : t0 + nt], ps[0:nrows, 0:nt],
                                   ACTF.Copy), "s_act")
        else:
            t = P.inc(eng, lambda e, ps=ps, t0=t0, nt=nt, nrows=nrows, dst=dst:
                      e.tensor_copy(dst[0:nrows, t0 : t0 + nt], ps[0:nrows, 0:nt]),
                      "s_dve")
        P.pfree[b] = t
        out_t.append(t)
    return out_t



def ln_stats(P, *, xhat, Ttot, sq_scratch, want_f32=False, banks=(6, 7),
             arena=None):
    ones = P.shared["ones"]
    ar = arena if arena is not None else P.ab
    mbc = ar.alloc(Ttot)
    rbc = ar.alloc(Ttot)
    mf = P.af.alloc(Ttot) if want_f32 else None
    rf = P.af.alloc(Ttot) if want_f32 else None
    mkst = P.af.mark()
    var = P.af.alloc(Ttot)
    ticks = []
    c1 = 1.0 / D
    sq_mm = [None]
    for (t0, nt) in tok_chunks(Ttot):
        bs, bq = banks
        ps_s, ps_q = P.pbank(bs), P.pbank(bq)
        P.wait("tensor", [P.pfree[bs], P.pfree[bq]])
        sqt = []
        aps = []
        for c in range(NCH):
            ap, tk = xhat(c, t0, nt)
            aps.append(ap)
            sq = sq_scratch[c]
            P.wait("vector", tk + ([sq_mm[0]] if sq_mm[0] else []))
            t_sq = P.inc("vector", lambda e, sq=sq, ap=ap, nt=nt:
                         e.tensor_mul(sq[:, 0:nt], ap[:, 0:nt], ap[:, 0:nt]),
                         "s_dve")
            sqt.append(t_sq)
        P.wait("tensor", sqt)
        def mmall(e, aps=aps, nt=nt, ps_s=ps_s, ps_q=ps_q, ones=ones,
                  sq_scratch=sq_scratch):
            for c in range(NCH):
                e.matmul(ps_s[0:128, 0:nt], ones[:, 0:128], aps[c][:, 0:nt],
                         start=(c == 0), stop=(c == NCH - 1))
            last = None
            for c in range(NCH):
                last = e.matmul(ps_q[0:128, 0:nt], ones[:, 0:128],
                                sq_scratch[c][:, 0:nt],
                                start=(c == 0), stop=(c == NCH - 1))
            return last
        mm_t = P.inc("tensor", mmall, "s_mm")
        sq_mm[0] = mm_t
        P.wait("scalar", [mm_t])
        t_mb = P.inc("scalar", lambda e, t0=t0, nt=nt, ps_s=ps_s, mbc=mbc, c1=c1:
                     e.activation(mbc[:, t0 : t0 + nt], ps_s[0:128, 0:nt],
                                  ACTF.Copy, scale=c1), "s_act")
        t_mf = None
        if want_f32:
            t_mf = P.inc("scalar", lambda e, t0=t0, nt=nt, ps_s=ps_s, mf=mf, c1=c1:
                         e.activation(mf[:, t0 : t0 + nt], ps_s[0:128, 0:nt],
                                      ACTF.Copy, scale=c1), "s_act")
        P.wait("vector", [mm_t, t_mb])
        t_v1 = P.inc("vector", lambda e, t0=t0, nt=nt, mbc=mbc, var=var:
                     e.scalar_tensor_tensor(var[:, t0 : t0 + nt],
                                            mbc[:, t0 : t0 + nt], -1.0,
                                            mbc[:, t0 : t0 + nt],
                                            ALU.mult, ALU.mult), "s_dve")
        P.wait("vector", [t_v1])
        t_v = P.inc("vector", lambda e, t0=t0, nt=nt, ps_q=ps_q, var=var, c1=c1:
                    e.scalar_tensor_tensor(var[:, t0 : t0 + nt],
                                           ps_q[0:128, 0:nt], c1,
                                           var[:, t0 : t0 + nt],
                                           ALU.mult, ALU.add), "s_dve")
        P.pfree[bs] = t_mf if want_f32 else t_mb
        P.pfree[bq] = t_v
        P.wait("scalar", [t_v] + P.shared["tick"])
        t_ln = P.inc("scalar", lambda e, t0=t0, nt=nt, var=var:
                     e.activation(var[:, t0 : t0 + nt], var[:, t0 : t0 + nt],
                                  ACTF.Ln,
                                  bias=P.shared["eps"][0:128, 0:1]), "s_act")
        P.wait("scalar", [t_ln])
        if want_f32:
            t_r = P.inc("scalar", lambda e, t0=t0, nt=nt, var=var, rf=rf:
                        e.activation(rf[:, t0 : t0 + nt], var[:, t0 : t0 + nt],
                                     ACTF.Exp, scale=-0.5), "s_act")
            t_rb = P.inc("scalar", lambda e, t0=t0, nt=nt, var=var, rbc=rbc:
                         e.activation(rbc[:, t0 : t0 + nt], var[:, t0 : t0 + nt],
                                      ACTF.Exp, scale=-0.5), "s_act")
            ticks += [t_mb, t_mf, t_r, t_rb]
        else:
            t_rb = P.inc("scalar", lambda e, t0=t0, nt=nt, var=var, rbc=rbc:
                         e.activation(rbc[:, t0 : t0 + nt], var[:, t0 : t0 + nt],
                                      ACTF.Exp, scale=-0.5), "s_act")
            ticks += [t_mb, t_rb]
    P.af.release(mkst)
    return dict(mbc=mbc, rbc=rbc, mf=mf, rf=rf, ticks=ticks)


def ln_apply_bf(P, *, xhat, st, Ttot, out, g_col=None, b_col=None, gb_ticks=()):
    ticks = []
    for (t0, nt) in tok_chunks(Ttot):
        for c in range(NCH):
            ap, tk = xhat(c, t0, nt)
            dst = out(c, t0, nt)
            P.wait("vector", tk + st["ticks"] + list(gb_ticks))
            ta_ = P.inc("vector", lambda e, ap=ap, dst=dst, st=st, t0=t0, nt=nt:
                        e.tensor_sub(dst, ap[:, 0:nt], st["mbc"][:, t0 : t0 + nt]),
                        "s_dve")
            P.wait("vector", [ta_])
            if g_col is not None:
                tb_ = P.inc("vector", lambda e, dst=dst, st=st, g=g_col, c=c, t0=t0,
                            nt=nt: e.scalar_tensor_tensor(dst, dst,
                                                          g[0:128, c : c + 1],
                                                          st["rbc"][:, t0 : t0 + nt],
                                                          ALU.mult, ALU.mult),
                            "s_dve")
                P.wait("vector", [tb_])
                t3 = P.inc("vector", lambda e, dst=dst, b=b_col, c=c:
                           e.tensor_scalar_add(dst, dst, b[0:128, c : c + 1]),
                           "s_dve")
                ticks.append(t3)
            else:
                t2 = P.inc("vector", lambda e, dst=dst, st=st, t0=t0, nt=nt:
                           e.tensor_mul(dst, dst, st["rbc"][:, t0 : t0 + nt]),
                           "s_dve")
                ticks.append(t2)
    return ticks


def topln(P, *, xadd_f32, xhat_out, st, T):
    ticks = []
    for c in range(NCH):
        ap, tk = xadd_f32(c)
        bf = xhat_out(c)
        P.wait("vector", tk + st["ticks"])
        t1_ = P.inc("vector", lambda e, ap=ap, st=st, T=T:
                    e.tensor_sub(ap[:, 0:T], ap[:, 0:T], st["mf"][:, 0:T]),
                    "s_dve")
        P.wait("vector", [t1_])
        t2 = P.inc("vector", lambda e, ap=ap, st=st, T=T:
                   e.tensor_mul(ap[:, 0:T], ap[:, 0:T], st["rf"][:, 0:T]), "s_dve")
        P.wait("vector", [t2])
        t3 = P.inc("vector", lambda e, ap=ap, bf=bf, T=T:
                   e.tensor_scalar_mul(bf[:, 0:T], ap[:, 0:T], 1.0), "s_dve")
        ticks += [t2, t3]
    return ticks



def attn_qtile(P, *, Kt, Kticks, Qt, Qticks, Vt, Vticks, dh, daug, q0, nq,
               chunks, scale, octx, vsum, onescol_b, sums_dst, O_dst, ptile):
    nch = len(chunks)
    GA, GB = (0, 1, 2), (3, 4, 5)
    ngroups = cdiv(nch, 3)
    ctx_started = [False]
    ctx_t = None
    g_exps = {}

    def do_smm_exp(gi):
        grp = chunks[gi * 3 : gi * 3 + 3]
        banks = GA if gi % 2 == 0 else GB
        waits = [P.pfree[b] for b in banks[: len(grp)]] + Kticks + Qticks
        P.wait("tensor", waits)
        def smms(e, grp=grp, banks=banks, Kt=Kt, Qt=Qt, daug=daug, q0=q0):
            last = None
            for j, ch in enumerate(grp):
                ps = P.pbank(banks[j])
                last = e.matmul(ps[0 : ch["nk"],
                                   ch["qc0"] : ch["qc0"] + ch["nqw"]],
                                Kt[0:daug, ch["kc"] : ch["kc"] + ch["nk"]],
                                Qt[0:daug,
                                   q0 + ch["qc0"] : q0 + ch["qc0"] + ch["nqw"]],
                                start=True, stop=True)
            return last
        mm_t = P.inc("tensor", smms, "s_mm")
        mask_t = []
        for j, ch in enumerate(grp):
            if ch.get("maskpre") is not None:
                b = banks[j]
                P.wait("vector", [mm_t])
                mt = P.inc("vector", lambda e, b=b, ch=ch:
                           e.tensor_add(
                               P.pbank(b)[0 : ch["nk"],
                                          ch["mq0"] : ch["mq0"] + ch["mqw"]],
                               P.pbank(b)[0 : ch["nk"],
                                          ch["mq0"] : ch["mq0"] + ch["mqw"]],
                               ch["maskpre"]), "s_dve")
                mask_t.append(mt)
        g_exp = []
        for j, ch in enumerate(grp):
            b = banks[j]
            slot = (gi % 2) * 3 + j
            pt = ptile[:, slot * 512 : slot * 512 + 512]
            if ch.get("pzero"):
                z0, zw = ch["pzero"]
                P.wait("vector", [mm_t])
                zt = P.inc("vector", lambda e, pt=pt, z0=z0, zw=zw, ch=ch:
                           e.memset(pt[0 : ch["nk"], z0 : z0 + zw], 0.0),
                           "s_dve")
                g_exp.append(zt)
            bias = ch.get("biascol")
            P.wait("scalar", [mm_t] + mask_t)
            et = P.inc("scalar", lambda e, pt=pt, b=b, ch=ch, bias=bias,
                       scale=scale:
                       e.activation(pt[0 : ch["nk"],
                                       ch["qc0"] : ch["qc0"] + ch["nqw"]],
                                    P.pbank(b)[0 : ch["nk"],
                                               ch["qc0"] : ch["qc0"]
                                               + ch["nqw"]],
                                    ACTF.Exp,
                                    bias=(bias if bias is not None else 0.0),
                                    scale=scale), "s_act")
            g_exp.append(et)
            P.pfree[b] = et
        g_exps[gi] = g_exp

    def do_ctx(gi):
        nonlocal ctx_t
        grp = chunks[gi * 3 : gi * 3 + 3]
        if gi == 0:
            P.wait("tensor", [P.pfree[octx]] + ([P.pfree[7]] if vsum == "mm"
                                                else []))
        P.wait("tensor", g_exps[gi] + Vticks)
        fin = gi == ngroups - 1
        def cmms(e, grp=grp, gi=gi, Vt=Vt, dh=dh, nq=nq, octx=octx, vsum=vsum,
                 ptile=ptile, ctx_started=ctx_started, onescol_b=onescol_b,
                 fin=fin):
            last = None
            vc = dh + (1 if vsum == "col" else 0)
            for j, ch in enumerate(grp):
                lastch = fin and j == len(grp) - 1
                slot = (gi % 2) * 3 + j
                pt = ptile[:, slot * 512 : slot * 512 + 512]
                st = not ctx_started[0]
                last = e.matmul(P.pbank(octx)[0:vc, 0:nq],
                                Vt[0 : ch["nk"],
                                   ch["vi"] * vc : ch["vi"] * vc + vc],
                                pt[0 : ch["nk"], 0:nq], start=st, stop=lastch)
                if vsum == "mm":
                    last = e.matmul(P.pbank(7)[0:1, 0:nq],
                                    onescol_b[0 : ch["nk"], 0:1],
                                    pt[0 : ch["nk"], 0:nq],
                                    start=st, stop=lastch)
                ctx_started[0] = True
            return last
        if fin:
            ctx_t = P.inc("tensor", cmms, "s_mm")
        else:
            P.on("tensor", lambda e, f=cmms: f(e), is_op=True)

    do_smm_exp(0)
    for gi in range(ngroups):
        if gi + 1 < ngroups:
            do_smm_exp(gi + 1)
        do_ctx(gi)
    P.wait("vector", [ctx_t])
    t_o = P.inc("vector", lambda e: e.tensor_copy(O_dst, P.pbank(octx)[0:dh, 0:nq]),
                "s_dve")
    if vsum == "col":
        assert dh == 64
        t_s = P.inc("vector", lambda e:
                    e.tensor_copy(sums_dst, P.pbank(octx)[64:65, 0:nq]), "s_dve")
    else:
        t_s = P.inc("vector", lambda e:
                    e.tensor_copy(sums_dst, P.pbank(7)[0:1, 0:nq]), "s_dve")
        P.pfree[7] = t_s
    P.pfree[octx] = t_s
    return [t_o, t_s]



class Ring:

    def __init__(self, P, nslots, cols, parts=128, arena=None):
        self.P = P
        ar = arena if arena is not None else P.ab
        self.slots = [ar.alloc(cols, parts=parts) for _ in range(nslots)]
        self.n = nslots
        self.i = 0
        self.cache = {}

    def get(self, key, fill):
        if key in self.cache:
            return self.cache[key]
        P = self.P
        slot = self.slots[self.i % self.n]
        self.i += 1
        self.cache = {k: v for k, v in self.cache.items() if v[0] is not slot}
        war = [P.tick("s_mm"), P.tick("s_dve")]
        ticks = fill(slot, war)
        self.cache[key] = (slot, ticks)
        return self.cache[key]


_RING_UID = [0]


def _ring_uid():
    _RING_UID[0] += 1
    return _RING_UID[0]


def xg_ring_loader(P, ring, src_dram, Wcols, dtype_cast=True):
    uid = _ring_uid()
    def xg(c, t0, nt):
        def fill(slot, war):
            t = P.dma(slot[:, 0:nt], src_dram.ap()[c * 128 : c * 128 + 128,
                                                   t0 : t0 + nt],
                      queue="gpsimd", wait=war)
            return [t]
        ap, ticks = ring.get(("xg", uid, c, t0, nt), fill)
        return ap[:, 0:nt], ticks
    return xg


def norm_ring_loader(P, ring, base, st, g_col=None, b_col=None, gticks=()):
    uid = _ring_uid()
    def xn(c, t0, nt):
        def fill(slot, war):
            ap, tk = base(c, t0, nt)
            P.wait("vector", war + tk + st["ticks"] + list(gticks))
            tx_ = P.inc("vector", lambda e, slot=slot, ap=ap, t0=t0, nt=nt:
                        e.tensor_sub(slot[:, 0:nt], ap[:, 0:nt],
                                     st["mbc"][:, t0 : t0 + nt]), "s_dve")
            P.wait("vector", [tx_])
            if g_col is not None:
                ty_ = P.inc("vector", lambda e, slot=slot, t0=t0, nt=nt, c=c:
                            e.scalar_tensor_tensor(slot[:, 0:nt], slot[:, 0:nt],
                                                   g_col[0:128, c : c + 1],
                                                   st["rbc"][:, t0 : t0 + nt],
                                                   ALU.mult, ALU.mult), "s_dve")
                P.wait("vector", [ty_])
                t = P.inc("vector", lambda e, slot=slot, nt=nt, c=c:
                          e.tensor_scalar_add(slot[:, 0:nt], slot[:, 0:nt],
                                              b_col[0:128, c : c + 1]), "s_dve")
            else:
                t = P.inc("vector", lambda e, slot=slot, t0=t0, nt=nt:
                          e.tensor_mul(slot[:, 0:nt], slot[:, 0:nt],
                                       st["rbc"][:, t0 : t0 + nt]), "s_dve")
            return [t]
        ap, ticks = ring.get(("xn", uid, c, t0, nt), fill)
        return ap[:, 0:nt], ticks
    return xn


def sbuf_loader(P, tiles, ticks_map=None):
    def f(c, t0, nt):
        return tiles[c][:, t0 : t0 + nt], (ticks_map or {}).get(c, [])
    return f



def v_transpose(P, *, vT, vticks, Sk, dh, vtok, vsum, banks=(0, 1, 2, 3, 4, 5),
                ident_off=0):
    ident = P.shared["ident"]
    nkc = cdiv(Sk, 128)
    vc = dh + (1 if vsum == "col" else 0)
    per = 512 // dh
    bi = 0
    ticks = []
    for g0 in range(0, nkc, per):
        gn = min(per, nkc - g0)
        b = banks[bi % len(banks)]
        bi += 1
        ps = P.pbank(b)
        P.wait("tensor", [P.pfree[b]] + vticks + P.shared["tick"])
        def tmm(e, g0=g0, gn=gn, ps=ps, vT=vT, dh=dh, Sk=Sk, ident=ident,
                io=ident_off):
            last = None
            for j in range(gn):
                k0 = (g0 + j) * 128
                nk = min(128, Sk - k0)
                last = e.matmul(ps[0:nk, j * dh : j * dh + dh],
                                vT[0:dh, k0 : k0 + nk],
                                ident[io : io + dh, io : io + dh],
                                start=True, stop=True)
            return last
        mm_t = P.inc("tensor", tmm, "s_mm")
        P.wait("vector", [mm_t])
        def ev(e, g0=g0, gn=gn, ps=ps, vtok=vtok, dh=dh, vc=vc, Sk=Sk):
            last = None
            for j in range(gn):
                k0 = (g0 + j) * 128
                nk = min(128, Sk - k0)
                last = e.tensor_copy(
                    vtok[0:nk, (g0 + j) * vc : (g0 + j) * vc + dh],
                    ps[0:nk, j * dh : j * dh + dh])
            return last
        ev_t = P.inc("vector", ev, "s_dve")
        P.pfree[b] = ev_t
        ticks.append(ev_t)
    if vsum == "col":
        t = P.inc("vector", lambda e, vtok=vtok, vc=vc, nkc=nkc:
                  e.memset(vtok[:, 0 : nkc * vc].rearrange(
                      "p (k v) -> p k v", v=vc)[:, :, dh : dh + 1], 1.0), "s_dve")
        ticks.append(t)
    return ticks


class StageOC:

    def __init__(self, P, Ttot, n=4, queue="sync", parts=128, ncols=512):
        self.P = P
        self.tiles = [P.ab.alloc(ncols, parts=parts) for _ in range(n)]
        self.n = n
        self.i = 0
        self.cur = {}
        self.dmatick = [None] * n
        self.lastev = [None] * n
        self.queue = queue
        self.dram_fn = None

    def out(self, gf, t0, nt):
        i = self.i % self.n
        self.i += 1
        self.cur[(gf, t0)] = i
        return self.tiles[i][:, 0:nt]

    def evac_wait(self, gf, t0, nt):
        i = self.cur[(gf, t0)]
        return [self.dmatick[i], self.lastev[i]]

    def note_ev(self, gf, tick):
        pass

    def on_slice(self, gf, fh, t0, nt, tick):
        i = self.cur[(gf, t0)]
        self.lastev[i] = tick
        self.dmatick[i] = self.P.dma(self.dram_fn(gf, fh, t0, nt),
                                     self.tiles[i][0:fh, 0:nt],
                                     queue=self.queue, wait=[tick])



def build_l0a(P, C):
    cfg = P.cfg
    S, T = cfg.S, cfg.T
    W = C["w"]
    x = C["x"]
    xb = C["xb"]
    sq = C["sq"]
    own = sbuf_loader(P, xb)
    m_blk = P.ab.mark()
    mf_blk = P.af.mark()
    sel = C["sel"]

    cols = {}
    colticks = []
    for nm, F in [("L0_attn_ln_g", D), ("L0_attn_ln_b", D), ("L0_attn_ln1_g", D),
                  ("L0_attn_ln1_b", D), ("L0_attn_ln2_g", D), ("L0_attn_ln2_b", D),
                  ("L0_attn_lnf_g", D), ("L0_attn_lnf_b", D),
                  ("L0_attn_bgate", D), ("L0_attn_bq", D), ("L0_attn_b1", 4096),
                  ("L0_attn_bf", D)]:
        cols[nm], tk = load_bias_cols(P, W[nm].ap(), F)
        colticks += tk
    cols["l0_bdi"], tk = load_bias_cols(P, W["l0_bdi"].ap(), 64)
    colticks += tk
    cols["l0_bkv"], tk = load_bias_cols(P, W["l0_bkv"].ap(), 2 * D)
    colticks += tk
    rows = {}
    for nm in ("L0_attn_bo", "L0_attn_b2"):
        rows[nm], tk = load_bias_row(P, W[nm].ap(), D)
        colticks += tk
    P.barrier()
    m_work = P.ab.mark()
    mf_work = P.af.mark()
    C["scanf"] = P.af.alloc(S)
    C["scanb"] = P.ab.alloc(S + 8)

    xg = xg_ring_loader(P, C["ring"], C["x_bT"], S)
    mst = P.ab.mark()
    st_g_ln = ln_stats(P, xhat=xg, Ttot=S, sq_scratch=sq)
    P.barrier()

    scanf = C["scanf"]
    scanb = C["scanb"]
    xn_g = norm_ring_loader(P, C["ring"], xg, st_g_ln, cols["L0_attn_ln_g"],
                            cols["L0_attn_ln_b"], colticks)
    mrk = P.ab.mark()
    slabs_di = P.ab.alloc(8 * 64)
    tdis = [P.dma(slabs_di[:, c * 64 : c * 64 + 64],
                  W["l0_wdi"].ap()[c * 128 : c * 128 + 128, :], queue="gpsimd")
            for c in range(NCH)]
    tdi = tdis[-1]
    for ti, (t0, nt) in enumerate(tok_chunks(S)):
        b = (0, 1)[ti % 2]
        ps = P.pbank(b)
        for c in range(NCH):
            xg(c, t0, nt)
        sl = [xn_g(c, t0, nt) for c in range(NCH)]
        P.wait("tensor", [P.pfree[b]] + tdis + [t for s in sl for t in s[1]])
        def mms(e, sl=sl, ps=ps, nt=nt, slabs_di=slabs_di):
            last = None
            for c in range(NCH):
                last = e.matmul(ps[0:64, 0:nt],
                                slabs_di[:, c * 64 : c * 64 + 64],
                                sl[c][0][:, 0:nt], start=(c == 0),
                                stop=(c == NCH - 1))
            return last
        mm_t = P.inc("tensor", mms, "s_mm")
        P.wait("scalar", [mm_t])
        t1 = P.inc("scalar", lambda e, ps=ps, t0=t0, nt=nt:
                   e.activation(scanf[0:16, t0 : t0 + nt], ps[0:16, 0:nt],
                                ACTF.Sigmoid,
                                bias=cols["l0_bdi"][0:16, 0:1]), "s_act")
        t2 = P.inc("scalar", lambda e, ps=ps, t0=t0, nt=nt:
                   e.activation(scanf[32:48, t0 : t0 + nt], ps[32:48, 0:nt],
                                ACTF.Identity,
                                bias=cols["l0_bdi"][32:48, 0:1]), "s_act")
        P.pfree[b] = t2
    P.barrier()
    P.ab.release(mst)

    mst = P.ab.mark()
    st_o_ln = ln_stats(P, xhat=own, Ttot=T, sq_scratch=sq)
    xn_o = norm_ring_loader(P, C["ring"], own, st_o_ln, cols["L0_attn_ln_g"],
                            cols["L0_attn_ln_b"], colticks)
    stg = StageOC(P, T)
    stg.dram_fn = lambda gf, fh, t0, nt: C["scr_gate"].ap()[gf : gf + fh,
                                                            t0 : t0 + nt]
    linear_fm(P, W=W["L0_attn_Wgate"].ap(), rows=D, Fout=D, rhs=xn_o,
              tokens=tok_chunks(T), out=stg.out, evac_wait=stg.evac_wait,
              on_slice=stg.on_slice,
              bias=(cols["L0_attn_bgate"], 0), act=ACTF.Sigmoid, fblock=512,
              nparity=2)
    P.barrier()
    P.ab.release(mst)

    abuf = P.ab.alloc(16)[32:48, 0:16]
    ta = P.dma(abuf, W["L0_attn_A"].ap(), queue="gpsimd")
    ti = P.inc("vector", lambda e: e.tensor_scalar_mul(scanf[64:80, 0:S],
                                                       scanf[32:48, 0:S], 1.0),
               "s_dve")
    P.inc("vector", lambda e: e.memset(scanb[32:48, 0:1], 0.0), "s_dve")
    P.wait("vector", [ta, ti])
    for it in range(cfg.scan_iters):
        lastit = it == cfg.scan_iters - 1
        P.wait("scalar", [P.tick("s_dve"), P.tick("s_mm")])
        th = P.inc("scalar", lambda e: e.activation(scanb[0:16, 0:S],
                                                    scanf[64:80, 0:S],
                                                    ACTF.Tanh), "s_act")
        if lastit:
            break
        P.wait("vector", [th])
        tt = P.inc("vector", lambda e: e.tensor_mul(scanb[32:48, 1 : S + 1],
                                                    scanb[0:16, 0:S],
                                                    scanf[0:16, 0:S]), "s_dve")
        for ci, (t0, nt) in enumerate(tok_chunks(S)):
            b = 6 + (ci % 2)
            ps = P.pbank(b)
            P.wait("tensor", [P.pfree[b], tt, ta])
            mm_t = P.inc("tensor", lambda e, ps=ps, t0=t0, nt=nt:
                         e.matmul(ps[32:48, 0:nt], abuf,
                                  scanb[32:48, t0 : t0 + nt],
                                  start=True, stop=True), "s_mm")
            P.wait("vector", [mm_t])
            tz = P.inc("vector", lambda e, ps=ps, t0=t0, nt=nt:
                       e.scalar_tensor_tensor(scanf[64:80, t0 : t0 + nt],
                                              ps[32:48, 0:nt], 1.0,
                                              scanf[32:48, t0 : t0 + nt],
                                              ALU.mult, ALU.add), "s_dve")
            P.pfree[b] = tz
    P.barrier()

    mrk = P.ab.mark()
    hcst = [P.ab.alloc(S), P.ab.alloc(S)]
    gslot = [P.ab.alloc(T), P.ab.alloc(T)]
    hc_done = {}
    def hc_out(gf, t0, nt):
        return hcst[(gf // 128) % 2][:, t0 : t0 + nt]
    def hc_wait(gf, t0, nt):
        return [hc_done.get((gf // 128) % 2)]
    def hc_post(gf, fh, tick):
        oc = gf // 128
        i = oc % 2
        hl = hcst[i]
        gs = gslot[i]
        tg = P.dma(gs, C["scr_gate"].ap()[gf : gf + fh, :], queue="sync",
                   wait=[hc_done.get(i)])
        P.wait("vector", [tick, tg])
        h1 = P.inc("vector", lambda e, hl=hl, sel=sel:
                   e.tensor_scalar_mul(hl[:, 0:T], hl[:, 0:T], sel[0:128, 0:1]),
                   "s_dve")
        P.wait("vector", [h1])
        h2 = P.inc("vector", lambda e, hl=hl, sel=sel:
                   e.scalar_tensor_tensor(hl[:, 0:T], hl[:, T : 2 * T],
                                          sel[0:128, 1:2], hl[:, 0:T],
                                          ALU.mult, ALU.add), "s_dve")
        P.wait("vector", [h2])
        h3 = P.inc("vector", lambda e, hl=hl, gs=gs:
                   e.tensor_mul(hl[:, 0:T], hl[:, 0:T], gs[:, 0:T]), "s_dve")
        P.wait("vector", [h3])
        tm = P.inc("vector", lambda e, hl=hl, oc=oc:
                   e.tensor_add(hl[:, 0:T], hl[:, 0:T], xb[oc][:, 0:T]),
                   "s_dve")
        hc_done[i] = P.dma(C["scr_mamba"].ap()[gf : gf + fh, :],
                           hl[0:fh, 0:T], queue="sync", wait=[tm])
    linear_fm(P, W=W["L0_attn_C"].ap(), rows=16, Fout=D,
              rhs=lambda c, t0, nt: (C["scanb"][0:16, t0 : t0 + nt], []),
              tokens=tok_chunks(S), out=hc_out, evac_wait=hc_wait,
              post_oc=hc_post, fblock=256)
    P.barrier()
    P.ab.release(m_work)
    P.af.release(mf_work)

    mst = P.ab.mark()
    st_o_ln1 = ln_stats(P, xhat=own, Ttot=T, sq_scratch=sq)
    xn1_o = norm_ring_loader(P, C["ring"], own, st_o_ln1,
                             cols["L0_attn_ln1_g"], cols["L0_attn_ln1_b"],
                             colticks)
    stq = StageOC(P, T)
    stq.dram_fn = lambda gf, fh, t0, nt: C["scr_q"].ap()[gf // 128, 0:fh,
                                                         t0 : t0 + nt]
    linear_fm(P, W=W["L0_attn_Wq"].ap(), rows=D, Fout=D,
              rhs=xn1_o, tokens=tok_chunks(T),
              out=stq.out, evac_wait=stq.evac_wait, on_slice=stq.on_slice,
              bias=(cols["L0_attn_bq"], 0), fblock=512, nparity=2)
    P.barrier()
    P.ab.release(mst)

    mst = P.ab.mark()
    st_g_ln1 = ln_stats(P, xhat=xg, Ttot=S, sq_scratch=sq)
    xstage = [P.ab.alloc(512), P.ab.alloc(512)]
    xdma = [None, None]
    si = 0
    for c in range(NCH):
        for (t0, nt) in tok_chunks(S):
            raw, tk = xg(c, t0, nt)
            slot = xstage[si % 2]
            P.wait("vector", tk + st_g_ln1["ticks"] + [xdma[si % 2]])
            w1_ = P.inc("vector", lambda e, slot=slot, raw=raw, t0=t0, nt=nt:
                        e.tensor_sub(slot[:, 0:nt], raw[:, 0:nt],
                                     st_g_ln1["mbc"][:, t0 : t0 + nt]), "s_dve")
            P.wait("vector", [w1_])
            w2_ = P.inc("vector", lambda e, slot=slot, t0=t0, nt=nt, c=c:
                        e.scalar_tensor_tensor(
                            slot[:, 0:nt], slot[:, 0:nt],
                            cols["L0_attn_ln1_g"][0:128, c : c + 1],
                            st_g_ln1["rbc"][:, t0 : t0 + nt],
                            ALU.mult, ALU.mult), "s_dve")
            P.wait("vector", [w2_])
            w3_ = P.inc("vector", lambda e, slot=slot, nt=nt, c=c:
                        e.tensor_scalar_add(slot[:, 0:nt], slot[:, 0:nt],
                                            cols["L0_attn_ln1_b"][0:128, c : c + 1]),
                        "s_dve")
            xdma[si % 2] = P.dma(
                C["scr_xn1"].ap()[c * 128 : c * 128 + 128, t0 : t0 + nt],
                slot[:, 0:nt], queue="sync", wait=[w3_])
            si += 1
    P.barrier()
    P.ab.release(mst)

    mst = P.ab.mark()
    def kvrhs(c, t0, nt):
        def fill(slot, war):
            t = P.dma(slot[:, 0:nt],
                      C["scr_xn1"].ap()[c * 128 : c * 128 + 128, t0 : t0 + nt],
                      queue="gpsimd", wait=war)
            return [t]
        ap, ticks = C["ring"].get(("kv", c, t0, nt), fill)
        return ap[:, 0:nt], ticks
    stkv = StageOC(P, S)
    def kv_dst(gf, fh, t0, nt):
        h = gf // 128
        if h < 8:
            return C["scr_k"].ap()[h, 0:fh, t0 : t0 + nt]
        return C["scr_vT"].ap()[h - 8, 0:fh, t0 : t0 + nt]
    stkv.dram_fn = kv_dst
    linear_fm(P, W=W["l0_wkv"].ap(), rows=D, Fout=2 * D, rhs=kvrhs,
              tokens=tok_chunks(S), out=stkv.out, evac_wait=stkv.evac_wait,
              on_slice=stkv.on_slice,
              bias=(cols["l0_bkv"], 0), fblock=256, nparity=2)
    P.barrier()
    P.ab.release(mst)

    nqt = T // 512
    nkc = S // 128
    m_attn = P.ab.mark()
    mf_attn = P.af.mark()
    sums = P.af.alloc(2 * 512, parts=97)
    P.barrier()
    OtT = P.ab.alloc(T)
    recb = P.ab.alloc(2 * 512, parts=97)
    Kt = P.ab.alloc(S)
    Vtok = P.ab.alloc(nkc * 128)
    vst = P.ab.alloc(4 * 128)
    Qt = P.ab.alloc(T)
    ptile = P.ab.alloc(6 * 512)
    o_dma = None
    prev_bc = []
    vst_war = None
    tr = None
    hq_mm = None
    for h in range(NCH):
        prev = [hq_mm] if hq_mm else [P.tick("s_mm"), P.tick("s_dve"),
                                      P.tick("s_act")]
        tk_k = P.dma(Kt, C["scr_k"].ap()[h], queue="sync", wait=prev)
        tk_q = P.dma(Qt, C["scr_q"].ap()[h], queue="sync", wait=prev)
        vticks = []
        for g0 in range(0, nkc, 4):
            dts = [P.dma(vst[:, j * 128 : j * 128 + 128],
                         C["scr_vT"].ap()[h, :, (g0 + j) * 128 :
                                          (g0 + j) * 128 + 128],
                         queue="gpsimd", wait=[vst_war] + prev)
                   for j in range(4)]
            b = (g0 // 4) % 6
            ps = P.pbank(b)
            P.wait("tensor", [P.pfree[b]] + dts + P.shared["tick"])
            def tmm(e, ps=ps):
                last = None
                for j in range(4):
                    last = e.matmul(ps[0:128, j * 128 : j * 128 + 128],
                                    vst[0:128, j * 128 : j * 128 + 128],
                                    P.shared["ident"][0:128, 0:128],
                                    start=True, stop=True)
                return last
            mm_t = P.inc("tensor", tmm, "s_mm")
            vst_war = mm_t
            P.wait("vector", [mm_t])
            ev_t = P.inc("vector", lambda e, ps=ps, g0=g0:
                         e.tensor_copy(Vtok[:, g0 * 128 : g0 * 128 + 512],
                                       ps[0:128, 0:512]), "s_dve")
            P.pfree[b] = ev_t
            vticks.append(ev_t)
        P.wait("vector", [tr])
        ts_ms = P.inc("vector", lambda e: e.memset(sums[0:97, :], 1.0),
                      "s_dve")
        qticks = [ts_ms]
        for qt in range(nqt):
            sr, sc = 32 * (qt % 2), 512 * (qt // 2)
            chunks = [dict(kc=j * 128, nk=128, vi=j, qc0=0, nqw=512)
                      for j in range(nkc)]
            if o_dma is not None:
                P.wait("vector", [o_dma])
            ts = attn_qtile(P, Kt=Kt, Kticks=[tk_k], Qt=Qt, Qticks=[tk_q],
                            Vt=Vtok, Vticks=vticks + [ts_ms], dh=128, daug=128,
                            q0=qt * 512, nq=512, chunks=chunks,
                            scale=1.0 / math.sqrt(128.0), octx=6, vsum="mm",
                            onescol_b=P.shared["onescol"],
                            sums_dst=sums[sr : sr + 1, sc : sc + 512],
                            O_dst=OtT[:, qt * 512 : qt * 512 + 512],
                            ptile=ptile)
            qticks += ts
        hq_mm = P.tick("s_mm")
        P.wait("scalar", qticks + prev_bc)
        tl = P.inc("scalar", lambda e:
                   e.activation(sums[0:97, :], sums[0:97, :],
                                ACTF.Ln), "s_act")
        P.wait("scalar", [tl])
        tr = P.inc("scalar", lambda e:
                   e.activation(recb[0:97, :], sums[0:97, :],
                                ACTF.Exp, scale=-1.0), "s_act")
        prev_bc = []
        stt_ts = []
        for qt in range(nqt):
            sr, sc = 32 * (qt % 2), 512 * (qt // 2)
            b = 6 + qt % 2
            ps = P.pbank(b)
            P.wait("tensor", [P.pfree[b], tr] + P.shared["tick"])
            mm_t = P.inc("tensor", lambda e, ps=ps, sr=sr, sc=sc:
                         e.matmul(ps[0:128, 0:512],
                                  P.shared["ones"][sr : sr + 1, 0:128],
                                  recb[sr : sr + 1, sc : sc + 512],
                                  start=True, stop=True), "s_mm")
            prev_bc.append(mm_t)
            P.wait("vector", [mm_t])
            tn = P.inc("vector", lambda e, ps=ps, qt=qt:
                       e.scalar_tensor_tensor(
                           OtT[:, qt * 512 : qt * 512 + 512],
                           OtT[:, qt * 512 : qt * 512 + 512], 1.0,
                           ps[0:128, 0:512], ALU.mult, ALU.mult), "s_dve")
            P.pfree[b] = tn
            stt_ts.append(tn)
        o_dma = P.dma(C["scr_xn1"].ap()[h * 128 : h * 128 + 128, 0:T],
                      OtT[:, 0:T], queue="sync", wait=stt_ts)
    P.wait("sync", [o_dma])
    P.barrier()
    P.ab.release(m_attn)
    P.af.release(mf_attn)

    m_x2h = P.ab.mark()
    x2h = [P.ab.alloc(T) for _ in range(NCH)]
    mrk = P.ab.mark()
    def worhs(c, t0, nt):
        def fill(slot, war, c=c, t0=t0, nt=nt):
            t = P.dma(slot[:, 0:nt],
                      C["scr_xn1"].ap()[c * 128 : c * 128 + 128, t0 : t0 + nt],
                      queue="sync", wait=war)
            return [t]
        ap, tk = C["ring"].get(("ctx", c, t0, nt), fill)
        return ap[:, 0:nt], tk
    linear_fm(P, W=W["L0_attn_Wo"].ap(), rows=D, Fout=D,
              rhs=worhs, tokens=tok_chunks(T),
              out=lambda gf, t0, nt: x2h[gf // 128][:, t0 : t0 + nt],
              stt_resid=lambda gf, t0, nt: xb[gf // 128][:, t0 : t0 + nt],
              bias_aug=(rows["L0_attn_bo"], []), fblock=256, nparity=2)
    P.barrier()
    P.ab.release(mrk)

    mf_ffn = P.af.mark()
    mrk = P.ab.mark()
    st2 = ln_stats(P, xhat=sbuf_loader(P, x2h), Ttot=T, sq_scratch=sq,
                   arena=P.af)
    xn2 = norm_ring_loader(P, C["ring"], sbuf_loader(P, x2h), st2,
                           cols["L0_attn_ln2_g"], cols["L0_attn_ln2_b"],
                           colticks)
    P.barrier()
    stmid = StageOC(P, T, n=2)
    stmid.dram_fn = lambda gf, fh, t0, nt: C["scr_mid"].ap()[gf : gf + fh,
                                                             t0 : t0 + nt]
    linear_fm(P, W=W["L0_attn_W1"].ap(), rows=D, Fout=4096,
              rhs=xn2, tokens=tok_chunks(T),
              out=stmid.out, evac_wait=stmid.evac_wait, on_slice=stmid.on_slice,
              bias=(cols["L0_attn_b1"], 0), act=ACTF.Relu, fblock=512)
    P.barrier()
    P.ab.release(mrk)
    P.af.release(mf_ffn)

    mrk = P.ab.mark()
    mfk = P.af.mark()
    w2slab = [[P.ab.alloc(256) for _ in range(NCH)] for _ in range(2)]
    halfac = [P.af.alloc(512) for _ in range(NCH)]
    prev_w2 = {}
    ev_q = [None] * NCH
    for ti2, (t0, nt) in enumerate(tok_chunks(T)):
        for g in range(4):
            rsl = []
            for c in range(NCH):
                gc = g * 8 + c
                def fill(slot, war, gc=gc, t0=t0, nt=nt):
                    t = P.dma(slot[:, 0:nt],
                              C["scr_mid"].ap()[gc * 128 : gc * 128 + 128,
                                                t0 : t0 + nt],
                              queue="sync", wait=war)
                    return [t]
                rsl.append(C["ring"].get(("mid", ti2, gc), fill))
            for q4 in range(4):
                par = (g * 4 + q4) % 2
                slabs = w2slab[par]
                sticks = []
                for c in range(NCH):
                    t = P.dma(slabs[c],
                              W["L0_attn_W2"].ap()[g * 1024 + c * 128 :
                                                   g * 1024 + c * 128 + 128,
                                                   q4 * 256 : q4 * 256 + 256],
                              queue="gpsimd", wait=[prev_w2.get(par)])
                    sticks.append(t)
                bks = (0, 1) if q4 % 2 == 0 else (2, 3)
                P.wait("tensor", [P.pfree[bk] for bk in bks] + sticks
                       + [t for (_, tk) in rsl for t in tk])
                def mmg(e, rsl=rsl, g=g, nt=nt, q4=q4, slabs=slabs, bks=bks):
                    last = None
                    for ol in range(2):
                        for c in range(NCH):
                            last = e.matmul(
                                P.pbank(bks[ol])[0:128, 0:nt],
                                slabs[c][:, ol * 128 : ol * 128 + 128],
                                rsl[c][0][:, 0:nt],
                                start=(c == 0), stop=(g < 3 and c == NCH - 1))
                    if g == 3:
                        for ol in range(2):
                            oc = q4 * 2 + ol
                            last = e.matmul(
                                P.pbank(bks[ol])[0:128, 0:nt],
                                rows["L0_attn_b2"][0:1,
                                                   oc * 128 : oc * 128 + 128],
                                P.shared["onesrow"][0:1, 0:nt],
                                start=False, stop=True)
                    return last
                mm_t = P.inc("tensor", mmg, "s_mm")
                prev_w2[par] = mm_t
                P.wait("vector", [mm_t] + [ev_q[q4 * 2 + ol]
                                           for ol in range(2)])
                if g == 0:
                    def ev0(e, nt=nt, q4=q4, bks=bks):
                        last = None
                        for ol in range(2):
                            last = e.tensor_copy(halfac[q4 * 2 + ol][:, 0:nt],
                                                 P.pbank(bks[ol])[0:128, 0:nt])
                        return last
                    ev_t = P.inc("vector", ev0, "s_dve")
                else:
                    def ev1(e, nt=nt, q4=q4, bks=bks):
                        last = None
                        for ol in range(2):
                            oc = q4 * 2 + ol
                            last = e.scalar_tensor_tensor(
                                halfac[oc][:, 0:nt],
                                P.pbank(bks[ol])[0:128, 0:nt],
                                1.0, halfac[oc][:, 0:nt], ALU.mult, ALU.add)
                        return last
                    ev_t = P.inc("vector", ev1, "s_dve")
                for ol in range(2):
                    ev_q[q4 * 2 + ol] = ev_t
                for bk in bks:
                    P.pfree[bk] = ev_t
        P.wait("vector", [t for t in ev_q])
        def ev2(e, t0=t0, nt=nt):
            last = None
            for oc in range(NCH):
                last = e.tensor_add(x2h[oc][:, t0 : t0 + nt],
                                    x2h[oc][:, t0 : t0 + nt],
                                    halfac[oc][:, 0:nt])
            return last
        ev_t = P.inc("vector", ev2, "s_dve")
        for oc in range(NCH):
            ev_q[oc] = ev_t
        P.wait("vector", [ev_t])
    P.barrier()
    P.ab.release(mrk)
    P.af.release(mfk)

    wf = xb
    mrk = P.ab.mark()
    def wfrhs(c, t0, nt):
        if c >= 8:
            return x2h[c - 8][:, t0 : t0 + nt], []
        def fill(slot, war, c=c, t0=t0, nt=nt):
            t = P.dma(slot[:, 0:nt],
                      C["scr_mamba"].ap()[c * 128 : c * 128 + 128, t0 : t0 + nt],
                      queue="sync", wait=war)
            return [t]
        ap, tk = C["ring"].get(("mam", c, t0, nt), fill)
        return ap[:, 0:nt], tk
    linear_fm(P, W=W["L0_attn_Wf"].ap(), rows=2 * D, Fout=D, rhs=wfrhs,
              tokens=tok_chunks(T),
              out=lambda gf, t0, nt: wf[gf // 128][:, t0 : t0 + nt],
              bias=(cols["L0_attn_bf"], 0), fblock=256)
    P.barrier()
    P.ab.release(mrk)

    m_lnf = P.ab.mark()
    stf = ln_stats(P, xhat=sbuf_loader(P, wf), Ttot=T, sq_scratch=sq)
    ln_apply_bf(P, xhat=sbuf_loader(P, wf), st=stf, Ttot=T,
                out=lambda c, t0, nt: wf[c][:, t0 : t0 + nt],
                g_col=cols["L0_attn_lnf_g"], b_col=cols["L0_attn_lnf_b"],
                gb_ticks=colticks)
    P.barrier()
    for c in range(NCH):
        P.inc("vector", lambda e, c=c: e.tensor_add(x[c][:, 0:T], x[c][:, 0:T],
                                                    wf[c][:, 0:T]), "s_dve")
    P.barrier()
    P.ab.release(m_lnf)

    for c in range(NCH):
        P.inc("vector", lambda e, c=c: e.tensor_scalar_mul(x2h[c][:, 0:T],
                                                           x[c][:, 0:T], 1.0),
              "s_dve")
    P.barrier()
    stt = ln_stats(P, xhat=sbuf_loader(P, x2h), Ttot=T, sq_scratch=sq,
                   want_f32=True)
    topln(P, xadd_f32=lambda c: (x[c], []), xhat_out=lambda c: xb[c],
          st=stt, T=T)
    P.barrier()
    P.ab.release(m_blk)
    P.af.release(mf_blk)


def allgather(P, *, tiles, ticks, ccin, ccout):
    dts = []
    for c in range(NCH):
        dts.append(P.dma(ccin.ap()[c * 128 : c * 128 + 128, :], tiles[c][:, :],
                         queue="gpsimd", wait=ticks))
    P.wait("gpsimd", dts)
    t = P.inc("gpsimd", lambda e: e.collective_compute(
        "AllGather", ALU.bypass, replica_groups=P.cfg.groups,
        ins=[ccin.ap()], outs=[ccout.ap()]), "s_cc")
    return t



def build_program(cfg, arrs, percore):
    nc = bass.Bass("TRN2", target_bir_lowering=False, debug=False)
    P = Prog(nc, cfg)
    S, T = cfg.S, cfg.T
    dt_of = {}
    for name, a in arrs.items():
        dt = BF16 if a.dtype == BFNP else F32
        P.dram_t(name, list(a.shape), dt, kind="ExternalInput")
    for name, lst in percore.items():
        a = lst[0]
        dt = BF16 if a.dtype == BFNP else F32
        P.dram_t(name, list(a.shape), dt, kind="ExternalInput")
    out_t = P.dram_t("out", [D, T], F32, kind="ExternalOutput")
    P.setup_shared()

    C = {"w": P.dram}
    C["x_bT"] = P.dram["x_bT"]
    C["sel"] = None
    C["x"] = [P.af.alloc(T) for _ in range(NCH)]
    C["xb"] = [P.ab.alloc(T) for _ in range(NCH)]
    C["sq"] = [P.ab.alloc(512) for _ in range(NCH)]
    C["ring"] = Ring(P, 14, 512)
    sel = P.af.alloc(2)
    P.dma(sel, P.dram["sel01"].ap())
    C["sel"] = sel
    if "l0a" in cfg.parts:
        for nm, shp, dt in [("scr_gate", [D, T], BF16), ("scr_q", [8, 128, T], BF16),
                            ("scr_k", [8, 128, S], BF16),
                            ("scr_vT", [8, 128, S], BF16),
                            ("scr_xn1", [D, S], BF16), ("scr_mid", [4096, T], BF16),
                            ("scr_mamba", [D, T], BF16)]:
            C[nm] = P.dram_t(nm, shp, dt)
        C["cc1_in"] = P.dram_t("cc1_in", [D, T], BF16)
        C["cc1_out"] = P.dram_t("cc1_out", [2, D, T], BF16)
    if "l1a" in cfg.parts:
        C["l1a_q"] = P.dram_t("l1a_q", [8, 128, T], BF16)
        C["l1a_k"] = P.dram_t("l1a_k", [8, 128, S + 2], BF16)
        C["l1a_vT"] = P.dram_t("l1a_vT", [8, 128, S + 2], BF16)
        C["cc3_in"] = P.dram_t("cc3_in", [D, T], BF16)
        C["cc3_out"] = P.dram_t("cc3_out", [2, D, T], BF16)
    if "l1f" in cfg.parts:
        KV = T + 512
        C["l1f_x3"] = P.dram_t("l1f_x3", [D, T], F32)
        C["l1f_mid"] = P.dram_t("l1f_mid", [2816, T], BF16)
        C["l1f_q"] = P.dram_t("l1f_q", [4, 128, T], BF16)
        C["l1f_k"] = P.dram_t("l1f_k", [4, 128, KV], BF16)
        C["l1f_vT"] = P.dram_t("l1f_vT", [4, 128, KV], BF16)
        C["cc4_in"] = P.dram_t("cc4_in", [D, T], BF16)
        C["cc4_out"] = P.dram_t("cc4_out", [2, D, T], BF16)
        C["ccc_in"] = P.dram_t("ccc_in", [D, 1], BF16)
        C["ccc_out"] = P.dram_t("ccc_out", [2, D, 1], BF16)
    if "l0f" in cfg.parts:
        C["l0f_q"] = P.dram_t("l0f_q", [4, 128, T], BF16)
        C["l0f_k"] = P.dram_t("l0f_k", [4, 128, S], BF16)
        C["l0f_vT"] = P.dram_t("l0f_vT", [4, 128, S], BF16)
        C["cc2_in"] = P.dram_t("cc2_in", [D, T], BF16)
        C["cc2_out"] = P.dram_t("cc2_out", [2, D, T], BF16)
    for c in range(NCH):
        P.dma(C["x"][c], P.dram["x_ownT"].ap()[c * 128 : c * 128 + 128, :])
        P.dma(C["xb"][c], P.dram["x_ownT"].ap()[c * 128 : c * 128 + 128, :],
              queue="gpsimd")
    P.barrier()

    if "l0a" in cfg.parts:
        build_l0a(P, C)
        if len(cfg.parts) > 1:
            t_cc = allgather(P, tiles=C["xb"], ticks=[P.tick("s_dve")],
                             ccin=C["cc1_in"], ccout=C["cc1_out"])
            C["cc_x1"] = (C["cc1_out"], t_cc)
    if "l0f" in cfg.parts:
        build_l0f(P, C)
        if "l1a" in cfg.parts or "l1f" in cfg.parts:
            t_cc = allgather(P, tiles=C["xb"], ticks=[P.tick("s_dve")],
                             ccin=C["cc2_in"], ccout=C["cc2_out"])
            C["cc_x2"] = (C["cc2_out"], t_cc)
    if "l1a" in cfg.parts:
        build_l1a(P, C)
        if "l1f" in cfg.parts:
            t_cc = allgather(P, tiles=C["xb"], ticks=[P.tick("s_dve")],
                             ccin=C["cc3_in"], ccout=C["cc3_out"])
            C["cc_x3"] = (C["cc3_out"], t_cc)
    if "l1f" in cfg.parts:
        build_l1f(P, C)

    ots = []
    for c in range(NCH):
        ots.append(P.dma(out_t.ap()[c * 128 : c * 128 + 128, :], C["x"][c],
                         queue="sync", wait=[P.tick("s_dve"), P.tick("s_act"),
                                             P.tick("s_mm")]))
    P.wait("sync", ots)
    P.play()
    nc.finalize()
    return nc, P


def host_prep(inputs, cfg):
    S, T, n = cfg.S, cfg.T, cfg.n_cores
    f32 = np.float32
    arrs = {k: np.ascontiguousarray(np.asarray(v, dtype=f32))
            for k, v in inputs.items()}
    wdi = np.zeros((D, 64), f32)
    wdi[:, 0:16] = arrs["L0_attn_Wdelta"]
    wdi[:, 32:48] = arrs["L0_attn_Win"]
    bdi = np.zeros((64,), f32)
    bdi[0:16] = arrs["L0_attn_bdelta"]
    bdi[32:48] = arrs["L0_attn_b_in"]
    arrs["l0_wdi"] = wdi
    arrs["l0_bdi"] = bdi
    arrs["l0_wkv"] = np.concatenate([arrs["L0_attn_Wk"], arrs["L0_attn_Wv"]], 1)
    arrs["l0_bkv"] = np.concatenate([arrs["L0_attn_bk"], arrs["L0_attn_bv"]])
    arrs["l0_wv2"] = np.ascontiguousarray(
        arrs["L0_ffn_Wv"].transpose(1, 0, 2).reshape(D, 320))
    arrs["l0_wo2"] = np.ascontiguousarray(arrs["L0_ffn_Wo"].reshape(320, D))
    SP = S + 128
    nkc = SP // 128
    H = 16
    arrs["l1a_qs"] = np.tile(arrs["L1_attn_q_scale"], 2)[:, None]
    arrs["l1a_ks"] = np.tile(arrs["L1_attn_k_scale"], 2)[:, None]
    nl = arrs["L1_attn_null_kv"].reshape(16, 2, 2, 64)
    arrs["l1a_nk"] = np.ascontiguousarray(
        nl[:, :, 0, :].transpose(0, 2, 1).reshape(D, 2)).astype(BFNP)
    arrs["l1a_nv"] = np.ascontiguousarray(
        nl[:, :, 1, :].transpose(0, 2, 1).reshape(D, 2)).astype(BFNP)
    slopes = np.array([(2.0 ** (-2.0 ** (-(math.log2(H) - 3)))) ** (i + 1)
                       for i in range(H)], f32)
    dg = np.zeros((4, 128, 512), f32)
    for j in range(4):
        pp, cc2 = np.meshgrid(np.arange(128), np.arange(512), indexing="ij")
        dg[j] = np.where(pp + 128 * j > cc2, NEG, 0.0)
    arrs["c_l1a_diag"] = dg
    ka = np.zeros((H, 2, SP), f32)
    for h in range(H):
        ka[h, 0, :] = slopes[h] * 64.0 / 8.0
        ka[h, 1, :] = slopes[h] / 8.0
    arrs["c_l1a_kaug"] = ka.astype(BFNP)
    KV = T + 512
    arrs["l1f_qs"] = np.tile(arrs["L1_ffn_q_scale"], (1, 2))[:, :, None]
    arrs["l1f_ks"] = np.tile(arrs["L1_ffn_k_scale"], (1, 2))[:, :, None]
    pp, cc2 = np.meshgrid(np.arange(128), np.arange(128), indexing="ij")
    arrs["c_l1f_diag"] = np.where(pp > cc2, NEG, 0.0).astype(f32)
    arrs["c_l1f_prev"] = np.where(cc2 > pp, NEG, 0.0).astype(f32)
    w1 = arrs["L1_ffn_w1"]
    b1 = arrs["L1_ffn_b1"]
    w1p = np.zeros((2, D, 5632), f32)
    b1p = np.zeros((2, 5632), f32)
    for j in range(22):
        a0, a1 = j * 128, min(2730, (j + 1) * 128)
        nn_ = a1 - a0
        w1p[:, :, j * 256 : j * 256 + nn_] = w1[:, :, a0:a1]
        w1p[:, :, j * 256 + 128 : j * 256 + 128 + nn_] = w1[:, :, 2730 + a0 :
                                                            2730 + a1]
        b1p[:, j * 256 : j * 256 + nn_] = b1[:, a0:a1]
        b1p[:, j * 256 + 128 : j * 256 + 128 + nn_] = b1[:, 2730 + a0 :
                                                         2730 + a1]
    arrs["l1f_w1p"] = w1p
    arrs["l1f_b1p"] = b1p
    x = arrs.pop("L0_attn_x")
    B = x.shape[0]
    percore = {"x_bT": [], "x_ownT": [], "sel01": [], "l1a_qaug": [],
               "l1a_row66": [], "l1a_bias": [], "l1f_prev0": [],
               "l1f_qcs": [], "l1f_qss": [], "l1f_kcs": [], "l1f_kss": []}
    for c in range(n):
        b, h = c // 2, c % 2
        xbT = np.ascontiguousarray(x[b].T)
        percore["x_bT"].append(xbT)
        percore["x_ownT"].append(np.ascontiguousarray(xbT[:, h * T:(h + 1) * T]))
        s01 = np.zeros((128, 2), f32)
        s01[:, 0] = 1.0 if h == 0 else 0.0
        s01[:, 1] = 0.0 if h == 0 else 1.0
        percore["sel01"].append(s01)
        own0 = h * T
        pair0 = (1 - h) * T
        qg = own0 + np.arange(T)
        qa = np.zeros((3, T), f32)
        qa[0] = -(qg // 64)
        qa[1] = -(qg % 64)
        qa[2] = 1.0
        percore["l1a_qaug"].append(qa.astype(BFNP))
        r66 = np.zeros((1, SP), f32)
        if h == 0:
            r66[0, T:S] = NEG
        r66[0, S + 2 :] = NEG
        percore["l1a_row66"].append(r66.astype(BFNP))
        bias = np.zeros((H, 128, nkc), f32)
        cloc = (np.arange(128)[:, None] + 128 * np.arange(nkc)[None, :])
        jg = np.zeros_like(cloc, dtype=np.float64)
        m_own = cloc < T
        m_pair = (cloc >= T) & (cloc < S)
        m_null = (cloc >= S) & (cloc < S + 2)
        jg[m_own] = own0 + cloc[m_own] + 2
        jg[m_pair] = pair0 + (cloc[m_pair] - T) + 2
        jg[m_null] = cloc[m_null] - S
        for hh in range(H):
            bias[hh] = slopes[hh] * (jg - 2.0)
            bias[hh][~(m_own | m_pair | m_null)] = 0.0
        percore["l1a_bias"].append(bias)
        pv0 = np.zeros((4, 128, 512), f32)
        for j in range(4):
            nqw = j * 128 + 128
            if h == 0:
                pv0[j, :, 0:nqw] = NEG
            else:
                ppj, ccj = np.meshgrid(np.arange(128), np.arange(128),
                                       indexing="ij")
                pv0[j, :, j * 128 : j * 128 + 128] = np.where(ccj > ppj, NEG,
                                                              0.0)
        percore["l1f_prev0"].append(pv0)
        DH2, WIN = 64, 512
        half_idx = np.arange(0, DH2, 2, dtype=np.float64)
        inv_freq = 1.0 / 10000.0 ** (half_idx / DH2)
        base = (half_idx + 0.4 * DH2) / (1.4 * DH2)
        def rot_tabs(tg):
            fr = tg[None, :] * inv_freq[:, None]
            fr = np.concatenate([fr, fr], 0)
            power = (tg - S // 2) / float(WIN // 2)
            sc = base[:, None] ** power[None, :]
            sc = np.concatenate([sc, sc], 0)
            return np.cos(fr), np.sin(fr), sc
        tq = (own0 + np.arange(T)).astype(np.float64)
        cosq, sinq, scq = rot_tabs(tq)
        tk = (own0 - 512 + np.arange(KV)).astype(np.float64)
        cosk, sink, sck = rot_tabs(tk)
        percore["l1f_qcs"].append(np.tile(cosq * scq, (2, 1)).astype(BFNP))
        percore["l1f_qss"].append(np.tile(sinq * scq, (2, 1)).astype(BFNP))
        percore["l1f_kcs"].append(np.tile(cosk / sck, (2, 1)).astype(BFNP))
        percore["l1f_kss"].append(np.tile(sink / sck, (2, 1)).astype(BFNP))
    return arrs, percore


_CACHE = {}


def run_cores(inputs, cfg):
    arrs, percore = host_prep(inputs, cfg)
    key = (cfg.S, cfg.n_cores, tuple(cfg.parts))
    if key not in _CACHE:
        nc, P = build_program(cfg, arrs, percore)
        _CACHE[key] = (nc, P)
    nc, P = _CACHE[key]
    in_maps = []
    for c in range(cfg.n_cores):
        m = {}
        for name, a in arrs.items():
            m[name] = a
        for name, a in P.consts.items():
            m[name] = a
        for name, lst in percore.items():
            m[name] = lst[c]
        in_maps.append(m)
    res = run_bass_kernel_spmd(nc, in_maps, core_ids=list(range(cfg.n_cores)))
    return res.results


def kernel(**inputs):
    cfg = Cfg(S=4096, n_cores=8)
    results = run_cores(inputs, cfg)
    B, S = 4, cfg.S
    out = np.zeros((B, S, D), np.float32)
    for c in range(cfg.n_cores):
        b, h = c // 2, c % 2
        out[b, h * cfg.T:(h + 1) * cfg.T, :] = results[c]["out"].T
    return out


def cc_ring_loader(P, ring, cc_out, cc_tick, S):
    T = S // 2
    def f(c, t0, nt):
        def fill(slot, war):
            sl_ = 0 if t0 < T else 1
            tl = t0 - sl_ * T
            t = P.dma(slot[:, 0:nt],
                      cc_out.ap()[sl_, c * 128 : c * 128 + 128, tl : tl + nt],
                      queue="sync", wait=war + [cc_tick])
            return [t]
        assert t0 // T == (t0 + nt - 1) // T
        ap, ticks = ring.get(("cc", id(cc_out), c, t0, nt), fill)
        return ap[:, 0:nt], ticks
    return f


def pe_transpose128(P, *, src_ap, src_ticks, dst_ap, nrows, ident_off=0,
                    bank=5, dst_war=(), fp32=False):
    ident = P.shared["identf"] if fp32 else P.shared["ident"]
    ps = P.pbank(bank)
    P.wait("tensor", [P.pfree[bank]] + src_ticks + P.shared["tick"])
    nc_ = src_ap.shape[1]
    mm_t = P.inc("tensor", lambda e: e.matmul(
        ps[0:nc_, 0:nrows], src_ap,
        ident[ident_off : ident_off + nrows, ident_off : ident_off + nrows],
        start=True, stop=True), "s_mm")
    P.wait("vector", [mm_t] + [t for t in dst_war if t])
    ev = P.inc("vector", lambda e: e.tensor_copy(dst_ap, ps[0:nc_, 0:nrows]),
               "s_dve")
    P.pfree[bank] = ev
    return ev



def build_l0f(P, C):
    cfg = P.cfg
    S, T = cfg.S, cfg.T
    W = C["w"]
    x = C["x"]
    xb = C["xb"]
    sq = C["sq"]
    cc_out, cc_tick = C["cc_x1"]
    nts_f = S // 128
    nts_o = T // 128
    nqt = T // 512
    nkc = S // 128
    H, DH, E = 8, 64, 5
    m_blk = P.ab.mark()
    mf_blk = P.af.mark()
    ccl = cc_ring_loader(P, C["ring"], cc_out, cc_tick, S)
    own = sbuf_loader(P, xb)

    glv = P.af.alloc(nts_f * 40)
    glo = P.af.alloc(nts_o * 40)
    wv = P.ab.alloc(nts_f * 40)
    mo = P.ab.alloc(nts_o * 40)
    sg_t = P.af.alloc(40)
    mx = P.af.alloc(16)
    def topk_ops(gt, dst, ts, do_sig, tick_in):
        g5 = gt.rearrange("p (h e) -> p h e", e=5)
        m1 = mx[:, 0:8]
        m2 = mx[:, 8:16]
        P.wait("vector", [tick_in, P.tick("s_act")])
        t1 = P.inc("vector", lambda e: e.reduce_max(m1, g5, axis=AX.X), "s_dve")
        P.wait("vector", [t1])
        t2 = P.inc("vector", lambda e: e.tensor_tensor(
            dst.rearrange("p (h e) -> p h e", e=5), g5,
            m1.rearrange("p (h o) -> p h o", o=1).broadcast_to((128, 8, 5)),
            ALU.is_ge), "s_dve")
        P.wait("vector", [t2])
        t3 = P.inc("vector", lambda e: e.scalar_tensor_tensor(
            sg_t[0:128, 0:40], dst, -1e30, gt, ALU.mult, ALU.add), "s_dve")
        P.wait("vector", [t3])
        t4 = P.inc("vector", lambda e: e.reduce_max(
            m2, sg_t[0:128, 0:40].rearrange("p (h e) -> p h e", e=5),
            axis=AX.X), "s_dve")
        P.wait("vector", [t4])
        t5 = P.inc("vector", lambda e: e.tensor_tensor(
            dst.rearrange("p (h e) -> p h e", e=5), g5,
            m2.rearrange("p (h o) -> p h o", o=1).broadcast_to((128, 8, 5)),
            ALU.is_ge), "s_dve")
        out_t = t5
        if do_sig:
            P.wait("scalar", [t5, tick_in])
            t6 = P.inc("scalar", lambda e: e.activation(
                sg_t[0:128, 0:40], gt, ACTF.Sigmoid), "s_act")
            P.wait("vector", [t6])
            out_t = P.inc("vector", lambda e: e.tensor_mul(
                dst, dst, sg_t[0:128, 0:40]), "s_dve")
        return out_t

    wv_ticks = {}
    def glv_out(ti, f0, nf):
        return glv[:, ti * 40 : ti * 40 + 40]
    lt, mk0 = linear_tm(P, W=W["L0_ffn_Ws"].ap(), rows=D, Fout=40, rhs=ccl,
                        tslices=[(i * 128, 128) for i in range(nts_f)],
                        out=glv_out, banks=(0, 1, 2, 3))
    for ti in range(nts_f):
        wv_ticks[ti] = topk_ops(glv[:, ti * 40 : ti * 40 + 40],
                                wv[:, ti * 40 : ti * 40 + 40], ti, True, lt)
    P.barrier()
    P.ab.release(mk0)
    lt, mk0 = linear_tm(P, W=W["L0_ffn_Wd"].ap(), rows=D, Fout=40, rhs=own,
                        tslices=[(i * 128, 128) for i in range(nts_o)],
                        out=lambda ti, f0, nf: glo[:, ti * 40 : ti * 40 + 40],
                        banks=(0, 1, 2, 3))
    mo_ticks = {}
    for ti in range(nts_o):
        mo_ticks[ti] = topk_ops(glo[:, ti * 40 : ti * 40 + 40],
                                mo[:, ti * 40 : ti * 40 + 40], ti, False, lt)
    P.barrier()
    P.ab.release(mk0)

    mkv = P.ab.mark()
    xvt = [P.ab.alloc(320), P.ab.alloc(320)]
    vtt = [P.ab.alloc(512), P.ab.alloc(512)]
    vstage = [P.ab.alloc(512) for _ in range(4)]
    vst_dma = [None] * 4
    asm_tick = {}

    def v_asm(ti, f0, nf, tick):
        xv = xvt[ti % 2]
        vt = vtt[ti % 2]
        wvt = wv[:, ti * 40 : ti * 40 + 40].rearrange("p (h e) -> p h e", e=5)
        P.wait("vector", [tick, wv_ticks[ti], asm_tick.get(ti % 2)])
        tacc = None
        for e_ in range(E):
            wcol = wvt[:, :, e_ : e_ + 1].broadcast_to((128, 8, 64)) \
                .rearrange("p h k -> p (h k)")
            xcol = xv[:, e_ * 64 : e_ * 64 + 64] \
                .rearrange("p (o k) -> p o k", o=1).broadcast_to((128, 8, 64)) \
                .rearrange("p h k -> p (h k)")
            if e_ == 0:
                tacc = P.inc("vector", lambda e, vt=vt, wcol=wcol, xcol=xcol:
                             e.tensor_tensor(vt[:, 0:512], wcol, xcol, ALU.mult),
                             "s_dve")
            else:
                tmp = sg_t[0:128, 0:40]
                pass
        return tacc

    vtmp = P.ab.alloc(512)

    def v_asm(ti, f0, nf, tick):
        xv = xvt[ti % 2]
        vt = vtt[ti % 2]
        wvt = wv[:, ti * 40 : ti * 40 + 40].rearrange("p (h e) -> p h e", e=5)
        P.wait("vector", [tick, wv_ticks[ti], asm_tick.get(ti % 2)])
        tl = None
        tl = None
        for e_ in range(E):
            wcol = wvt[:, :, e_ : e_ + 1].broadcast_to((128, 8, 64))
            xcol = xv[:, e_ * 64 : e_ * 64 + 64] \
                .rearrange("p (o k) -> p o k", o=1).broadcast_to((128, 8, 64))
            dst = (vt[:, 0:512] if e_ == 0 else
                   vtmp[:, 0:512]).rearrange("p (h k) -> p h k", k=64)
            tm = P.inc("vector", lambda e, dst=dst, wcol=wcol, xcol=xcol:
                       e.tensor_tensor(dst, wcol, xcol, ALU.mult), "s_dve")
            P.wait("vector", [tm])
            if e_ > 0:
                tl = P.inc("vector", lambda e, vt=vt:
                           e.tensor_add(vt[:, 0:512], vt[:, 0:512],
                                        vtmp[:, 0:512]), "s_dve")
                P.wait("vector", [tl])
        for pr in range(4):
            ev = pe_transpose128(P, src_ap=vt[:, pr * 128 : pr * 128 + 128],
                                 src_ticks=[tl], nrows=128,
                                 dst_ap=vstage[pr][:, (ti % 4) * 128 :
                                                   (ti % 4) * 128 + 128],
                                 bank=4 + (pr % 2), dst_war=[vst_dma[pr]])
            if ti % 4 == 3:
                vst_dma[pr] = P.dma(
                    C["l0f_vT"].ap()[pr, :, (ti - 3) * 128 : (ti + 1) * 128],
                    vstage[pr][:, 0:512], queue="sync", wait=[ev])
        asm_tick[ti % 2] = tl

    def xv_wait(ti):
        w = [asm_tick.get(ti % 2)]
        if ti % 4 < 4:
            w += [vst_dma[pr] for pr in range(4)]
        return [t for t in w if t]

    lt2, mk1 = linear_tm(P, W=W["l0_wv2"].ap(), rows=D, Fout=320, rhs=ccl,
                         tslices=[(i * 128, 128) for i in range(nts_f)],
                         out=lambda ti, f0, nf: xvt[ti % 2][:, f0 : f0 + nf],
                         banks=(0, 1, 2, 3), on_out=v_asm, evac_wait=xv_wait)
    P.barrier()
    P.ab.release(mkv)

    mkq = P.ab.mark()
    stq = StageOC(P, T)
    stq.dram_fn = lambda gf, fh, t0, nt: C["l0f_q"].ap()[gf // 128, 0:fh,
                                                         t0 : t0 + nt]
    linear_fm(P, W=W["L0_ffn_Wq"].ap(), rows=D, Fout=512,
              rhs=own, tokens=tok_chunks(T),
              out=stq.out, evac_wait=stq.evac_wait, on_slice=stq.on_slice,
              fblock=256, nparity=2)
    stk = StageOC(P, S)
    stk.dram_fn = lambda gf, fh, t0, nt: C["l0f_k"].ap()[gf // 128, 0:fh,
                                                         t0 : t0 + nt]
    linear_fm(P, W=W["L0_ffn_Wk"].ap(), rows=D, Fout=512,
              rhs=ccl, tokens=tok_chunks(S),
              out=stk.out, evac_wait=stk.evac_wait, on_slice=stk.on_slice,
              fblock=256, nparity=2)
    P.barrier()
    P.ab.release(mkq)

    m_attn = P.ab.mark()
    mf_attn = P.af.mark()
    Ot = [P.ab.alloc(T) for _ in range(4)]
    sums = P.af.alloc(2 * 512, parts=97)
    recb = P.ab.alloc(2 * 512, parts=97)
    m_heads = P.ab.mark()
    Kt = P.ab.alloc(S)
    vst = P.ab.alloc(8 * 128)
    Vtok = P.ab.alloc(nkc * 65)
    Qt = P.ab.alloc(T)
    ptile = P.ab.alloc(6 * 512)
    P.inc("vector", lambda e: e.memset(
        Vtok[:, 0 : nkc * 65].rearrange("p (k v) -> p k v", v=65)
        [:, :, 64:65], 1.0), "s_dve")
    P.barrier()
    vst_war = None
    prev_bc = []
    tr = None
    hq_mm = None
    for pr in range(4):
        prev = [hq_mm] if hq_mm else [P.tick("s_mm"), P.tick("s_dve"),
                                      P.tick("s_act")]
        tk_k = P.dma(Kt, C["l0f_k"].ap()[pr], queue="sync", wait=prev)
        tk_q = P.dma(Qt, C["l0f_q"].ap()[pr], queue="sync", wait=prev)
        for hh in range(2):
            off = hh * 64
            vprev = [hq_mm] if hq_mm else [P.tick("s_mm")]
            vticks = []
            for g0 in range(0, nkc, 8):
                dts = [P.dma(vst[:, j * 128 : j * 128 + 128],
                             C["l0f_vT"].ap()[pr, :, (g0 + j) * 128 :
                                              (g0 + j) * 128 + 128],
                             queue="gpsimd", wait=[vst_war] + vprev)
                       for j in range(8)]
                b = (g0 // 8) % 6
                ps = P.pbank(b)
                P.wait("tensor", [P.pfree[b]] + dts + P.shared["tick"])
                def tmm(e, ps=ps, off=off):
                    last = None
                    for j in range(8):
                        last = e.matmul(
                            ps[0:128, j * 64 : j * 64 + 64],
                            vst[off : off + 64, j * 128 : j * 128 + 128],
                            P.shared["ident"][off : off + 64, off : off + 64],
                            start=True, stop=True)
                    return last
                mm_t = P.inc("tensor", tmm, "s_mm")
                vst_war = mm_t
                P.wait("vector", [mm_t])
                def ev(e, ps=ps, g0=g0):
                    last = None
                    for j in range(8):
                        last = e.tensor_copy(
                            Vtok[:, (g0 + j) * 65 : (g0 + j) * 65 + 64],
                            ps[0:128, j * 64 : j * 64 + 64])
                    return last
                ev_t = P.inc("vector", ev, "s_dve")
                P.pfree[b] = ev_t
                vticks.append(ev_t)
            h = pr * 2 + hh
            P.wait("vector", [tr])
            ts_ms = P.inc("vector", lambda e: e.memset(sums[0:97, :], 1.0),
                          "s_dve")
            qticks = [ts_ms]
            for qt in range(nqt):
                sr, sc = 32 * (qt % 2), 512 * (qt // 2)
                chunks = [dict(kc=j * 128, nk=128, vi=j, qc0=0, nqw=512)
                          for j in range(nkc)]
                ts = attn_qtile(P, Kt=Kt[off : off + 64, :], Kticks=[tk_k],
                                Qt=Qt[off : off + 64, :], Qticks=[tk_q],
                                Vt=Vtok, Vticks=vticks + [ts_ms], dh=64,
                                daug=64,
                                q0=qt * 512, nq=512,
                                chunks=chunks, scale=1.0 / 8.0, octx=6,
                                vsum="col", onescol_b=P.shared["onescol"],
                                sums_dst=sums[sr : sr + 1, sc : sc + 512],
                                O_dst=Ot[pr][off : off + 64,
                                             qt * 512 : qt * 512 + 512],
                                ptile=ptile)
                qticks += ts
            hq_mm = P.tick("s_mm")
            P.wait("scalar", qticks + prev_bc)
            tl = P.inc("scalar", lambda e:
                       e.activation(sums[0:97, :], sums[0:97, :],
                                    ACTF.Ln), "s_act")
            P.wait("scalar", [tl])
            tr = P.inc("scalar", lambda e:
                       e.activation(recb[0:97, :], sums[0:97, :],
                                    ACTF.Exp, scale=-1.0), "s_act")
            prev_bc = []
            for qt in range(nqt):
                sr, sc = 32 * (qt % 2), 512 * (qt // 2)
                b = 6 + qt % 2
                ps = P.pbank(b)
                P.wait("tensor", [P.pfree[b], tr] + P.shared["tick"])
                mm_t = P.inc("tensor", lambda e, ps=ps, sr=sr, sc=sc, hh=hh:
                             e.matmul(ps[hh * 64 : hh * 64 + 64, 0:512],
                                      P.shared["ones"][sr : sr + 1,
                                                       hh * 64 : hh * 64 + 64],
                                      recb[sr : sr + 1, sc : sc + 512],
                                      start=True, stop=True), "s_mm")
                prev_bc.append(mm_t)
                P.wait("vector", [mm_t])
                tn = P.inc("vector", lambda e, ps=ps, pr=pr, hh=hh, qt=qt:
                           e.scalar_tensor_tensor(
                               Ot[pr][hh * 64 : hh * 64 + 64,
                                      qt * 512 : qt * 512 + 512],
                               Ot[pr][hh * 64 : hh * 64 + 64,
                                      qt * 512 : qt * 512 + 512], 1.0,
                               ps[hh * 64 : hh * 64 + 64, 0:512],
                               ALU.mult, ALU.mult), "s_dve")
                P.pfree[b] = tn
    P.barrier()
    P.ab.release(m_heads)

    atok = [P.ab.alloc(512), P.ab.alloc(512)]
    ztok = [P.af.alloc(320), P.af.alloc(320)]
    ztmp = P.ab.alloc(512)
    zT = [P.ab.alloc(T) for _ in range(3)]
    zstage = [P.ab.alloc(512) for _ in range(3)]
    zst_ev = {}
    at_tick = {}
    zt_tick = {}
    zdma = [None] * 3
    for ti in range(nts_o):
        ts0 = ti * 128
        at = atok[ti % 2]
        evs = []
        for pr in range(4):
            ev = pe_transpose128(P, src_ap=Ot[pr][:, ts0 : ts0 + 128],
                                 src_ticks=[], nrows=128,
                                 dst_ap=at[:, pr * 128 : pr * 128 + 128],
                                 bank=4 + (pr % 2),
                                 dst_war=[at_tick.get(ti % 2)])
            evs.append(ev)
        zt = ztok[ti % 2]
        moi = mo[:, ti * 40 : ti * 40 + 40].rearrange("p (h e) -> p h e", e=5)
        tz = None
        for e_ in range(E):
            mcol = moi[:, :, e_ : e_ + 1].broadcast_to((128, 8, 64))
            P.wait("vector", evs + [mo_ticks[ti], zt_tick.get(ti % 2)])
            tm = P.inc("vector", lambda e, at=at, mcol=mcol:
                       e.tensor_tensor(
                           ztmp[:, 0:512].rearrange("p (h k) -> p h k", k=64),
                           at.rearrange("p (h k) -> p h k", k=64), mcol,
                           ALU.mult), "s_dve")
            P.wait("vector", [tm])
            tz = P.inc("vector", lambda e, zt=zt, e_=e_:
                       e.reduce_sum(zt[:, e_ * 64 : e_ * 64 + 64],
                                    ztmp[:, 0:512]
                                    .rearrange("p (h k) -> p k h", h=8),
                                    axis=AX.X), "s_dve")
            P.wait("vector", [tz])
        zt_tick[ti % 2] = tz
        at_tick[ti % 2] = tz
        for zb in range(3):
            ncol = 128 if zb < 2 else 64
            ev = pe_transpose128(P, src_ap=zt[:, zb * 128 : zb * 128 + ncol],
                                 src_ticks=[tz], nrows=128,
                                 dst_ap=zT[zb][0:ncol, ts0 : ts0 + 128],
                                 bank=4 + (zb % 2), dst_war=[], fp32=True)
        zt_tick[ti % 2] = ev
    P.barrier()

    mko = P.ab.mark()
    def zrhs(c, t0, nt):
        return zT[c][0 : (128 if c < 2 else 64), t0 : t0 + nt], []
    linear_fm(P, W=W["l0_wo2"].ap(), rows=320, Fout=D, rhs=zrhs,
              tokens=tok_chunks(T),
              out=lambda gf, t0, nt: x[gf // 128][:, t0 : t0 + nt],
              stt_resid=lambda gf, t0, nt: x[gf // 128][:, t0 : t0 + nt],
              fblock=256, nparity=2)
    P.barrier()
    P.ab.release(m_attn)
    P.af.release(mf_attn)

    mkt = P.ab.mark()
    xcast = [P.ab.alloc(T) for _ in range(NCH)]
    for c in range(NCH):
        P.inc("vector", lambda e, c=c: e.tensor_scalar_mul(xcast[c][:, 0:T],
                                                           x[c][:, 0:T], 1.0),
              "s_dve")
    P.barrier()
    stt = ln_stats(P, xhat=sbuf_loader(P, xcast), Ttot=T, sq_scratch=sq,
                   want_f32=True)
    topln(P, xadd_f32=lambda c: (x[c], []), xhat_out=lambda c: xb[c],
          st=stt, T=T)
    P.barrier()
    P.ab.release(mkt)
    P.ab.release(m_blk)
    P.af.release(mf_blk)




def build_l1a(P, C):
    cfg = P.cfg
    S, T = cfg.S, cfg.T
    W = C["w"]
    x, xb, sq = C["x"], C["xb"], C["sq"]
    cc_out, cc_tick = C["cc_x2"]
    SP = S + 128
    nkc = SP // 128
    nqt = T // 512
    H = 16
    m_blk = P.ab.mark()
    mf_blk = P.af.mark()
    own = sbuf_loader(P, xb)

    cols = {}
    colticks = []
    for nm in ("L1_attn_norm_g", "L1_attn_norm_b"):
        cols[nm], tk = load_bias_cols(P, W[nm].ap(), D)
        colticks += tk
    qs_col = P.af.alloc(2)
    ks_col = P.af.alloc(2)
    colticks.append(P.dma(qs_col[:, 0:1], P.dram["l1a_qs"].ap()))
    colticks.append(P.dma(ks_col[:, 0:1], P.dram["l1a_ks"].ap()))
    blk2T = P.ab.alloc(16)
    blk2 = P.ab.alloc(128, parts=2)
    P.const("c_blk2T", np.concatenate(
        [np.concatenate([np.ones((64, 1)), np.zeros((64, 1))]),
         np.concatenate([np.zeros((64, 1)), np.ones((64, 1))])], 1), BF16)
    P.const("c_blk2", np.stack([np.concatenate([np.ones(64), np.zeros(64)]),
                                np.concatenate([np.zeros(64), np.ones(64)])]),
            BF16)
    colticks.append(P.dma(blk2T[:, 0:2], P.dram["c_blk2T"].ap()))
    colticks.append(P.dma(blk2, P.dram["c_blk2"].ap()))
    l2lanes = [dict(nsq=P.ab.alloc(512), nrm=P.af.alloc(512, parts=2),
                    bank=6, mm=None, t5=None),
               dict(nsq=P.ab.alloc(512), nrm=P.af.alloc(512, parts=2),
                    bank=7, mm=None, t5=None)]
    l2i = [0]
    P.barrier()

    mrk = P.ab.mark()
    st_o = ln_stats(P, xhat=own, Ttot=T, sq_scratch=sq)
    xn = norm_ring_loader(P, C["ring"], own, st_o,
                          cols["L1_attn_norm_g"], cols["L1_attn_norm_b"],
                          colticks)

    def l2n_slice(stage_ap, fh, t0, nt, tick, sc_col, bank=None):
        L = l2lanes[l2i[0] % 2]
        l2i[0] += 1
        nsq, nrm, bank = L["nsq"], L["nrm"], L["bank"]
        ps = P.pbank(bank)
        P.wait("vector", [tick, L["mm"]])
        t1 = P.inc("vector", lambda e: e.tensor_mul(nsq[:, 0:nt], stage_ap,
                                                    stage_ap), "s_dve")
        P.wait("tensor", [t1, P.pfree[bank]] + colticks)
        t2 = P.inc("tensor", lambda e: e.matmul(ps[0:2, 0:nt], blk2T[:, 0:2],
                                                nsq[:, 0:nt], start=True,
                                                stop=True), "s_mm")
        P.wait("scalar", [t2, L["t5"]])
        t3 = P.inc("scalar", lambda e: e.activation(nrm[0:2, 0:nt],
                                                    ps[0:2, 0:nt], ACTF.Ln),
                   "s_act")
        P.wait("scalar", [t3])
        t4 = P.inc("scalar", lambda e: e.activation(nrm[0:2, 0:nt],
                                                    nrm[0:2, 0:nt], ACTF.Exp,
                                                    scale=-0.5), "s_act")
        P.wait("vector", [t4])
        t5 = P.inc("vector", lambda e: e.tensor_scalar_mul(
            nsq[0:2, 0:nt], nrm[0:2, 0:nt], 1.0), "s_dve")
        P.wait("tensor", [t5, P.pfree[bank]])
        t6 = P.inc("tensor", lambda e: e.matmul(ps[0:128, 0:nt], blk2,
                                                nsq[0:2, 0:nt], start=True,
                                                stop=True), "s_mm")
        L["mm"] = t6
        L["t5"] = t5
        P.wait("vector", [t6])
        t7 = P.inc("vector", lambda e: e.scalar_tensor_tensor(
            stage_ap, stage_ap, sc_col[0:128, 0:1], ps[0:128, 0:nt],
            ALU.mult, ALU.mult), "s_dve")
        P.pfree[bank] = t7
        return t7

    stq = StageOC(P, T)
    stq.dram_fn = lambda gf, fh, t0, nt: C["l1a_q"].ap()[gf // 128, 0:fh,
                                                         t0 : t0 + nt]
    orig_slice = stq.on_slice
    def q_slice(gf, fh, t0, nt, tick):
        i = stq.cur[(gf, t0)]
        t7 = l2n_slice(stq.tiles[i][0:fh, 0:nt], fh, t0, nt, tick, qs_col)
        stq.lastev[i] = t7
        stq.dmatick[i] = P.dma(stq.dram_fn(gf, fh, t0, nt),
                               stq.tiles[i][0:fh, 0:nt], queue="sync",
                               wait=[t7])
    linear_fm(P, W=W["L1_attn_Wq"].ap(), rows=D, Fout=D,
              rhs=xn, tokens=tok_chunks(T),
              out=stq.out, evac_wait=stq.evac_wait, on_slice=q_slice,
              fblock=512, nparity=2)
    P.barrier()
    P.ab.release(mrk)

    mrk = P.ab.mark()
    exA = P.ab.alloc(512)
    exB = P.ab.alloc(512)
    ex_t = [None]
    sel = C["sel"]

    def kvrhs(c, t0, nt):
        if t0 < T:
            return xb[c][:, t0 : t0 + nt], []
        def fill(slot, war):
            tl = t0 - T
            tA = P.dma(exA[:, 0:nt], cc_out.ap()[0, c * 128 : c * 128 + 128,
                                                 tl : tl + nt],
                       queue="sync", wait=war + [cc_tick, ex_t[0]])
            tB = P.dma(exB[:, 0:nt], cc_out.ap()[1, c * 128 : c * 128 + 128,
                                                 tl : tl + nt],
                       queue="sync", wait=[cc_tick, ex_t[0]])
            P.wait("vector", [tA, tB])
            t1 = P.inc("vector", lambda e: e.tensor_scalar_mul(
                slot[:, 0:nt], exA[:, 0:nt], sel[0:128, 1:2]), "s_dve")
            P.wait("vector", [t1])
            t2 = P.inc("vector", lambda e: e.scalar_tensor_tensor(
                slot[:, 0:nt], exB[:, 0:nt], sel[0:128, 0:1], slot[:, 0:nt],
                ALU.mult, ALU.add), "s_dve")
            ex_t[0] = t2
            return [t2]
        ap, tk = C["ring"].get(("l1kv", c, t0, nt), fill)
        return ap[:, 0:nt], tk

    stk = StageOC(P, S)
    def kv_dst(gf, fh, t0, nt):
        h = gf // 128
        if h < 8:
            return C["l1a_k"].ap()[h, 0:fh, t0 : t0 + nt]
        return C["l1a_vT"].ap()[h - 8, 0:fh, t0 : t0 + nt]
    stk.dram_fn = kv_dst
    def kv_slice(gf, fh, t0, nt, tick):
        i = stk.cur[(gf, t0)]
        if gf < D:
            t7 = l2n_slice(stk.tiles[i][0:fh, 0:nt], fh, t0, nt, tick, ks_col)
            stk.lastev[i] = t7
            stk.dmatick[i] = P.dma(stk.dram_fn(gf, fh, t0, nt),
                                   stk.tiles[i][0:fh, 0:nt], queue="sync",
                                   wait=[t7])
        else:
            stk.on_slice(gf, fh, t0, nt, tick)
    linear_fm(P, W=W["L1_attn_Wkv"].ap(), rows=D, Fout=2 * D, rhs=kvrhs,
              tokens=tok_chunks(S), out=stk.out, evac_wait=stk.evac_wait,
              on_slice=kv_slice, fblock=256, nparity=2)
    P.barrier()

    nkt = P.ab.alloc(16)
    nvt = P.ab.alloc(16)
    od1 = od2 = None
    for pr in range(8):
        tk1 = P.dma(nkt[:, 0:2], P.dram["l1a_nk"].ap()
                    [pr * 128 : pr * 128 + 128, :], queue="gpsimd",
                    wait=[P.tick("s_dve"), P.tick("s_mm"), od1])
        tk2 = P.dma(nvt[:, 0:2], P.dram["l1a_nv"].ap()
                    [pr * 128 : pr * 128 + 128, :], queue="gpsimd", wait=[od2])
        t7 = l2n_slice(nkt[:, 0:2], 128, 0, 2, tk1, ks_col, bank=6)
        od1 = P.dma(C["l1a_k"].ap()[pr, :, S : S + 2], nkt[:, 0:2],
                    queue="sync", wait=[t7])
        od2 = P.dma(C["l1a_vT"].ap()[pr, :, S : S + 2], nvt[:, 0:2],
                    queue="sync", wait=[tk2])
    P.barrier()
    P.ab.release(mrk)

    m_attn = P.ab.mark()
    mf_attn = P.af.mark()
    sums = P.af.alloc(2 * 512, parts=97)
    recb = P.ab.alloc(2 * 512, parts=97)
    dmask = P.af.alloc(4 * 512)
    for j in range(4):
        dmt = P.dma(dmask[:, j * 512 : j * 512 + 512],
                    P.dram["c_l1a_diag"].ap()[j], queue="sync")
    bias_t = P.af.alloc(H * nkc)
    for h_ in range(H):
        bt_t = P.dma(bias_t[:, h_ * nkc : h_ * nkc + nkc],
                     P.dram["l1a_bias"].ap()[h_], queue="sync")
    P.barrier()
    OtT = P.ab.alloc(T)
    Kt = P.ab.alloc(SP)
    vst = P.ab.alloc(8 * 128)
    Vtok = P.ab.alloc(nkc * 65)
    Qt = P.ab.alloc(T)
    ptile = P.ab.alloc(6 * 512)
    P.inc("vector", lambda e: e.memset(
        Vtok[:, 0 : nkc * 65].rearrange("p (k v) -> p k v", v=65)
        [:, :, 64:65], 1.0), "s_dve")
    P.barrier()
    tq_aug = P.dma(Qt[64:67, :], P.dram["l1a_qaug"].ap(), queue="sync")
    tq_one = tq_aug
    tk_r66 = P.dma(Kt[66:67, :], P.dram["l1a_row66"].ap(), queue="sync")
    vst_war = None
    prev_bc = []
    tr = None
    o_dma = None
    hq_mm = None
    for h in range(H):
        pr, hh = h // 2, h % 2
        prev = [hq_mm] if hq_mm else [P.tick("s_mm"), P.tick("s_dve"),
                                      P.tick("s_act")]
        tk_k = P.dma(Kt[0:64, 0 : S + 2],
                     C["l1a_k"].ap()[pr, hh * 64 : hh * 64 + 64, :],
                     queue="sync", wait=prev)
        tk_ka = P.dma(Kt[64:66, :], P.dram["c_l1a_kaug"].ap()[h], queue="sync",
                      wait=prev)
        tk_q = P.dma(Qt[0:64, :], C["l1a_q"].ap()[pr, hh * 64 : hh * 64 + 64, :],
                     queue="sync", wait=prev)
        tpad = P.inc("vector", lambda e: e.memset(Kt[0:64, S + 2 : SP], 0.0),
                     "s_dve")
        vticks = []
        for g0 in range(0, nkc, 8):
            gn = min(8, nkc - g0)
            dts = []
            for j in range(gn):
                k0 = (g0 + j) * 128
                ncl = max(0, min(128, S + 2 - k0))
                if ncl > 0:
                    dts.append(P.dma(vst[:, j * 128 : j * 128 + ncl],
                                     C["l1a_vT"].ap()[pr, :, k0 : k0 + ncl],
                                     queue="gpsimd", wait=[vst_war] + prev))
                if ncl < 128:
                    P.wait("vector", [vst_war] + prev)
                    dts.append(P.inc("vector", lambda e, j=j, ncl=ncl:
                                     e.memset(vst[:, j * 128 + ncl :
                                                  j * 128 + 128], 0.0),
                                     "s_dve"))
            b = (g0 // 8) % 6
            ps = P.pbank(b)
            P.wait("tensor", [P.pfree[b]] + dts + P.shared["tick"])
            def tmm(e, ps=ps, off=hh * 64, gn=gn):
                last = None
                for j in range(gn):
                    last = e.matmul(
                        ps[0:128, j * 64 : j * 64 + 64],
                        vst[off : off + 64, j * 128 : j * 128 + 128],
                        P.shared["ident"][off : off + 64, off : off + 64],
                        start=True, stop=True)
                return last
            mm_t = P.inc("tensor", tmm, "s_mm")
            vst_war = mm_t
            P.wait("vector", [mm_t])
            def ev(e, ps=ps, g0=g0, gn=gn):
                last = None
                for j in range(gn):
                    last = e.tensor_copy(
                        Vtok[:, (g0 + j) * 65 : (g0 + j) * 65 + 64],
                        ps[0:128, j * 64 : j * 64 + 64])
                return last
            ev_t = P.inc("vector", ev, "s_dve")
            P.pfree[b] = ev_t
            vticks.append(ev_t)
        loadt = [tk_k, tk_ka, tk_q, tq_aug, tq_one, tk_r66, tpad, bt_t, dmt]
        P.wait("vector", [tr])
        ts_ms = P.inc("vector", lambda e: e.memset(sums[0:97, :], 1.0),
                      "s_dve")
        qticks = [ts_ms]
        for qt in range(nqt):
            q0 = qt * 512
            chunks = []
            nown = q0 // 128 + 4
            for j in range(nown):
                ch = dict(kc=j * 128, nk=128, vi=j, qc0=0, nqw=512,
                          biascol=bias_t[:, h * nkc + j : h * nkc + j + 1])
                jj = j - q0 // 128
                if jj >= 0:
                    ch["maskpre"] = dmask[:, jj * 512 : jj * 512 + 512]
                    ch["mq0"] = 0
                    ch["mqw"] = 512
                chunks.append(ch)
            for j in range(T // 128, nkc):
                chunks.append(dict(kc=j * 128, nk=128, vi=j, qc0=0, nqw=512,
                                   biascol=bias_t[:, h * nkc + j :
                                                  h * nkc + j + 1]))
            sr, sc = 32 * (qt % 2), 512 * (qt // 2)
            if o_dma is not None:
                P.wait("vector", [o_dma, ts_ms])
            ts = attn_qtile(P, Kt=Kt, Kticks=loadt, Qt=Qt, Qticks=loadt,
                            Vt=Vtok, Vticks=vticks + [ts_ms], dh=64, daug=67,
                            q0=q0, nq=512,
                            chunks=chunks, scale=8.0, octx=6, vsum="col",
                            onescol_b=P.shared["onescol"],
                            sums_dst=sums[sr : sr + 1, sc : sc + 512],
                            O_dst=OtT[hh * 64 : hh * 64 + 64,
                                      qt * 512 : qt * 512 + 512],
                            ptile=ptile)
            qticks += ts
        hq_mm = P.tick("s_mm")
        P.wait("scalar", qticks + prev_bc)
        tl = P.inc("scalar", lambda e:
                   e.activation(sums[0:97, :], sums[0:97, :], ACTF.Ln),
                   "s_act")
        P.wait("scalar", [tl])
        tr = P.inc("scalar", lambda e:
                   e.activation(recb[0:97, :], sums[0:97, :],
                                ACTF.Exp, scale=-1.0), "s_act")
        prev_bc = []
        stt_ts = []
        for qt in range(nqt):
            sr, sc = 32 * (qt % 2), 512 * (qt // 2)
            b = 6 + qt % 2
            ps = P.pbank(b)
            P.wait("tensor", [P.pfree[b], tr] + P.shared["tick"])
            mm_t = P.inc("tensor", lambda e, ps=ps, sr=sr, sc=sc, hh=hh:
                         e.matmul(ps[hh * 64 : hh * 64 + 64, 0:512],
                                  P.shared["ones"][sr : sr + 1,
                                                   hh * 64 : hh * 64 + 64],
                                  recb[sr : sr + 1, sc : sc + 512],
                                  start=True, stop=True), "s_mm")
            prev_bc.append(mm_t)
            P.wait("vector", [mm_t])
            tn = P.inc("vector", lambda e, ps=ps, hh=hh, qt=qt:
                       e.scalar_tensor_tensor(
                           OtT[hh * 64 : hh * 64 + 64,
                               qt * 512 : qt * 512 + 512],
                           OtT[hh * 64 : hh * 64 + 64,
                               qt * 512 : qt * 512 + 512], 1.0,
                           ps[hh * 64 : hh * 64 + 64, 0:512],
                           ALU.mult, ALU.mult), "s_dve")
            P.pfree[b] = tn
            stt_ts.append(tn)
        if hh == 1:
            o_dma = P.dma(C["l1a_q"].ap()[pr], OtT[:, 0:T], queue="sync",
                          wait=stt_ts + [P.tick("s_dve")])
    P.wait("sync", [o_dma])
    P.barrier()

    def ctxrhs(c, t0, nt):
        def fill(slot, war, c=c, t0=t0, nt=nt):
            t = P.dma(slot[:, 0:nt],
                      C["l1a_q"].ap()[c, :, t0 : t0 + nt],
                      queue="sync", wait=war)
            return [t]
        ap, tk = C["ring"].get(("l1ctx", c, t0, nt), fill)
        return ap[:, 0:nt], tk
    linear_fm(P, W=W["L1_attn_Wout"].ap(), rows=D, Fout=D,
              rhs=ctxrhs, tokens=tok_chunks(T),
              out=lambda gf, t0, nt: x[gf // 128][:, t0 : t0 + nt],
              stt_resid=lambda gf, t0, nt: x[gf // 128][:, t0 : t0 + nt],
              fblock=256, nparity=2)
    P.barrier()
    P.ab.release(m_attn)
    P.af.release(mf_attn)

    mkt = P.ab.mark()
    for c in range(NCH):
        P.inc("vector", lambda e, c=c: e.tensor_scalar_mul(xb[c][:, 0:T],
                                                           x[c][:, 0:T], 1.0),
              "s_dve")
    P.barrier()
    stt = ln_stats(P, xhat=sbuf_loader(P, xb), Ttot=T, sq_scratch=sq,
                   want_f32=True)
    topln(P, xadd_f32=lambda c: (x[c], []), xhat_out=lambda c: xb[c],
          st=stt, T=T)
    P.barrier()
    P.ab.release(mkt)
    P.ab.release(m_blk)
    P.af.release(mf_blk)



def build_l1f(P, C):
    cfg = P.cfg
    S, T = cfg.S, cfg.T
    W = C["w"]
    x, xb, sq = C["x"], C["xb"], C["sq"]
    sel = C["sel"]
    KV = T + 512
    nw = T // 512
    m_blk = P.ab.mark()
    mf_blk = P.af.mark()

    for c in range(NCH):
        P.dma(C["l1f_x3"].ap()[c * 128 : c * 128 + 128, :], x[c][:, 0:T],
              queue="sync", wait=[P.tick("s_dve"), P.tick("s_act")])
    P.barrier()

    qs_col = [P.af.alloc(2) for _ in range(2)]
    ks_col = [P.af.alloc(2) for _ in range(2)]
    colticks = []
    for l in range(2):
        colticks.append(P.dma(qs_col[l][:, 0:1], P.dram["l1f_qs"].ap()[l]))
        colticks.append(P.dma(ks_col[l][:, 0:1], P.dram["l1f_ks"].ap()[l]))
    mtri_d = P.af.alloc(128)
    mtri_p = P.af.alloc(128)
    colticks.append(P.dma(mtri_d, P.dram["c_l1f_diag"].ap()))
    colticks.append(P.dma(mtri_p, P.dram["c_l1f_prev"].ap()))
    mprev0 = P.af.alloc(4 * 512)
    for j in range(4):
        colticks.append(P.dma(mprev0[:, j * 512 : j * 512 + 512],
                              P.dram["l1f_prev0"].ap()[j], queue="sync"))
    def rtab(nm, t0, nt):
        def fill(slot, war):
            t = P.dma(slot[:, 0:nt], P.dram[nm].ap()[:, t0 : t0 + nt],
                      queue="sync", wait=war)
            return [t]
        ap, tk = C["ring"].get((nm, t0), fill)
        return ap[:, 0:nt], tk
    l2lanes = [dict(nsq=P.ab.alloc(512), nrm=P.af.alloc(512, parts=2),
                    rsc=P.ab.alloc(512), rs2=P.ab.alloc(512),
                    bank=6, mm=None, t5=None)
               for _ in range(2)]
    l2lanes[1]["bank"] = 7
    l2i = [0]
    blk2T = P.ab.alloc(16)
    blk2 = P.ab.alloc(128, parts=2)
    colticks.append(P.dma(blk2T[:, 0:2], P.dram["c_blk2T"].ap()))
    colticks.append(P.dma(blk2, P.dram["c_blk2"].ap()))
    lng = {}
    for l in range(2):
        lng[(l, "g")], tk = load_bias_cols(P, W["L1_ffn_ln_g"].ap()[l], D)
        colticks += tk
        lng[(l, "b")], tk = load_bias_cols(P, W["L1_ffn_ln_b"].ap()[l], D)
        colticks += tk
        lng[(l, "b1")], tk = load_bias_cols(P, W["l1f_b1p"].ap()[l], 5632)
        colticks += tk
    b2row = {}
    for l in range(2):
        b2row[l], tk = load_bias_row(P, W["L1_ffn_b2"].ap()[l], D)
        colticks += tk
    P.barrier()

    def l2n_rot(stage_ap, nt, tick, sc, cs_sl, ss_sl, tabticks, bank=None):
        L = l2lanes[l2i[0] % 2]
        l2i[0] += 1
        nsq, nrm, rsc, rs2, bank = (L["nsq"], L["nrm"], L["rsc"], L["rs2"],
                                    L["bank"])
        ps = P.pbank(bank)
        P.wait("vector", [tick, L["mm"]])
        t1 = P.inc("vector", lambda e: e.tensor_mul(nsq[:, 0:nt], stage_ap,
                                                    stage_ap), "s_dve")
        P.wait("tensor", [t1, P.pfree[bank]] + colticks)
        t2 = P.inc("tensor", lambda e: e.matmul(ps[0:2, 0:nt], blk2T[:, 0:2],
                                                nsq[:, 0:nt], start=True,
                                                stop=True), "s_mm")
        P.wait("scalar", [t2, L["t5"]])
        t3 = P.inc("scalar", lambda e: e.activation(nrm[0:2, 0:nt],
                                                    ps[0:2, 0:nt], ACTF.Ln),
                   "s_act")
        P.wait("scalar", [t3])
        t4 = P.inc("scalar", lambda e: e.activation(nrm[0:2, 0:nt],
                                                    nrm[0:2, 0:nt], ACTF.Exp,
                                                    scale=-0.5), "s_act")
        P.wait("vector", [t4])
        t5 = P.inc("vector", lambda e: e.tensor_scalar_mul(
            nsq[0:2, 0:nt], nrm[0:2, 0:nt], 1.0), "s_dve")
        P.wait("tensor", [t5, P.pfree[bank]])
        t6 = P.inc("tensor", lambda e: e.matmul(ps[0:128, 0:nt], blk2,
                                                nsq[0:2, 0:nt], start=True,
                                                stop=True), "s_mm")
        L["mm"] = t6
        L["t5"] = t5
        P.wait("vector", [t6])
        t7 = P.inc("vector", lambda e: e.scalar_tensor_tensor(
            stage_ap, stage_ap, sc[0:128, 0:1], ps[0:128, 0:nt],
            ALU.mult, ALU.mult), "s_dve")
        P.pfree[bank] = t7
        P.wait("vector", [t7])
        tprev = t7
        for bb in range(4):
            sgn = -1.0 if bb % 2 == 0 else 1.0
            srcb = (bb + 1) if bb % 2 == 0 else (bb - 1)
            tprev = P.inc("vector", lambda e, bb=bb, srcb=srcb, sgn=sgn:
                          e.tensor_scalar_mul(
                              rsc[bb * 32 : bb * 32 + 32, 0:nt],
                              stage_ap[srcb * 32 : srcb * 32 + 32, 0:nt], sgn),
                          "s_dve")
        P.wait("vector", [tprev] + tabticks)
        ta = P.inc("vector", lambda e: e.tensor_mul(rs2[:, 0:nt], stage_ap,
                                                    cs_sl), "s_dve")
        P.wait("vector", [ta])
        tb = P.inc("vector", lambda e: e.tensor_mul(rsc[:, 0:nt], rsc[:, 0:nt],
                                                    ss_sl), "s_dve")
        P.wait("vector", [tb])
        tc = P.inc("vector", lambda e: e.tensor_add(stage_ap, rs2[:, 0:nt],
                                                    rsc[:, 0:nt]), "s_dve")
        return tc

    for l in range(2):
        ccl_out, ccl_tick = (C["cc_x3"] if l == 0 else C["cc_x4"])
        mlay = P.ab.mark()
        mflay = P.af.mark()
        halo = [P.ab.alloc(514) for _ in range(NCH)]
        hticks = {}
        hl0 = max(0, T - 514)
        hlen = T - hl0
        for c in range(NCH):
            tks = [P.dma(halo[c][:, 514 - hlen : 514],
                         ccl_out.ap()[0, c * 128 : c * 128 + 128, hl0:T],
                         queue="sync", wait=[ccl_tick])]
            if hlen < 514:
                tks.append(P.inc("vector", lambda e, c=c: e.memset(
                    halo[c][:, 0 : 514 - hlen], 0.0), "s_dve"))
            hticks[c] = tks
        shf0 = [P.ab.alloc(512) for _ in range(NCH)]
        for c in range(NCH):
            P.wait("vector", hticks[c] + [P.tick("s_mm"), P.tick("s_act")])
            tb0 = P.inc("vector", lambda e, c=c: e.tensor_scalar_mul(
                shf0[c][:, 0:1], halo[c][:, 513:514], sel[0:128, 1:2]), "s_dve")
            tb1 = P.inc("vector", lambda e, c=c: e.tensor_scalar_mul(
                shf0[c][:, 1:512], xb[c][:, 0:511], 1.0), "s_dve")
            hticks[c] += [tb0, tb1]

        def qrhs(c, t0, nt):
            if c < 4:
                return xb[c][:, t0 : t0 + nt], []
            if t0 == 0:
                return shf0[c][:, 0:nt], hticks[c]
            return xb[c][:, t0 - 1 : t0 - 1 + nt], []

        def kvrhs(c, t0, nt):
            if t0 < 512:
                assert nt <= 512
                off = (2 + t0) if c < 4 else (1 + t0)
                return halo[c][:, off : off + nt], hticks[c]
            tl = t0 - 512
            if c < 4:
                return xb[c][:, tl : tl + nt], []
            if tl == 0:
                return shf0[c][:, 0:nt], hticks[c]
            return xb[c][:, tl - 1 : tl - 1 + nt], []

        wq_ap = W["L1_ffn_wqkv"].ap()[l]
        mkq_ = P.ab.mark()
        stq = StageOC(P, T)
        stq.dram_fn = lambda gf, fh, t0, nt: C["l1f_q"].ap()[gf // 128, 0:fh,
                                                             t0 : t0 + nt]
        def q_slice(gf, fh, t0, nt, tick):
            i = stq.cur[(gf, t0)]
            csl, tk1 = rtab("l1f_qcs", t0, nt)
            ssl, tk2 = rtab("l1f_qss", t0, nt)
            tc_ = l2n_rot(stq.tiles[i][0:fh, 0:nt], nt, tick, qs_col[l],
                          csl, ssl, tk1 + tk2)
            stq.lastev[i] = tc_
            stq.dmatick[i] = P.dma(stq.dram_fn(gf, fh, t0, nt),
                                   stq.tiles[i][0:fh, 0:nt], queue="sync",
                                   wait=[tc_])
        linear_fm(P, W=wq_ap[:, 0:512], rows=D, Fout=512, rhs=qrhs,
                  tokens=tok_chunks(T), out=stq.out, evac_wait=stq.evac_wait,
                  on_slice=q_slice, fblock=256, nparity=2)
        P.barrier()
        P.ab.release(mkq_)

        mkv_ = P.ab.mark()
        stk = StageOC(P, KV)
        def kv_dst(gf, fh, t0, nt):
            h = gf // 128
            if h < 4:
                return C["l1f_k"].ap()[h, 0:fh, t0 : t0 + nt]
            return C["l1f_vT"].ap()[h - 4, 0:fh, t0 : t0 + nt]
        stk.dram_fn = kv_dst
        def kv_slice(gf, fh, t0, nt, tick):
            i = stk.cur[(gf, t0)]
            if gf < 512:
                csl, tk1 = rtab("l1f_kcs", t0, nt)
                ssl, tk2 = rtab("l1f_kss", t0, nt)
                tc_ = l2n_rot(stk.tiles[i][0:fh, 0:nt], nt, tick, ks_col[l],
                              csl, ssl, tk1 + tk2)
                stk.lastev[i] = tc_
                stk.dmatick[i] = P.dma(stk.dram_fn(gf, fh, t0, nt),
                                       stk.tiles[i][0:fh, 0:nt], queue="sync",
                                       wait=[tc_])
            else:
                stk.on_slice(gf, fh, t0, nt, tick)
        linear_fm(P, W=wq_ap[:, 512:1536], rows=D, Fout=1024, rhs=kvrhs,
                  tokens=tok_chunks(KV), out=stk.out, evac_wait=stk.evac_wait,
                  on_slice=kv_slice, fblock=256, nparity=2)
        P.barrier()
        P.ab.release(mlay)

        m_attn = P.ab.mark()
        mf_attn = P.af.mark()
        sums = P.af.alloc(2 * 512, parts=97)
        recb = P.ab.alloc(2 * 512, parts=97)
        m_heads = P.ab.mark()
        nkv = KV // 128
        OtT = P.ab.alloc(T)
        Kt = P.ab.alloc(KV)
        vst = P.ab.alloc(8 * 128)
        Vtok = P.ab.alloc(nkv * 65)
        Qt = P.ab.alloc(T)
        ptile = P.ab.alloc(6 * 512)
        P.inc("vector", lambda e: e.memset(
            Vtok[:, 0 : nkv * 65].rearrange("p (k v) -> p k v", v=65)
            [:, :, 64:65], 1.0), "s_dve")
        P.barrier()
        vst_war = None
        prev_bc = []
        tr = None
        o_dma = None
        hq_mm = None
        for pr in range(4):
            prev = [hq_mm] if hq_mm else [P.tick("s_mm"), P.tick("s_dve"),
                                          P.tick("s_act")]
            tk_k = P.dma(Kt, C["l1f_k"].ap()[pr], queue="sync", wait=prev)
            tk_q = P.dma(Qt, C["l1f_q"].ap()[pr], queue="sync", wait=prev)
            stt_pair = []
            for hh in range(2):
                off = hh * 64
                h = pr * 2 + hh
                vprev = [hq_mm] if hq_mm else [P.tick("s_mm")]
                vticks = []
                for g0 in range(0, nkv, 8):
                    gn = min(8, nkv - g0)
                    dts = [P.dma(vst[:, j * 128 : j * 128 + 128],
                                 C["l1f_vT"].ap()[pr, :, (g0 + j) * 128 :
                                                  (g0 + j) * 128 + 128],
                                 queue="gpsimd", wait=[vst_war] + vprev)
                           for j in range(gn)]
                    b = (g0 // 8) % 6
                    ps = P.pbank(b)
                    P.wait("tensor", [P.pfree[b]] + dts + P.shared["tick"])
                    def tmm(e, ps=ps, off=off, gn=gn):
                        last = None
                        for j in range(gn):
                            last = e.matmul(
                                ps[0:128, j * 64 : j * 64 + 64],
                                vst[off : off + 64, j * 128 : j * 128 + 128],
                                P.shared["ident"][off : off + 64,
                                                  off : off + 64],
                                start=True, stop=True)
                        return last
                    mm_t = P.inc("tensor", tmm, "s_mm")
                    vst_war = mm_t
                    P.wait("vector", [mm_t])
                    def ev(e, ps=ps, g0=g0, gn=gn):
                        last = None
                        for j in range(gn):
                            last = e.tensor_copy(
                                Vtok[:, (g0 + j) * 65 : (g0 + j) * 65 + 64],
                                ps[0:128, j * 64 : j * 64 + 64])
                        return last
                    ev_t = P.inc("vector", ev, "s_dve")
                    P.pfree[b] = ev_t
                    vticks.append(ev_t)
                P.wait("vector", [tr])
                ts_ms = P.inc("vector", lambda e: e.memset(sums[0:97, :], 1.0),
                              "s_dve")
                qticks = [ts_ms]
                for w in range(nw):
                    q0 = w * 512
                    chunks = []
                    for j in range(8):
                        kc = q0 + j * 128
                        vi = kc // 128
                        if j < 4:
                            nqw = j * 128 + 128
                            ch = dict(kc=kc, nk=128, vi=vi, qc0=0, nqw=nqw,
                                      pzero=(nqw, 512 - nqw) if nqw < 512
                                      else None)
                            if w == 0:
                                ch["maskpre"] = mprev0[:, j * 512 :
                                                       j * 512 + nqw]
                                ch["mq0"] = 0
                                ch["mqw"] = nqw
                            else:
                                ch["maskpre"] = mtri_p[:, 0:128]
                                ch["mq0"] = j * 128
                                ch["mqw"] = 128
                        else:
                            qc0 = (j - 4) * 128
                            nqw = 512 - qc0
                            ch = dict(kc=kc, nk=128, vi=vi, qc0=qc0, nqw=nqw,
                                      pzero=(0, qc0) if qc0 > 0 else None,
                                      maskpre=mtri_d[:, 0:128], mq0=qc0,
                                      mqw=128)
                        chunks.append(ch)
                    sr, sc = 32 * (w % 2), 512 * (w // 2)
                    if o_dma is not None:
                        P.wait("vector", [o_dma, ts_ms])
                    ts = attn_qtile(P, Kt=Kt[off : off + 64, :],
                                    Kticks=[tk_k],
                                    Qt=Qt[off : off + 64, :], Qticks=[tk_q],
                                    Vt=Vtok, Vticks=vticks + [ts_ms],
                                    dh=64, daug=64,
                                    q0=q0, nq=512, chunks=chunks, scale=8.0,
                                    octx=6, vsum="col",
                                    onescol_b=P.shared["onescol"],
                                    sums_dst=sums[sr : sr + 1, sc : sc + 512],
                                    O_dst=OtT[off : off + 64, q0 : q0 + 512],
                                    ptile=ptile)
                    qticks += ts
                hq_mm = P.tick("s_mm")
                P.wait("scalar", qticks + prev_bc)
                tl = P.inc("scalar", lambda e:
                           e.activation(sums[0:97, :], sums[0:97, :],
                                        ACTF.Ln), "s_act")
                P.wait("scalar", [tl])
                tr = P.inc("scalar", lambda e:
                           e.activation(recb[0:97, :], sums[0:97, :],
                                        ACTF.Exp, scale=-1.0), "s_act")
                prev_bc = []
                for w in range(nw):
                    sr, sc = 32 * (w % 2), 512 * (w // 2)
                    b = 6 + w % 2
                    ps = P.pbank(b)
                    P.wait("tensor", [P.pfree[b], tr] + P.shared["tick"])
                    mm_t = P.inc("tensor", lambda e, ps=ps, sr=sr, sc=sc,
                                 hh=hh:
                                 e.matmul(ps[hh * 64 : hh * 64 + 64, 0:512],
                                          P.shared["ones"][sr : sr + 1,
                                                           hh * 64 :
                                                           hh * 64 + 64],
                                          recb[sr : sr + 1, sc : sc + 512],
                                          start=True, stop=True), "s_mm")
                    prev_bc.append(mm_t)
                    P.wait("vector", [mm_t])
                    tn = P.inc("vector", lambda e, ps=ps, hh=hh, w=w:
                               e.scalar_tensor_tensor(
                                   OtT[hh * 64 : hh * 64 + 64,
                                       w * 512 : w * 512 + 512],
                                   OtT[hh * 64 : hh * 64 + 64,
                                       w * 512 : w * 512 + 512], 1.0,
                                   ps[hh * 64 : hh * 64 + 64, 0:512],
                                   ALU.mult, ALU.mult), "s_dve")
                    P.pfree[b] = tn
                    stt_pair.append(tn)
            o_dma = P.dma(C["l1f_q"].ap()[pr], OtT[:, 0:T], queue="sync",
                          wait=stt_pair)
        P.wait("sync", [o_dma])
        P.barrier()
        P.ab.release(m_heads)

        mko = P.ab.mark()
        def ctxrhs(c, t0, nt, l=l):
            def fill(slot, war, c=c, t0=t0, nt=nt):
                t = P.dma(slot[:, 0:nt],
                          C["l1f_q"].ap()[c, :, t0 : t0 + nt],
                          queue="sync", wait=war)
                return [t]
            ap, tk = C["ring"].get(("l1fctx", l, c, t0, nt), fill)
            return ap[:, 0:nt], tk
        linear_fm(P, W=W["L1_ffn_wo"].ap()[l], rows=512, Fout=D,
                  rhs=ctxrhs, tokens=tok_chunks(T),
                  out=lambda gf, t0, nt: x[gf // 128][:, t0 : t0 + nt],
                  stt_resid=lambda gf, t0, nt: x[gf // 128][:, t0 : t0 + nt],
                  fblock=256, nparity=2)
        P.barrier()
        P.ab.release(m_attn)
        P.af.release(mf_attn)

        for c in range(NCH):
            P.inc("vector", lambda e, c=c: e.tensor_scalar_mul(
                xb[c][:, 0:T], x[c][:, 0:T], 1.0), "s_dve")
        P.barrier()
        tcc = allgather(P, tiles=[xb[c][:, T - 1 : T] for c in range(NCH)],
                        ticks=[P.tick("s_dve")], ccin=C["ccc_in"],
                        ccout=C["ccc_out"])
        bnd = P.ab.alloc(16)
        tbd = P.dma(bnd[:, 0:8], C["ccc_out"].ap()[0].rearrange(
            "(c p) o -> p (c o)", p=128), queue="sync", wait=[tcc], slow=True)
        P.wait("vector", [tbd])
        tbs = P.inc("vector", lambda e: e.tensor_scalar_mul(
            bnd[:, 0:8], bnd[:, 0:8], sel[0:128, 1:2]), "s_dve")

        def h2base(c, t0, nt):
            if c < 4:
                return xb[c][:, t0 : t0 + nt], []
            if t0 == 0:
                return ffn0[c][:, 0:nt], f0t[c]
            return xb[c][:, t0 - 1 : t0 - 1 + nt], []
        ffn0 = [P.ab.alloc(512) for _ in range(NCH)]
        f0t = {}
        for c in range(NCH):
            P.wait("vector", [tbs, P.tick("s_mm"), P.tick("s_act")])
            ta_ = P.inc("vector", lambda e, c=c: e.tensor_scalar_mul(
                ffn0[c][:, 0:1], bnd[:, c : c + 1], 1.0), "s_dve")
            tb_ = P.inc("vector", lambda e, c=c: e.tensor_scalar_mul(
                ffn0[c][:, 1:512], xb[c][:, 0:511], 1.0), "s_dve")
            f0t[c] = [ta_, tb_]
        st_f = ln_stats(P, xhat=h2base, Ttot=T, sq_scratch=sq)
        h2n = norm_ring_loader(P, C["ring"], h2base, st_f, lng[(l, "g")],
                               lng[(l, "b")], colticks)
        mkw = P.ab.mark()
        apair = [P.ab.alloc(512), P.ab.alloc(512)]
        at_t = {}
        stm_dma = {}
        def w1out(gf, t0, nt):
            blk = (gf // 128) % 2
            return apair[blk][:, 0:nt]
        def w1wait(gf, t0, nt):
            return [at_t.get((gf // 128) % 2), stm_dma.get(0)]
        def w1slice(gf, fh, t0, nt, tick):
            blk = (gf // 128) % 2
            at_t[blk] = tick
            if blk == 1:
                pairi = (gf // 256)
                P.wait("vector", [at_t.get(0), tick, stm_dma.get(0)])
                tm = P.inc("vector", lambda e, nt=nt:
                           e.tensor_mul(apair[1][:, 0:nt], apair[0][:, 0:nt],
                                        apair[1][:, 0:nt]), "s_dve")
                stm_dma[0] = P.dma(
                    C["l1f_mid"].ap()[pairi * 128 : pairi * 128 + 128,
                                      t0 : t0 + nt],
                    apair[1][:, 0:nt], queue="sync", wait=[tm])
        linear_fm(P, W=W["l1f_w1p"].ap()[l], rows=D, Fout=5632, rhs=h2n,
                  tokens=tok_chunks(T), out=w1out, evac_wait=w1wait,
                  on_slice=w1slice, bias=(lng[(l, "b1")], 0), fblock=512,
                  nparity=2,
                  act=lambda gf: ACTF.Gelu if (gf // 128) % 2 == 1
                  else ACTF.Identity)
        P.barrier()
        P.ab.release(mkw)

        mkw2 = P.ab.mark()
        w2slab = [[P.ab.alloc(512) for _ in range(8)] for _ in range(2)]
        halfac = [P.af.alloc(512) for _ in range(NCH)]
        groups = [list(range(0, 8)), list(range(8, 16)), list(range(16, 22))]
        prev_w2 = {}
        ev_h = [None, None]
        pidx = 0
        for ti2, (t0, nt) in enumerate(tok_chunks(T)):
            for gi, grp in enumerate(groups):
                rsl = []
                for sidx, mc in enumerate(grp):
                    p = min(128, 2730 - mc * 128)
                    def fill(slot, war, mc=mc, t0=t0, nt=nt, p=p):
                        t = P.dma(slot[0:p, 0:nt],
                                  C["l1f_mid"].ap()[mc * 128 : mc * 128 + p,
                                                    t0 : t0 + nt],
                                  queue="sync", wait=war)
                        return [t]
                    rsl.append((C["ring"].get(("l1fmid", l, ti2, mc), fill), p))
                for half in range(2):
                    par = pidx % 2
                    pidx += 1
                    slabs = w2slab[par]
                    bks = (0, 1, 2, 3) if half == 0 else (4, 5, 6, 7)
                    sticks = []
                    for sidx, mc in enumerate(grp):
                        p = min(128, 2730 - mc * 128)
                        t = P.dma(slabs[sidx][0:p, :],
                                  W["L1_ffn_w2"].ap()[l][mc * 128 :
                                                         mc * 128 + p,
                                                         half * 512 :
                                                         half * 512 + 512],
                                  queue="gpsimd", wait=[prev_w2.get(par)])
                        sticks.append(t)
                    P.wait("tensor", [P.pfree[bk] for bk in bks] + sticks
                           + [t for ((_, tk), _) in rsl for t in tk])
                    def mmg(e, rsl=rsl, gi=gi, nt=nt, grp=grp, slabs=slabs,
                            bks=bks, half=half):
                        last = None
                        for ol in range(4):
                            oc = half * 4 + ol
                            for sidx in range(len(grp)):
                                (ap, tk), p = rsl[sidx]
                                last = e.matmul(
                                    P.pbank(bks[ol])[0:128, 0:nt],
                                    slabs[sidx][0:p,
                                                ol * 128 : ol * 128 + 128],
                                    ap[0:p, 0:nt],
                                    start=(sidx == 0),
                                    stop=(gi < 2 and sidx == len(grp) - 1))
                        if gi == 2:
                            for ol in range(4):
                                oc = half * 4 + ol
                                last = e.matmul(
                                    P.pbank(bks[ol])[0:128, 0:nt],
                                    b2row[l][0:1, oc * 128 : oc * 128 + 128],
                                    P.shared["onesrow"][0:1, 0:nt],
                                    start=False, stop=True)
                        return last
                    mm_t = P.inc("tensor", mmg, "s_mm")
                    prev_w2[par] = mm_t
                    P.wait("vector", [mm_t, ev_h[half]])
                    if gi == 0:
                        def ev0(e, nt=nt, half=half, bks=bks):
                            last = None
                            for ol in range(4):
                                last = e.tensor_copy(
                                    halfac[half * 4 + ol][:, 0:nt],
                                    P.pbank(bks[ol])[0:128, 0:nt])
                            return last
                        ev_t = P.inc("vector", ev0, "s_dve")
                    else:
                        def ev1(e, nt=nt, half=half, bks=bks):
                            last = None
                            for ol in range(4):
                                oc = half * 4 + ol
                                last = e.scalar_tensor_tensor(
                                    halfac[oc][:, 0:nt],
                                    P.pbank(bks[ol])[0:128, 0:nt],
                                    1.0, halfac[oc][:, 0:nt],
                                    ALU.mult, ALU.add)
                            return last
                        ev_t = P.inc("vector", ev1, "s_dve")
                    ev_h[half] = ev_t
                    for bk in bks:
                        P.pfree[bk] = ev_t
            P.wait("vector", ev_h)
            def ev2(e, t0=t0, nt=nt):
                last = None
                for oc in range(NCH):
                    last = e.tensor_add(x[oc][:, t0 : t0 + nt],
                                        x[oc][:, t0 : t0 + nt],
                                        halfac[oc][:, 0:nt])
                return last
            ev_t = P.inc("vector", ev2, "s_dve")
            ev_h = [ev_t, ev_t]
            P.wait("vector", [ev_t])
        P.barrier()
        P.ab.release(mkw2)
        P.af.release(mflay)

        for c in range(NCH):
            P.inc("vector", lambda e, c=c: e.tensor_scalar_mul(
                xb[c][:, 0:T], x[c][:, 0:T], 1.0), "s_dve")
        P.barrier()
        P.ab.release(mlay)
        if l == 0:
            t_cc = allgather(P, tiles=xb, ticks=[P.tick("s_dve")],
                             ccin=C["cc4_in"], ccout=C["cc4_out"])
            C["cc_x4"] = (C["cc4_out"], t_cc)

    mkt = P.ab.mark()
    mft = P.af.mark()
    x3r = Ring(P, 4, 512, arena=P.af)
    for c in range(NCH):
        for (t0, nt) in tok_chunks(T):
            def fill(slot, war, c=c, t0=t0, nt=nt):
                t = P.dma(slot[0:128, 0:nt],
                          C["l1f_x3"].ap()[c * 128 : c * 128 + 128,
                                           t0 : t0 + nt],
                          queue="sync", wait=war)
                return [t]
            ap, tk = x3r.get(("x3", c, t0), fill)
            P.wait("vector", tk + [P.tick("s_act")])
            P.inc("vector", lambda e, c=c, t0=t0, nt=nt, ap=ap:
                  e.tensor_add(x[c][:, t0 : t0 + nt], x[c][:, t0 : t0 + nt],
                               ap[0:128, 0:nt]), "s_dve")
    P.barrier()
    P.af.release(mft)
    for c in range(NCH):
        P.inc("vector", lambda e, c=c: e.tensor_scalar_mul(xb[c][:, 0:T],
                                                           x[c][:, 0:T], 1.0),
              "s_dve")
    P.barrier()
    stt = ln_stats(P, xhat=sbuf_loader(P, xb), Ttot=T, sq_scratch=sq,
                   want_f32=True)
    topln(P, xadd_f32=lambda c: (x[c], []), xhat_out=lambda c: xb[c],
          st=stt, T=T)
    P.barrier()
    P.ab.release(mkt)
    P.ab.release(m_blk)
    P.af.release(mf_blk)
